# version 15
# baseline (speedup 1.0000x reference)
import contextlib
import numpy as np
import concourse.bass as bass
import concourse.mybir as mybir
from concourse.bass_utils import run_bass_kernel_spmd

F32 = mybir.dt.float32
BF16 = mybir.dt.bfloat16
F16 = mybir.dt.float16
I32 = mybir.dt.int32
U8 = mybir.dt.uint8
AF = mybir.ActivationFunctionType
ALU = mybir.AluOpType
AX = mybir.AxisListType

B, T, D = 4, 4096, 1024
H, HC, NST, DTR, FI = 2048, 1024, 16, 64, 2816
TC = 1024
NCH = T // TC
TH = 2048
LN_EPS = 1e-5
ISL = FI // 8
DSL = D // 8

ENGS = ("sync", "scalar", "vector", "tensor", "gpsimd")


class Prog:
    def __init__(self, nc, es):
        self.nc = nc
        self.q = {e: [] for e in ENGS}
        self.semh = {}
        self.semv = {}
        for s in ENGS + ("cc", "out"):
            self.semh[s] = es.enter_context(nc.semaphore(f"sem_{s}"))
            self.semv[s] = 0

    def emit(self, eng, fn):
        self.q[eng].append(fn)

    def wait(self, eng, sem, val):
        if val and val > 0:
            h = self.semh[sem]
            self.q[eng].append(lambda e, h=h, v=val: e.wait_ge(h, v))

    def inc(self, eng, sem, fn, k=1):
        self.semv[sem] += k
        tgt = self.semv[sem]
        h = self.semh[sem]
        self.q[eng].append(lambda e, f=fn, h=h, kk=k: f(e).then_inc(h, kk))
        return tgt

    def run(self, block):
        @block.sync
        def _(e):
            for f in self.q["sync"]:
                f(e)

        @block.scalar
        def _(e):
            for f in self.q["scalar"]:
                f(e)

        @block.vector
        def _(e):
            for f in self.q["vector"]:
                f(e)

        @block.tensor
        def _(e):
            for f in self.q["tensor"]:
                f(e)

        @block.gpsimd
        def _(e):
            for f in self.q["gpsimd"]:
                f(e)


def cdiv(a, b):
    return (a + b - 1) // b


ARENA_BYTES = 194688


class Arena:

    def __init__(self, ap):
        self.ap = ap
        self.off = 0

    def alloc(self, nbytes, align=512):
        o = (self.off + align - 1) // align * align
        self.off = o + nbytes
        assert self.off <= ARENA_BYTES, f"arena overflow {self.off}"
        return o

    def view(self, off, dtype, shape):
        esz = {F32: 4, BF16: 2, F16: 2, I32: 4}[dtype]
        n = int(np.prod(shape[1:])) if len(shape) > 1 else 1
        ap = self.ap[:, off:off + n * esz].bitcast(dtype)
        if len(shape) > 2:
            dims = " ".join(f"d{i}" for i in range(1, len(shape)))
            ap = ap.rearrange(f"p ({dims}) -> p {dims}",
                              **{f"d{i}": shape[i] for i in range(1, len(shape) - 1)})
        return ap


def build(debug=False):
    nc = bass.Bass("TRN2", target_bir_lowering=False,
                   detect_race_conditions=False)

    dx_T = nc.dram_tensor("xT", [D, T], F32, kind="ExternalInput").ap()
    dx_res = nc.dram_tensor("xres", [TH, D], F32, kind="ExternalInput").ap()
    dwinT = nc.dram_tensor("winT", [D, 2 * HC], F32, kind="ExternalInput").ap()
    dconvw = nc.dram_tensor("convw", [HC, 3], F32, kind="ExternalInput").ap()
    dconvb = nc.dram_tensor("convb", [128, 8], F32, kind="ExternalInput").ap()
    dwxpT = nc.dram_tensor("wxpT", [HC, 96], F32, kind="ExternalInput").ap()
    dwdtT = nc.dram_tensor("wdtT", [DTR, HC], F32, kind="ExternalInput").ap()
    dbdt = nc.dram_tensor("bdt", [128, 8], F32, kind="ExternalInput").ap()
    dalog = nc.dram_tensor("alog", [1, NST], F32, kind="ExternalInput").ap()
    ddvec = nc.dram_tensor("dvec", [128, 8], F32, kind="ExternalInput").ap()
    dwoutT = nc.dram_tensor("woutT", [HC, D], F32, kind="ExternalInput").ap()
    dwg_s = nc.dram_tensor("wg_s", [ISL, D], F32, kind="ExternalInput").ap()
    dwu_s = nc.dram_tensor("wu_s", [ISL, D], F32, kind="ExternalInput").ap()
    dwd_s = nc.dram_tensor("wd_s", [DSL, FI], F32, kind="ExternalInput").ap()
    dident = nc.dram_tensor("identin", [128, 128], F32, kind="ExternalInput").ap()

    dout = nc.dram_tensor("out", [TH, D], F32, kind="ExternalOutput").ap()

    if debug == 2:
        ddbg2 = {"ymine": nc.dram_tensor("dbg2_ymine", [128, 8 * TC], F32,
                                         kind="ExternalOutput").ap(),
                 "state": nc.dram_tensor("dbg2_state", [128, 8 * NST * NCH], F32,
                                         kind="ExternalOutput").ap(),
                 "dt": nc.dram_tensor("dbg2_dt", [128, 8 * TC], F32,
                                      kind="ExternalOutput").ap(),
                 "rsin": nc.dram_tensor("dbg2_rsin", [T, D], F32,
                                        kind="ExternalOutput").ap(),
                 "rsout": nc.dram_tensor("dbg2_rsout", [TH, D], F32,
                                         kind="ExternalOutput").ap(),
                 "ym3": nc.dram_tensor("dbg2_ym3", [128, 8 * TC], F32,
                                       kind="ExternalOutput").ap()}
    if debug is True:
        ddbg = {k: nc.dram_tensor(f"dbg_{k}", [128, 8 * TC], F32,
                                  kind="ExternalOutput").ap()
                for k in ("u", "uc", "sz", "dt", "ymine")}
        ddbg["xdbl"] = nc.dram_tensor("dbg_xdbl", [96, TC], F32,
                                      kind="ExternalOutput").ap()
        ddbg["rs"] = nc.dram_tensor("dbg_rs", [128, D], F32,
                                    kind="ExternalOutput").ap()
        ddbg["x2"] = nc.dram_tensor("dbg_x2", [128, D], F32,
                                    kind="ExternalOutput").ap()
        ddbg["x2q"] = nc.dram_tensor("dbg_x2q", [128, D], F32,
                                     kind="ExternalOutput").ap()
        ddbg["hq"] = nc.dram_tensor("dbg_hq", [128, FI], F32,
                                    kind="ExternalOutput").ap()
        ddbg["gate"] = nc.dram_tensor("dbg_gate", [128, 512], F32,
                                      kind="ExternalOutput").ap()
        ddbg["wgT"] = nc.dram_tensor("dbg_wgT", [128, FI], F32,
                                     kind="ExternalOutput").ap()
        ddbg["wgq"] = nc.dram_tensor("dbg_wgq", [FI, D], F32,
                                     kind="ExternalOutput").ap()
        ddbg["ln"] = nc.dram_tensor("dbg_ln", [128, 8], F32,
                                    kind="ExternalOutput").ap()
        ddbg["r"] = nc.dram_tensor("dbg_r", [128, D], F32,
                                   kind="ExternalOutput").ap()
    d_abounce = nc.dram_tensor("a_bounce", [1, NST], F32, kind="Internal").ap()
    d_xdbl_in = nc.dram_tensor("cc_xdbl_in", [96, TC], F32, kind="Internal").ap()
    d_xdbl_out = nc.dram_tensor("cc_xdbl_out", [96, TC], F32, kind="Internal").ap()
    d_rs_in = nc.dram_tensor("cc_rs_in", [T, D], F32, kind="Internal").ap()
    d_rs_out = nc.dram_tensor("cc_rs_out", [TH, D], F32, kind="Internal").ap()
    d_wgT_pre = nc.dram_tensor("cc_wgT_pre", [D, ISL], BF16, kind="Internal").ap()
    d_wgT_blk = nc.dram_tensor("cc_wgT_blk", [8 * D, ISL], BF16, kind="Internal").ap()
    d_wuT_pre = nc.dram_tensor("cc_wuT_pre", [D, ISL], BF16, kind="Internal").ap()
    d_wuT_blk = nc.dram_tensor("cc_wuT_blk", [8 * D, ISL], BF16, kind="Internal").ap()
    d_wdT_pre = nc.dram_tensor("cc_wdT_pre", [FI, DSL], BF16, kind="Internal").ap()
    d_wdT_blk = nc.dram_tensor("cc_wdT_blk", [8 * FI, DSL], BF16, kind="Internal").ap()
    d_x2 = nc.dram_tensor("x2_scr", [TH, D], F32, kind="Internal").ap()
    d_hq = nc.dram_tensor("hq_scr", [TH, FI], BF16, kind="Internal").ap()

    es = contextlib.ExitStack()
    with es:
        sb = lambda name, shape, dt: es.enter_context(nc.sbuf_tensor(name, shape, dt))

        abc = sb("sb_abc", [128, NST], F32)
        ident = sb("sb_ident", [128, 128], BF16)
        convw = sb("sb_convw", [128, 8, 3], F32)
        convb = sb("sb_convb", [128, 8], F32)
        bdt = sb("sb_bdt", [128, 8], F32)
        dvec = sb("sb_dvec", [128, 8], F32)
        state = sb("sb_state", [128, 8, NST], F32)
        state_rd = sb("sb_state_rd", [128, 8, NST], F32)
        wxpT_h = sb("sb_wxpT_h", [128, 8, 96], BF16)
        wxpT_l = sb("sb_wxpT_l", [128, 8, 96], BF16)
        wdtT_h = sb("sb_wdtT_h", [DTR, HC], BF16)
        wdtT_l = sb("sb_wdtT_l", [DTR, HC], BF16)
        dtp = sb("sb_dtp", [DTR, TC], BF16)
        xdbl = sb("sb_xdbl", [96, TC], F32)
        alog_t = sb("sb_alog_t", [1, NST], F32)
        aneg_t = sb("sb_aneg_t", [1, NST], F32)
        ln = {k: sb(f"sb_ln_{k}", [128, 1], F32)
              for k in ("sr", "ssq", "mu", "m2e", "mu2", "ve", "iv2", "rstd")}
        qsml = {k: sb(f"sb_q_{k}", [128, FI // 16], F32) for k in ("m", "sh", "rm", "rm2")}

        arena_t = sb("sb_arena", [128, ARENA_BYTES], U8)
        A = Arena(arena_t[:])

        o_winT = A.alloc(8 * 2048 * 2)
        o_xTc = A.alloc(8 * (TC + 2) * 2, align=64)
        o_u = A.alloc(8 * (TC + 2) * 2, align=64)
        o_uc = A.alloc(8 * TC * 2, align=64)
        o_sz = A.alloc(8 * TC * 2, align=64)
        o_dt = A.alloc(8 * TC * 2, align=64)
        o_dA = [A.alloc(2 * TC * 4, align=64) for _ in range(2)]
        o_bu = A.alloc(TC * 2, align=64)
        o_pb = [A.alloc(TC * 2, align=64) for _ in range(6)]
        o_hbuf = A.alloc(4 * TC * 2, align=64)
        o_ymh = A.alloc(8 * TC * 2, align=64)
        o_woutT = A.alloc(8 * D * 2, align=64)
        o_rse = [A.alloc(D * 4, align=64) for _ in range(2)]
        mix_end = A.off
        o_BC = o_xTc
        o_yacc = o_u
        o_yml = o_dt

        winT = A.view(o_winT, BF16, [128, 8, 2048])
        xTc = A.view(o_xTc, BF16, [128, 8, TC + 2])
        u_sb = A.view(o_u, F16, [128, 8, TC + 2])
        uc_sb = A.view(o_uc, BF16, [128, 8, TC])
        sz_sb = A.view(o_sz, F16, [128, 8, TC])
        dt_sb = A.view(o_dt, F16, [128, 8, TC])
        dA = [A.view(o, F32, [128, 2, TC]) for o in o_dA]
        bu = A.view(o_bu, F16, [128, TC])
        pb = [A.view(o, F16, [128, TC]) for o in o_pb]
        hbuf = A.view(o_hbuf, F16, [128, 4, TC])
        ymh = A.view(o_ymh, BF16, [128, 8, TC])
        yml = A.view(o_yml, BF16, [128, 8, TC])
        yacc = A.view(o_yacc, F16, [128, 2, 4, TC])
        woutT = A.view(o_woutT, BF16, [128, 8, D])
        rse = [A.view(o, F32, [128, D]) for o in o_rse]
        sp_tmp = A.view(o_rse[1], F32, [128, 2, 512])
        BC = A.view(o_BC, F16, [128, 4, 2, TC])
        BCflat = A.view(o_BC, F16, [128, 4 * 2 * TC])
        wxf = A.view(o_sz, F32, [128, 8, 96])
        wdf = A.view(o_sz + 4096, F32, [128, HC])

        q_big_offs = [o_dt, o_ymh, o_woutT, o_xTc]

        oB = Arena(arena_t[:])
        oB.off = 0
        b_x2qT = oB.alloc(8 * TH * 2)
        b_rs = [oB.alloc(D * 4) for _ in range(2)]
        b_r = oB.alloc(D * 4)
        b_x2 = oB.alloc(D * 4)
        b_x2q = oB.alloc(D * 2)
        b_junk = oB.alloc(D * 2)
        b_q = [oB.alloc(D * 4) for _ in range(4)]
        b_wld = [oB.alloc(FI * 2) for _ in range(2)]
        b_ev = oB.alloc(1024 * 2)
        b_ti = oB.alloc(D * 4)
        assert oB.off <= ARENA_BYTES

        x2qT = A.view(b_x2qT, BF16, [128, 8, TH])
        rsld = [A.view(o, F32, [128, D]) for o in b_rs]
        r_sb = A.view(b_r, F32, [128, D])
        x2_sb = A.view(b_x2, F32, [128, D])
        x2q_sb = A.view(b_x2q, BF16, [128, D])
        junk_sb = A.view(b_junk, BF16, [128, D])
        wld = [A.view(o, BF16, [128, FI]) for o in b_wld]
        ev_sb = A.view(b_ev, BF16, [128, 1024])

        oC = Arena(arena_t[:])
        oC.off = b_rs[0]
        c_wg = [oC.alloc(8 * 512 * 2) for _ in range(2)]
        c_wu = [oC.alloc(8 * 512 * 2) for _ in range(2)]
        c_h = oC.alloc(512 * 4)
        c_sg = oC.alloc(512 * 2)
        c_hq = oC.alloc(512 * 2)
        c_q = [oC.alloc(512 * 4) for _ in range(4)]
        c_wdqT = oC.alloc(22 * 1024 * 2)
        c_hqld = [oC.alloc(FI * 2) for _ in range(2)]
        c_hqT = oC.alloc(22 * 128 * 2)
        c_x2l = [oC.alloc(D * 4) for _ in range(2)]
        c_out = [oC.alloc(D * 4) for _ in range(2)]
        assert oC.off <= ARENA_BYTES, f"phaseC overflow {oC.off}"

        wgc = [A.view(o, BF16, [128, 8, 512]) for o in c_wg]
        wuc = [A.view(o, BF16, [128, 8, 512]) for o in c_wu]
        h_sb = A.view(c_h, F32, [128, 512])
        sg_sb = A.view(c_sg, F16, [128, 512])
        hq_sb = A.view(c_hq, BF16, [128, 512])
        wdqT = A.view(c_wdqT, BF16, [128, 22, 1024])
        hqld = [A.view(o, BF16, [128, FI]) for o in c_hqld]
        hqT = A.view(c_hqT, BF16, [128, 22, 128])
        x2l = [A.view(o, F32, [128, D]) for o in c_x2l]
        out_sb = [A.view(o, F32, [128, D]) for o in c_out]

        PB = [es.enter_context(nc.psum_tensor(f"pb{i}", [128, 512], F32))
              for i in range(8)]

        block = es.enter_context(nc.Block())
        P = Prog(nc, es)
        bank_free = {i: None for i in range(8)}

        def bank_wait(eng, i):
            if bank_free[i] is not None:
                s, v = bank_free[i]
                P.wait(eng, s, v)

        def vfence(v_val):
            P.wait("gpsimd", "vector", v_val)
            return P.inc("gpsimd", "gpsimd", lambda e: e.engine_nop(), 1)

        def sload(dst, src):
            return P.inc("sync", "sync",
                         lambda e, d=dst, s=src: e.dma_start(out=d, in_=s), 16)

        def gload(dst, src):
            return P.inc("gpsimd", "gpsimd",
                         lambda e, d=dst, s=src: e.dma_start(out=d, in_=s), 16)

        v_convw = sload(convw[:],
                        dconvw.rearrange("(a p) b -> p a b", p=128))
        v_convb = sload(convb[:], dconvb)
        v_bdt = sload(bdt[:], dbdt)
        v_dvec = sload(dvec[:], ddvec)
        v_alog = sload(alog_t[:], dalog)
        v_ident = gload(ident[:], dident)
        v_wxf = sload(wxf[:], dwxpT.rearrange("(a p) b -> p a b", p=128))
        v_wdf = sload(wdf[0:DTR, :], dwdtT)
        P.wait("vector", "sync", max(v_wxf, v_wdf))
        P.inc("vector", "vector", lambda e: e.tensor_copy(out=wxpT_h[:], in_=wxf[:]))
        P.inc("vector", "vector", lambda e: e.scalar_tensor_tensor(
            out=wxpT_l[:], in0=wxpT_h[:], scalar=-1.0, in1=wxf[:],
            op0=ALU.mult, op1=ALU.add))
        P.inc("vector", "vector",
              lambda e: e.tensor_copy(out=wdtT_h[:], in_=wdf[0:DTR, :]))
        v_wsplit = P.inc("vector", "vector", lambda e: e.scalar_tensor_tensor(
            out=wdtT_l[:], in0=wdtT_h[:], scalar=-1.0, in1=wdf[0:DTR, :],
            op0=ALU.mult, op1=ALU.add))
        v_f_wsplit = vfence(v_wsplit)

        P.wait("scalar", "sync", v_alog)
        v_ax = P.inc("scalar", "scalar",
                     lambda e: e.activation(out=alog_t[:], in_=alog_t[:], func=AF.Exp))
        P.wait("gpsimd", "scalar", v_ax)
        P.inc("gpsimd", "gpsimd",
              lambda e: e.tensor_copy(out=aneg_t[:], in_=alog_t[:]), 1)
        v_an = P.inc("gpsimd", "gpsimd",
                     lambda e: e.tensor_scalar_mul(aneg_t[:], aneg_t[:], -1.0), 1)
        P.wait("sync", "gpsimd", v_an)
        v_ab = sload(d_abounce, aneg_t[:])
        P.wait("sync", "sync", v_ab)
        v_abc = sload(abc[:], d_abounce.broadcast_to([128, NST]))

        v_st0 = P.inc("vector", "vector", lambda e: e.memset(state[:], 0.0))

        q_prev_use = [0]

        def quant_fp4(src_ap, dst_ap, F, scratch_f32, scratch_i32, pc=128,
                      src_ready=None):
            nb = F // 16
            m = qsml["m"][0:pc, 0:nb]
            sh = qsml["sh"][0:pc, 0:nb]
            rm = qsml["rm"][0:pc, 0:nb]
            rm2 = qsml["rm2"][0:pc, 0:nb]
            asrc, aw, t1, t2 = [sc[0:pc, 0:F] for sc in scratch_f32]
            src3 = src_ap.rearrange("p (b k) -> p b k", k=16)
            if src_ready is not None:
                P.wait("vector", src_ready[0], src_ready[1])
                P.wait("scalar", src_ready[0], src_ready[1])
            v_m = P.inc("vector", "vector", lambda e: e.tensor_reduce(
                out=m, in_=src3, axis=AX.X, op=ALU.max, apply_absolute_value=True))
            v_as = P.inc("scalar", "scalar",
                         lambda e: e.activation(out=asrc, in_=src_ap, func=AF.Abs))
            v_sg = P.inc("scalar", "scalar",
                         lambda e: e.activation(out=aw, in_=src_ap, func=AF.Sign))
            P.wait("gpsimd", "vector", v_m)
            P.wait("gpsimd", "scalar", v_sg)
            P.wait("gpsimd", "vector", q_prev_use[0])
            v_sh = P.inc("gpsimd", "gpsimd", lambda e: e.tensor_scalar(
                out=sh, in0=m, scalar1=1.0 / 12.0, scalar2=5e-13,
                op0=ALU.mult, op1=ALU.max), 1)
            P.wait("vector", "gpsimd", v_sh)
            v_rm = P.inc("vector", "vector", lambda e: e.reciprocal(out=rm, in_=sh))
            P.wait("scalar", "vector", v_rm)
            v_rm2 = P.inc("scalar", "scalar",
                          lambda e: e.copy(rm2, rm))
            P.wait("vector", "scalar", v_rm2)
            P.inc("vector", "vector", lambda e: e.tensor_tensor(
                out=t1.rearrange("p (b k) -> p b k", k=16),
                in0=asrc.rearrange("p (b k) -> p b k", k=16),
                in1=rm2.unsqueeze(-1).broadcast_to([pc, nb, 16]), op=ALU.mult))
            P.inc("vector", "vector", lambda e: e.tensor_scalar(
                out=asrc, in0=t1, scalar1=0.5, scalar2=1.0,
                op0=ALU.is_gt, op1=ALU.mult))
            for mj in (1.5, 2.5, 3.5):
                P.inc("vector", "vector", lambda e, mj=mj: e.scalar_tensor_tensor(
                    out=asrc, in0=t1, scalar=mj, in1=asrc,
                    op0=ALU.is_gt, op1=ALU.add))
            P.inc("vector", "vector", lambda e: e.tensor_scalar(
                out=t2, in0=t1, scalar1=10.0, scalar2=2.0,
                op0=ALU.is_gt, op1=ALU.mult))
            for mj in (5.0, 7.0):
                P.inc("vector", "vector", lambda e, mj=mj: e.scalar_tensor_tensor(
                    out=t2, in0=t1, scalar=mj, in1=t2,
                    op0=ALU.is_gt, op1=ALU.add))
            P.inc("vector", "vector", lambda e: e.scalar_tensor_tensor(
                out=asrc, in0=t2, scalar=2.0, in1=asrc,
                op0=ALU.mult, op1=ALU.add))
            P.wait("vector", "scalar", v_sg)
            P.inc("vector", "vector", lambda e: e.tensor_tensor(
                out=t2.rearrange("p (b k) -> p b k", k=16),
                in0=aw.rearrange("p (b k) -> p b k", k=16),
                in1=sh.unsqueeze(-1).broadcast_to([pc, nb, 16]), op=ALU.mult))
            v_q = P.inc("vector", "vector",
                        lambda e: e.tensor_mul(dst_ap, asrc, t2))
            q_prev_use[0] = v_q
            return v_q

        ln_prev_use = [0]

        def ln_tail(r_ap, out_ap, v_r):
            v_sr = P.inc("vector", "vector", lambda e: e.tensor_reduce(
                out=ln["sr"][:], in_=r_ap, axis=AX.X, op=ALU.add))
            P.wait("scalar", "vector", v_r)
            v_ssq = P.inc("scalar", "scalar", lambda e: e.activation(
                out=junk_sb[:], in_=r_ap, func=AF.Square, accum_out=ln["ssq"][:]))
            P.wait("gpsimd", "vector", v_sr)
            P.wait("gpsimd", "scalar", v_ssq)
            P.wait("gpsimd", "vector", ln_prev_use[0])
            P.inc("gpsimd", "gpsimd", lambda e: e.tensor_scalar(
                out=ln["mu"][:], in0=ln["sr"][:], scalar1=1.0 / D, scalar2=None,
                op0=ALU.mult), 1)
            P.inc("gpsimd", "gpsimd", lambda e: e.tensor_scalar(
                out=ln["m2e"][:], in0=ln["ssq"][:], scalar1=1.0 / D,
                scalar2=LN_EPS, op0=ALU.mult, op1=ALU.add), 1)
            P.inc("gpsimd", "gpsimd", lambda e: e.tensor_mul(
                ln["mu2"][:], ln["mu"][:], ln["mu"][:]), 1)
            v_ve = P.inc("gpsimd", "gpsimd", lambda e: e.tensor_sub(
                ln["ve"][:], ln["m2e"][:], ln["mu2"][:]), 1)
            P.wait("vector", "gpsimd", v_ve)
            v_iv = P.inc("vector", "vector", lambda e: e.reciprocal(
                out=ln["iv2"][:], in_=ln["ve"][:]))
            P.wait("scalar", "vector", v_iv)
            v_rstd = P.inc("scalar", "scalar", lambda e: e.activation(
                out=ln["rstd"][:], in_=ln["iv2"][:], func=AF.Sqrt))
            P.wait("vector", "scalar", v_rstd)
            v_out = P.inc("vector", "vector", lambda e: e.tensor_scalar(
                out=out_ap, in0=r_ap, scalar1=ln["mu"][:], scalar2=ln["rstd"][:],
                op0=ALU.subtract, op1=ALU.mult))
            ln_prev_use[0] = v_out
            return v_out, v_ssq

        wq_scr_f = [A.view(o, F32, [128, FI]) for o in q_big_offs]
        wq_ti = None
        wq_in_v = A.view(o_winT, F32, [128, 3, D])
        wq_in_d = A.view(o_winT, F32, [128, FI])
        wq_out_v = A.view(o_uc, BF16, [128, 3, D])
        wq_out_d = A.view(o_uc, BF16, [128, FI])

        wq_plan = [(dwg_s, d_wgT_pre, ISL, D, wq_in_v, wq_out_v),
                   (dwu_s, d_wuT_pre, ISL, D, wq_in_v, wq_out_v),
                   (dwd_s, d_wdT_pre, DSL, FI, wq_in_d, wq_out_d)]
        v_wq_store = 0
        wq_ev = [A.view(o_sz + 8192 + 256 * i, BF16, [128, 128]) for i in range(2)]
        P.wait("tensor", "gpsimd", v_ident)
        for (dsrc, ddstT, R, F, vin, vout) in wq_plan:
            nt = cdiv(R, 128)
            for rt in range(nt):
                p0, p1 = rt * 128, min((rt + 1) * 128, R)
                pc = p1 - p0
                vv = sload(vin[0:pc, rt, :] if nt > 1 else vin[0:pc, :],
                           dsrc[p0:p1, :])
                P.wait("vector", "sync", vv)
                v_q = quant_fp4(vin[0:pc, rt, :] if nt > 1 else vin[0:pc, :],
                                vout[0:pc, rt, :] if nt > 1 else vout[0:pc, :],
                                F, wq_scr_f, wq_ti, pc, src_ready=("sync", vv))
                v_f_wq = vfence(v_q)
                P.wait("tensor", "gpsimd", v_f_wq)
                for ft in range(F // 128):
                    bnk = 6 + (ft % 2)
                    bank_wait("tensor", bnk)
                    tin = (vout[0:pc, rt, ft * 128:(ft + 1) * 128] if nt > 1
                           else vout[0:pc, ft * 128:(ft + 1) * 128])
                    v_tr = P.inc("tensor", "tensor",
                                 lambda e, tin=tin, bnk=bnk, pc=pc: e.transpose(
                                     PB[bnk][:, 0:64].bitcast(BF16)[:, 0:pc],
                                     tin, ident[0:pc, 0:pc]))
                    P.wait("vector", "tensor", v_tr)
                    P.wait("vector", "sync", v_wq_store)
                    v_evt = P.inc("vector", "vector",
                                  lambda e, ft=ft, bnk=bnk, pc=pc: e.tensor_copy(
                                      out=wq_ev[ft % 2][:, 0:pc],
                                      in_=PB[bnk][:, 0:64].bitcast(BF16)[:, 0:pc]))
                    bank_free[bnk] = ("vector", v_evt)
                    v_f_ev = vfence(v_evt)
                    P.wait("sync", "gpsimd", v_f_ev)
                    P.wait("sync", "vector", v_evt)
                    v_wq_store = sload(ddstT[ft * 128:(ft + 1) * 128, p0:p1],
                                       wq_ev[ft % 2][:, 0:pc])
        v_wq_quant_done = P.semv["vector"]
        v_wq_sgn_done = P.semv["scalar"]

        v_bc_read_prev = 0
        v_xdo_prev = 0
        v_inproj_last_prev = 0
        v_uevac_read_prev = 0
        v_dA_last_prev = 0
        v_gate_prev = 0
        v_ucread_last_prev = 0
        v_ar_prev = 0
        v_rs_stores = 0
        v_wq_ag = None
        v_outproj_evac_prev = 0
        v_gy_last_prev = 0
        KT_G = (2, 3)
        PSLOT = {0: (0, 0), 1: (1, 1), 2: (2, 3), 3: (4, 5)}
        g_last = {2: [0, 0], 3: [0, 0]}

        for c in range(NCH):
            t0 = c * TC
            lo = max(t0 - 1, 0)
            hi = min(t0 + TC + 1, T)
            ofs = lo - (t0 - 1)

            if c == 0:
                P.wait("gpsimd", "vector", v_wq_quant_done)
                P.wait("gpsimd", "scalar", v_wq_sgn_done)
                P.wait("gpsimd", "sync", v_wq_store)
                P.wait("scalar", "vector", v_wq_quant_done)
                P.wait("scalar", "sync", v_wq_store)
            else:
                P.wait("gpsimd", "tensor", v_inproj_last_prev)
                P.wait("gpsimd", "vector", v_bc_read_prev)
            v_xc = 0
            for dt_ in range(8):
                v_xc = gload(xTc[:, dt_, ofs:ofs + (hi - lo)],
                             dx_T[dt_ * 128:(dt_ + 1) * 128, lo:hi])
            if c == 0:
                v_wl = gload(winT[:],
                             dwinT.rearrange("(a p) b -> p a b", p=128))
                v_wo = gload(woutT[:],
                             dwoutT.rearrange("(a p) b -> p a b", p=128))
                v_xc = max(v_xc, v_wl, v_wo)

            if c == 0:
                P.inc("vector", "vector", lambda e: e.memset(u_sb[:, :, 0:1], 0.0))
            if c == NCH - 1:
                P.inc("vector", "vector",
                      lambda e: e.memset(u_sb[:, :, TC + 1:TC + 2], 0.0))

            P.wait("tensor", "gpsimd", v_xc)
            if c > 0:
                P.wait("scalar", "vector", v_uevac_read_prev)
                P.wait("scalar", "vector", v_gate_prev)
                P.wait("scalar", "gpsimd", v_gy_last_prev)
            pbrot = [0]

            def inproj_tile(et, col0, width, dst_kind):
                nccw = cdiv(width, 512)
                for ccw in range(nccw):
                    w0 = ccw * 512
                    w1 = min(w0 + 512, width)
                    bi = pbrot[0] % 2
                    pbrot[0] += 1
                    bank_wait("tensor", bi)
                    for dt_ in range(8):
                        def mm(e, bi=bi, dt_=dt_, et=et, a=col0 + w0, b=col0 + w1,
                               st=(dt_ == 0), sp=(dt_ == 7)):
                            return e.matmul(PB[bi][:, 0:b - a],
                                            winT[:, dt_, et * 128:(et + 1) * 128],
                                            xTc[:, dt_, a:b], start=st, stop=sp)
                        if dt_ == 7:
                            v_mm = P.inc("tensor", "tensor", mm)
                        else:
                            P.emit("tensor", mm)
                    P.wait("scalar", "tensor", v_mm)
                    kind, kt = dst_kind
                    if kind == "u":
                        v_ev = P.inc("scalar", "scalar",
                                     lambda e, bi=bi, kt=kt, a=col0 + w0, b=col0 + w1:
                                     e.activation(out=u_sb[:, kt, a:b],
                                                  in_=PB[bi][:, 0:b - a],
                                                  func=AF.Copy))
                    else:
                        v_ev = P.inc("scalar", "scalar",
                                     lambda e, bi=bi, kt=kt, a=col0 + w0, b=col0 + w1:
                                     e.activation(out=sz_sb[:, kt, a - 1:b - 1],
                                                  in_=PB[bi][:, 0:b - a],
                                                  func=AF.Silu))
                    bank_free[bi] = ("scalar", v_ev)
                return P.semv["tensor"]

            v_u_evac_et = []
            for et in range(8):
                v_inproj_u = inproj_tile(et, ofs, hi - lo, ("u", et))
                v_u_evac_et.append(P.semv["scalar"])

            P.wait("vector", "sync", max(v_convw, v_convb, v_rs_stores))
            v_uc_silu_prev = [0]
            for kt in range(8):
                P.wait("vector", "scalar", v_u_evac_et[kt])
                P.wait("vector", "scalar", v_uc_silu_prev[0])
                P.inc("vector", "vector", lambda e, kt=kt: e.tensor_scalar_mul(
                    rse[0][0:128, 0:TC], u_sb[:, kt, 1:TC + 1], convw[:, kt, 1:2]))
                P.inc("vector", "vector", lambda e, kt=kt: e.scalar_tensor_tensor(
                    out=rse[0][0:128, 0:TC], in0=u_sb[:, kt, 0:TC],
                    scalar=convw[:, kt, 0:1], in1=rse[0][0:128, 0:TC],
                    op0=ALU.mult, op1=ALU.add))
                v_cv = P.inc("vector", "vector", lambda e, kt=kt: e.scalar_tensor_tensor(
                    out=rse[0][0:128, 0:TC], in0=u_sb[:, kt, 2:TC + 2],
                    scalar=convw[:, kt, 2:3], in1=rse[0][0:128, 0:TC],
                    op0=ALU.mult, op1=ALU.add))
                P.wait("scalar", "vector", v_cv)
                if c > 0 and kt == 0:
                    P.wait("scalar", "tensor", v_ucread_last_prev)
                v_uc_silu_prev[0] = P.inc("scalar", "scalar",
                    lambda e, kt=kt: e.activation(
                        out=uc_sb[:, kt, :], in_=rse[0][0:128, 0:TC], func=AF.Silu,
                        bias=convb[:, kt:kt + 1]))
            v_uc_done = P.semv["scalar"]

            for et in range(8, 16):
                v_inproj_z = inproj_tile(et, 1, TC, ("z", et - 8))
            v_inproj_last_prev = P.semv["tensor"]
            v_z_evac = P.semv["scalar"]

            P.wait("tensor", "scalar", v_uc_done)
            if c == 0:
                P.wait("tensor", "gpsimd", v_f_wsplit)
            P.wait("vector", "sync", v_xdo_prev)
            for ccw in range(TC // 512):
                bi = 2 + (ccw % 2)
                bank_wait("tensor", bi)
                for kt in range(8):
                    for si, wsb in enumerate((wxpT_h, wxpT_l)):
                        def mmx(e, bi=bi, kt=kt, ccw=ccw, wsb=wsb,
                                st=(kt == 0 and si == 0), sp=(kt == 7 and si == 1)):
                            return e.matmul(PB[bi][0:96, 0:512], wsb[:, kt, :],
                                            uc_sb[:, kt, ccw * 512:(ccw + 1) * 512],
                                            start=st, stop=sp)
                        if kt == 7 and si == 1:
                            v_mmx = P.inc("tensor", "tensor", mmx)
                        else:
                            P.emit("tensor", mmx)
                P.wait("vector", "tensor", v_mmx)
                v_xd = P.inc("vector", "vector", lambda e, bi=bi, ccw=ccw:
                             e.tensor_copy(out=xdbl[:, ccw * 512:(ccw + 1) * 512],
                                           in_=PB[bi][0:96, 0:512]))
                bank_free[bi] = ("vector", v_xd)
            v_f_xd = vfence(v_xd)
            P.wait("sync", "gpsimd", v_f_xd)
            P.wait("sync", "vector", v_xd)
            if c > 0:
                P.wait("sync", "cc", v_ar_prev)
            v_xdo = sload(d_xdbl_in, xdbl[:])
            v_xdo_prev = v_xdo
            P.wait("gpsimd", "sync", v_xdo)
            v_ar = P.inc("gpsimd", "cc", lambda e: e.collective_compute(
                "AllReduce", ALU.add,
                replica_groups=[[0, 1], [2, 3], [4, 5], [6, 7]],
                ins=[d_xdbl_in], outs=[d_xdbl_out]))
            v_ar_prev = v_ar

            P.wait("gpsimd", "cc", v_ar)
            v_dtp = gload(dtp[:], d_xdbl_out[0:DTR, :])
            P.wait("tensor", "gpsimd", v_dtp)
            v_dtadd_prev = [0]
            if c > 0:
                P.wait("vector", "scalar", v_dA_last_prev)
                P.wait("scalar", "vector", v_outproj_evac_prev)
                P.wait("scalar", "sync", v_rs_stores)
            for kt in range(8):
                for ccw in range(TC // 512):
                    bi = 4 + ((kt * 2 + ccw) % 2)
                    bank_wait("tensor", bi)
                    for si, wsb in enumerate((wdtT_h, wdtT_l)):
                        def mmd_(e, bi=bi, kt=kt, ccw=ccw, wsb=wsb,
                                 st=(si == 0), sp=(si == 1)):
                            return e.matmul(
                                PB[bi][:, 0:512],
                                wsb[:, kt * 128:(kt + 1) * 128],
                                dtp[:, ccw * 512:(ccw + 1) * 512],
                                start=st, stop=sp)
                        if si == 1:
                            v_mmd = P.inc("tensor", "tensor", mmd_)
                        else:
                            P.emit("tensor", mmd_)
                    P.wait("scalar", "tensor", v_mmd)
                    P.wait("scalar", "vector", v_dtadd_prev[0])
                    P.inc("scalar", "scalar", lambda e, bi=bi, kt=kt: e.activation(
                        out=sp_tmp[:, 0, :], in_=PB[bi][:, 0:512], func=AF.Abs,
                        bias=bdt[:, kt:kt + 1]))
                    P.inc("scalar", "scalar", lambda e: e.activation(
                        out=sp_tmp[:, 0, :], in_=sp_tmp[:, 0, :], func=AF.Exp,
                        scale=-1.0))
                    P.inc("scalar", "scalar", lambda e: e.activation(
                        out=sp_tmp[:, 0, :], in_=sp_tmp[:, 0, :], func=AF.Ln,
                        bias=1.0))
                    v_sp = P.inc("scalar", "scalar", lambda e, bi=bi, kt=kt:
                                 e.activation(out=sp_tmp[:, 1, :],
                                              in_=PB[bi][:, 0:512], func=AF.Relu,
                                              bias=bdt[:, kt:kt + 1]))
                    bank_free[bi] = ("scalar", v_sp)
                    P.wait("vector", "scalar", v_sp)
                    v_dtadd_prev[0] = P.inc("vector", "vector",
                          lambda e, kt=kt, ccw=ccw:
                          e.tensor_add(dt_sb[:, kt, ccw * 512:(ccw + 1) * 512],
                                       sp_tmp[:, 1, :], sp_tmp[:, 0, :]))
            v_dt_done = P.semv["vector"]

            P.wait("scalar", "vector", v_st0 if c == 0 else P.semv["vector"])
            v_strd = P.inc("scalar", "scalar", lambda e: e.copy(
                state_rd[:].rearrange("p a b -> p (a b)"),
                state[:].rearrange("p a b -> p (a b)")))
            P.wait("vector", "scalar", v_strd)

            P.wait("scalar", "vector", v_dt_done)
            P.wait("scalar", "sync", v_abc)
            v_scan_vec_prev = [0]
            v_bc_pair = [0, 0]
            v_bc_read = [0, 0]

            def emit_bc_load(k, first_of_chunk):
                ng, j = (k % 8) // 2, k % 2
                if first_of_chunk:
                    P.wait("gpsimd", "tensor", v_inproj_last_prev)
                if c > 0 or k >= 2:
                    P.wait("gpsimd", "vector",
                           v_bc_read[j] if k >= 2 else v_bc_read_prev)
                for i2 in range(2):
                    s = 2 * j + i2
                    src = bass.AP(d_xdbl_out.tensor, (DTR + 4 * ng + s) * TC,
                                  [[0, 128], [NST * TC, 2], [1, TC]])
                    v_bc_pair[j] = gload(
                        BCflat[:, s * 2 * TC:(s + 1) * 2 * TC], src)

            emit_bc_load(0, True)
            emit_bc_load(1, True)
            for ktg in range(2):
                kts = range(ktg * 4, ktg * 4 + 4)
                for ng in range(4):
                    for n4 in range(4):
                        n = 4 * ng + n4
                        if n4 % 2 == 0:
                            P.wait("vector", "gpsimd", v_bc_pair[n4 // 2])
                        P.wait("scalar", "vector", v_scan_vec_prev[0])
                        v_dA_half = []
                        for q in range(2):
                            ktp = ktg * 4 + q * 2
                            v_dA_half.append(P.inc("scalar", "scalar",
                                  lambda e, q=q, ktp=ktp, n=n: e.activation(
                                      out=dA[q][:].rearrange("p a b -> p (a b)"),
                                      in_=dt_sb[:, ktp:ktp + 2, :]
                                      .rearrange("p a b -> p (a b)"),
                                      func=AF.Exp, scale=abc[:, n:n + 1])))
                        for ki, kt in enumerate(kts):
                            if ki % 2 == 0:
                                P.wait("vector", "scalar", v_dA_half[ki // 2])
                            P.inc("vector", "vector",
                                  lambda e, kt=kt, n4=n4: e.tensor_tensor(
                                      out=bu[:], in0=uc_sb[:, kt, :],
                                      in1=BC[:, n4, 0, :], op=ALU.mult))
                            P.inc("vector", "vector",
                                  lambda e, kt=kt, ki=ki, n=n:
                                  e.tensor_tensor_scan(
                                      out=hbuf[:, ki, :],
                                      data0=dA[(kt % 4) // 2][:, kt % 2, :],
                                      data1=bu[:],
                                      initial=state_rd[:, kt, n:n + 1],
                                      op0=ALU.mult, op1=ALU.add))
                        v_scan_vec_prev[0] = P.semv["vector"]
                        gp_kts = () if n == NST - 1 else KT_G
                        for ki, kt in enumerate(kts):
                            slot = PSLOT[ki][n % 2]
                            if ki in KT_G:
                                P.wait("vector", "gpsimd", g_last[ki][n % 2])
                            P.inc("vector", "vector",
                                  lambda e, ki=ki, n4=n4, slot=slot:
                                  e.tensor_tensor(out=pb[slot][:],
                                                  in0=hbuf[:, ki, :],
                                                  in1=BC[:, n4, 1, :],
                                                  op=ALU.mult))
                        P.inc("vector", "vector", lambda e, ktg=ktg, n=n:
                              e.tensor_copy(
                                  out=state[:, ktg * 4:ktg * 4 + 4, n:n + 1],
                                  in_=hbuf[:, :, TC - 1:TC]))
                        for ki in range(4):
                            if ki in gp_kts:
                                continue
                            slot = PSLOT[ki][n % 2]
                            if ki in KT_G:
                                P.wait("vector", "gpsimd", g_last[ki][(n + 1) % 2])
                            if n == 0:
                                P.inc("vector", "vector",
                                      lambda e, ki=ki, slot=slot: e.tensor_copy(
                                          out=yacc[:, 0, ki, :], in_=pb[slot][:]))
                            else:
                                P.inc("vector", "vector",
                                      lambda e, ki=ki, n=n, slot=slot:
                                      e.tensor_tensor(
                                          out=yacc[:, n % 2, ki, :],
                                          in0=yacc[:, (n + 1) % 2, ki, :],
                                          in1=pb[slot][:], op=ALU.add))
                        v_pm_end = P.semv["vector"]
                        if gp_kts:
                            P.wait("gpsimd", "vector", v_pm_end)
                            P.emit("gpsimd", lambda e: e.engine_nop())
                        for ki in gp_kts:
                            slot = PSLOT[ki][n % 2]
                            if n == 0:
                                g_last[ki][0] = P.inc(
                                    "gpsimd", "gpsimd",
                                    lambda e, ki=ki, slot=slot: e.tensor_copy(
                                        out=yacc[:, 0, ki, :], in_=pb[slot][:]), 1)
                            else:
                                g_last[ki][n % 2] = P.inc(
                                    "gpsimd", "gpsimd",
                                    lambda e, ki=ki, n=n, slot=slot:
                                    e.tensor_tensor(
                                        out=yacc[:, n % 2, ki, :],
                                        in0=yacc[:, (n + 1) % 2, ki, :],
                                        in1=pb[slot][:], op=ALU.add), 1)
                        if n4 % 2 == 1:
                            jj = n4 // 2
                            v_bc_read[jj] = P.semv["vector"]
                            kk = ktg * 8 + ng * 2 + jj
                            if kk + 2 < 16:
                                emit_bc_load(kk + 2, False)
                    v_bc_read_prev = P.semv["vector"]
                v_dA_ktg_end = P.semv["scalar"]
                v_gy_end = P.semv["gpsimd"]
                P.wait("vector", "gpsimd", v_gy_end)
                P.wait("vector", "scalar", v_dA_ktg_end)
                P.wait("vector", "tensor", v_ucread_last_prev)
                P.wait("vector", "scalar", v_z_evac)
                P.wait("vector", "sync", v_dvec)
                for ki, kt in enumerate(kts):
                    P.inc("vector", "vector", lambda e, kt=kt, ki=ki:
                          e.scalar_tensor_tensor(
                              out=rse[0][0:128, 0:TC], in0=uc_sb[:, kt, :],
                              scalar=dvec[:, kt:kt + 1],
                              in1=yacc[:, 1, ki, :], op0=ALU.mult, op1=ALU.add))
                    P.inc("vector", "vector", lambda e, kt=kt: e.tensor_tensor(
                        out=rse[1][0:128, 0:TC], in0=rse[0][0:128, 0:TC],
                        in1=sz_sb[:, kt, :], op=ALU.mult))
                    P.inc("vector", "vector", lambda e, kt=kt: e.tensor_copy(
                        out=ymh[:, kt, :], in_=rse[1][0:128, 0:TC]))
                    P.inc("vector", "vector", lambda e, kt=kt:
                          e.scalar_tensor_tensor(
                              out=yml[:, kt, :], in0=ymh[:, kt, :], scalar=-1.0,
                              in1=rse[1][0:128, 0:TC],
                              op0=ALU.mult, op1=ALU.add))
            v_gate_prev = P.semv["vector"]
            v_dA_last_prev = P.semv["scalar"]
            v_gy_last_prev = P.semv["gpsimd"]

            P.wait("gpsimd", "vector", v_gate_prev)
            v_echo_g = P.inc("gpsimd", "gpsimd", lambda e: e.engine_nop(), 1)
            P.wait("tensor", "gpsimd", v_echo_g)
            P.wait("tensor", "vector", v_gate_prev)
            for ts in range(8):
                ob = (0 + 2 * (ts % 2), 1 + 2 * (ts % 2))
                for bnk in ob:
                    bank_wait("tensor", bnk)
                for kt in range(8):
                    for si, ysb in enumerate((ymh, yml)):
                        for half in range(2):
                            def mmo(e, kt=kt, ts=ts, half=half, bnk=ob[half],
                                    ysb=ysb, st=(kt == 0 and si == 0),
                                    sp=(kt == 7 and si == 1)):
                                return e.matmul(
                                    PB[bnk][:, 0:512],
                                    ysb[:, kt, ts * 128:(ts + 1) * 128],
                                    woutT[:, kt, half * 512:(half + 1) * 512],
                                    start=st, stop=sp)
                            if kt == 7 and si == 1 and half == 1:
                                v_mmo = P.inc("tensor", "tensor", mmo)
                            else:
                                P.emit("tensor", mmo)
                P.wait("vector", "tensor", v_mmo)
                reb = ts % 2
                v_ev = None
                for half in range(2):
                    v_ev = P.inc("vector", "vector",
                                 lambda e, reb=reb, half=half, bnk=ob[half]:
                                 e.tensor_copy(
                                     out=rse[reb][0:128, half * 512:(half + 1) * 512],
                                     in_=PB[bnk][:, 0:512]))
                for half in range(2):
                    bank_free[ob[half]] = ("vector", v_ev)
                v_f_rs = vfence(v_ev)
                P.wait("sync", "gpsimd", v_f_rs)
                P.wait("sync", "vector", v_ev)
                v_rs_stores = sload(d_rs_in[t0 + ts * 128:t0 + (ts + 1) * 128, :],
                                    rse[reb][0:128, 0:D])
            v_ucread_last_prev = P.semv["tensor"]
            v_outproj_evac_prev = P.semv["vector"]

            if debug == 2:
                P.wait("gpsimd", "vector", v_gate_prev)
                if c == NCH - 1:
                    P.inc("gpsimd", "gpsimd", lambda e: e.dma_start(
                        out=ddbg2["ym3"].rearrange("p (a b) -> p a b", a=8),
                        in_=ymh[:]), 16)
                if c == 0:
                    P.inc("gpsimd", "gpsimd", lambda e: e.dma_start(
                        out=ddbg2["ymine"].rearrange("p (a b) -> p a b", a=8),
                        in_=ymh[:]), 16)
                    P.inc("gpsimd", "gpsimd", lambda e: e.dma_start(
                        out=ddbg2["dt"].rearrange("p (a b) -> p a b", a=8),
                        in_=dt_sb[:]), 16)
                P.inc("gpsimd", "gpsimd", lambda e, c=c: e.dma_start(
                    out=ddbg2["state"].rearrange("p (c a b) -> p c a b", c=NCH, a=8)[:, c],
                    in_=state[:]), 16)
            if debug is True and c == 0:
                P.wait("gpsimd", "vector", P.semv["vector"])
                P.wait("gpsimd", "scalar", P.semv["scalar"])
                for key, src_ap in (("u", u_sb[:, :, 1:TC + 1]), ("uc", uc_sb[:]),
                                    ("sz", sz_sb[:]), ("dt", dt_sb[:]),
                                    ("ymine", ymh[:])):
                    P.inc("gpsimd", "gpsimd",
                          lambda e, d=ddbg[key], s=src_ap: e.dma_start(
                              out=d.rearrange("p (a b) -> p a b", a=8), in_=s), 16)
                P.wait("sync", "vector", P.semv["vector"])
                sload(ddbg["xdbl"], xdbl[:])
            if c == 0:
                v_wq_ag = []
                for a, b in ((d_wgT_pre, d_wgT_blk), (d_wuT_pre, d_wuT_blk),
                             (d_wdT_pre, d_wdT_blk)):
                    P.wait("gpsimd", "sync", v_wq_store)
                    v_wq_ag.append(P.inc("gpsimd", "cc",
                                         lambda e, a=a, b=b: e.collective_compute(
                                             "AllGather", ALU.bypass,
                                             replica_groups=[[0, 1, 2, 3, 4, 5, 6, 7]],
                                             ins=[a], outs=[b])))

        P.wait("gpsimd", "sync", v_rs_stores)
        v_rs = P.inc("gpsimd", "cc", lambda e: e.collective_compute(
            "ReduceScatter", ALU.add,
            replica_groups=[[0, 1], [2, 3], [4, 5], [6, 7]],
            ins=[d_rs_in], outs=[d_rs_out]))

        if debug == 2:
            P.wait("gpsimd", "sync", v_rs_stores)
            P.inc("gpsimd", "gpsimd",
                  lambda e: e.dma_start(out=ddbg2["rsin"], in_=d_rs_in), 16)
            P.wait("gpsimd", "cc", v_rs)
            P.inc("gpsimd", "gpsimd",
                  lambda e: e.dma_start(out=ddbg2["rsout"], in_=d_rs_out), 16)
        P.wait("sync", "cc", v_wq_ag[-1])
        P.wait("sync", "tensor", P.semv["tensor"])
        P.wait("sync", "vector", P.semv["vector"])
        P.wait("sync", "scalar", P.semv["scalar"])
        barrier_s = P.inc("sync", "sync", lambda e: e.dma_start(
            out=d_abounce, in_=alog_t[:]), 16)
        for eng in ("tensor", "vector", "scalar", "gpsimd"):
            P.wait(eng, "sync", barrier_s)

        v_wd_l = 0
        for it in range(22):
            wdsrc = bass.AP(d_wdT_blk.tensor, it * 128 * DSL,
                            [[DSL, 128], [FI * DSL, 8], [1, DSL]])
            v_wd_l = sload(wdqT[:, it, :], wdsrc)

        P.wait("sync", "cc", v_rs)
        b_qf = [A.view(o, F32, [128, D]) for o in b_q]
        b_qi = A.view(b_ti, F32, [128, D])
        v_b2_r_prev = [0]
        v_b2_x2store_prev = [0]
        v_b2_tr_prev = [0]
        v_b2_ssq_prev = [0]
        for tt in range(16):
            if tt >= 1:
                P.wait("sync", "vector", v_b2_r_prev[0])
            v_rl = sload(rsld[tt % 2][:], d_rs_out[tt * 128:(tt + 1) * 128, :])
            v_xr = sload(x2l[tt % 2][:], dx_res[tt * 128:(tt + 1) * 128, :])
            P.wait("vector", "sync", max(v_rl, v_xr))
            P.wait("vector", "scalar", v_b2_ssq_prev[0])
            v_r = P.inc("vector", "vector", lambda e, tt=tt: e.scalar_tensor_tensor(
                out=r_sb[:], in0=x2l[tt % 2][:], scalar=2.0,
                in1=rsld[tt % 2][:], op0=ALU.mult, op1=ALU.add))
            v_b2_r_prev[0] = v_r
            P.wait("vector", "sync", v_b2_x2store_prev[0])
            v_x2, v_ssq = ln_tail(r_sb[:], x2_sb[:], v_r)
            v_b2_ssq_prev[0] = v_ssq
            v_f_x2 = vfence(v_x2)
            P.wait("sync", "gpsimd", v_f_x2)
            P.wait("sync", "vector", v_x2)
            v_b2_x2store_prev[0] = sload(d_x2[tt * 128:(tt + 1) * 128, :], x2_sb[:])
            if debug is True and tt == 0:
                P.wait("sync", "vector", P.semv["vector"])
                P.wait("sync", "scalar", P.semv["scalar"])
                sload(ddbg["rs"], rsld[0][:])
                sload(ddbg["x2"], x2_sb[:])
                sload(ddbg["r"], r_sb[:])
                for li, lk in enumerate(("sr", "ssq", "mu", "m2e", "mu2", "ve",
                                         "iv2", "rstd")):
                    P.inc("sync", "sync",
                          lambda e, li=li, lk=lk: e.dma_start(
                              out=ddbg["ln"][:, li:li + 1], in_=ln[lk][:],
                              allow_slow_non_contiguous=True), 16)
                P.wait("vector", "sync", P.semv["sync"])
                P.wait("scalar", "sync", P.semv["sync"])
            P.wait("vector", "tensor", v_b2_tr_prev[0])
            v_x2q = quant_fp4(x2_sb[:], x2q_sb[:], D, b_qf, b_qi, 128,
                              src_ready=("vector", v_x2))
            if debug is True and tt == 0:
                P.wait("gpsimd", "vector", v_x2q)
                P.inc("gpsimd", "gpsimd", lambda e: e.dma_start(
                    out=ddbg["x2q"], in_=x2q_sb[:]), 16)
            P.wait("tensor", "vector", v_x2q)
            for dt_ in range(8):
                bnk = 6 + (dt_ % 2)
                bank_wait("tensor", bnk)
                v_tr = P.inc("tensor", "tensor", lambda e, dt_=dt_, bnk=bnk:
                             e.transpose(PB[bnk][:, 0:64].bitcast(BF16),
                                         x2q_sb[:, dt_ * 128:(dt_ + 1) * 128],
                                         ident[:]))
                P.wait("vector", "tensor", v_tr)
                v_evt = P.inc("vector", "vector",
                              lambda e, dt_=dt_, tt=tt, bnk=bnk: e.tensor_copy(
                                  out=x2qT[:, dt_, tt * 128:(tt + 1) * 128],
                                  in_=PB[bnk][:, 0:64].bitcast(BF16)))
                bank_free[bnk] = ("vector", v_evt)
            v_b2_tr_prev[0] = P.semv["tensor"]
        v_x2qT_done = P.semv["vector"]

        c_qf = [A.view(o, F32, [128, 512]) for o in c_q]
        c_qi = A.view(c_hqld[0], F32, [128, 512])
        P.wait("tensor", "vector", v_x2qT_done)
        ics = [(k * ISL, (k + 1) * ISL) for k in range(8)]
        v_hq_store = 0
        v_ffn_mm_prev = [0]
        v_ffn_h_prev = [0]
        for ici, (i0, i1) in enumerate(ics):
            iw = i1 - i0
            cb = ici % 2
            if ici == 0:
                P.wait("sync", "vector", v_x2qT_done)
            else:
                P.wait("sync", "tensor", v_ffn_mm_prev[0])
            v_wg_l = 0
            for dt_ in range(8):
                r0 = ici * D + dt_ * 128
                v_wg_l = sload(wgc[cb][:, dt_, 0:iw], d_wgT_blk[r0:r0 + 128, :])
                v_wg_l = max(v_wg_l, sload(wuc[cb][:, dt_, 0:iw],
                                           d_wuT_blk[r0:r0 + 128, :]))
            P.wait("tensor", "sync", v_wg_l)
            for tt in range(16):
                bank_wait("tensor", 2)
                bank_wait("tensor", 3)
                for dt_ in range(8):
                    def mmg(e, tt=tt, dt_=dt_, cb=cb, iw=iw, st=(dt_ == 0),
                            sp=(dt_ == 7)):
                        return e.matmul(PB[2][:, 0:iw],
                                        x2qT[:, dt_, tt * 128:(tt + 1) * 128],
                                        wgc[cb][:, dt_, 0:iw], start=st, stop=sp)
                    if dt_ == 7:
                        v_mmg = P.inc("tensor", "tensor", mmg)
                    else:
                        P.emit("tensor", mmg)
                for dt_ in range(8):
                    def mmu(e, tt=tt, dt_=dt_, cb=cb, iw=iw, st=(dt_ == 0),
                            sp=(dt_ == 7)):
                        return e.matmul(PB[3][:, 0:iw],
                                        x2qT[:, dt_, tt * 128:(tt + 1) * 128],
                                        wuc[cb][:, dt_, 0:iw], start=st, stop=sp)
                    if dt_ == 7:
                        v_mmu = P.inc("tensor", "tensor", mmu)
                    else:
                        P.emit("tensor", mmu)
                P.wait("scalar", "tensor", v_mmg)
                P.wait("scalar", "vector", v_ffn_h_prev[0])
                v_ffn_mm_prev[0] = v_mmu
                v_sg = P.inc("scalar", "scalar", lambda e, iw=iw: e.activation(
                    out=sg_sb[:, 0:iw], in_=PB[2][:, 0:iw], func=AF.Silu))
                bank_free[2] = ("scalar", v_sg)
                P.wait("vector", "scalar", v_sg)
                P.wait("vector", "tensor", v_mmu)
                v_h = P.inc("vector", "vector", lambda e, iw=iw: e.tensor_tensor(
                    out=h_sb[:, 0:iw], in0=sg_sb[:, 0:iw], in1=PB[3][:, 0:iw],
                    op=ALU.mult))
                bank_free[3] = ("vector", v_h)
                v_ffn_h_prev[0] = v_h
                if debug is True and ici == 0 and tt == 0:
                    P.wait("gpsimd", "scalar", v_sg)
                    P.inc("gpsimd", "gpsimd", lambda e: e.dma_start(
                        out=ddbg["gate"], in_=sg_sb[:, 0:512]), 16)
                P.wait("vector", "sync", v_hq_store)
                v_hq = quant_fp4(h_sb[:, 0:iw], hq_sb[:, 0:iw], iw, c_qf, c_qi,
                                 128, src_ready=("vector", v_h))
                v_f_hq = vfence(v_hq)
                P.wait("sync", "gpsimd", v_f_hq)
                P.wait("sync", "vector", v_hq)
                v_hq_store = sload(d_hq[tt * 128:(tt + 1) * 128, i0:i1],
                                   hq_sb[:, 0:iw])
                if debug is True and tt == 0:
                    P.wait("gpsimd", "vector", v_hq)
                    P.inc("gpsimd", "gpsimd",
                          lambda e, i0=i0, i1=i1: e.dma_start(
                              out=ddbg["hq"][:, i0:i1], in_=hq_sb[:, 0:i1 - i0]), 16)

        P.wait("sync", "sync", v_hq_store)
        v_mmd_prev = [0]
        v_tr_prev = [0]
        v_r2_prev = [0]
        v_b4_ssq_prev = [0]
        for tt in range(16):
            if tt >= 1:
                P.wait("sync", "tensor", v_tr_prev[0])
                P.wait("sync", "vector", v_r2_prev[0])
            v_hl = sload(hqld[tt % 2][:], d_hq[tt * 128:(tt + 1) * 128, :])
            v_x2r = sload(x2l[tt % 2][:], d_x2[tt * 128:(tt + 1) * 128, :])
            P.wait("tensor", "sync", v_hl)
            P.wait("vector", "tensor", v_mmd_prev[0])
            for it in range(22):
                bnk = 6 + (it % 2)
                bank_wait("tensor", bnk)
                v_tr = P.inc("tensor", "tensor", lambda e, tt=tt, it=it, bnk=bnk:
                             e.transpose(PB[bnk][:, 0:64].bitcast(BF16),
                                         hqld[tt % 2][:, it * 128:(it + 1) * 128],
                                         ident[:]))
                P.wait("vector", "tensor", v_tr)
                v_evt = P.inc("vector", "vector", lambda e, it=it, bnk=bnk:
                              e.tensor_copy(out=hqT[:, it, :], in_=PB[bnk][:, 0:64].bitcast(BF16)))
                bank_free[bnk] = ("vector", v_evt)
            v_hqT_done = P.semv["vector"]
            v_tr_prev[0] = P.semv["tensor"]
            P.wait("gpsimd", "vector", v_hqT_done)
            v_echo_h = P.inc("gpsimd", "gpsimd", lambda e: e.engine_nop(), 1)
            P.wait("tensor", "gpsimd", v_echo_h)
            P.wait("tensor", "vector", v_hqT_done)
            if tt == 0:
                P.wait("tensor", "sync", v_wd_l)
            db = (0 + 2 * (tt % 2), 1 + 2 * (tt % 2))
            for bnk in db:
                bank_wait("tensor", bnk)
            for it in range(22):
                for half in range(2):
                    def mmd(e, it=it, half=half, bnk=db[half], st=(it == 0),
                            sp=(it == 21)):
                        return e.matmul(PB[bnk][:, 0:512], hqT[:, it, :],
                                        wdqT[:, it, half * 512:(half + 1) * 512],
                                        start=st, stop=sp)
                    if it == 21 and half == 1:
                        v_mmd = P.inc("tensor", "tensor", mmd)
                    else:
                        P.emit("tensor", mmd)
            P.wait("vector", "tensor", v_mmd)
            P.wait("vector", "sync", v_x2r)
            P.wait("vector", "scalar", v_b4_ssq_prev[0])
            v_r2 = None
            for half in range(2):
                v_r2 = P.inc("vector", "vector",
                             lambda e, tt=tt, half=half, bnk=db[half]:
                             e.tensor_tensor(
                                 out=r_sb[:, half * 512:(half + 1) * 512],
                                 in0=PB[bnk][:, 0:512],
                                 in1=x2l[tt % 2][:, half * 512:(half + 1) * 512],
                                 op=ALU.add))
            for half in range(2):
                bank_free[db[half]] = ("vector", v_r2)
            v_mmd_prev[0] = v_mmd
            v_r2_prev[0] = v_r2
            if tt >= 2:
                P.wait("vector", "out", 16 * (tt - 1))
            v_o, v_ssq4 = ln_tail(r_sb[:], out_sb[tt % 2][:], v_r2)
            v_b4_ssq_prev[0] = v_ssq4
            v_f_o = vfence(v_o)
            P.wait("sync", "gpsimd", v_f_o)
            P.wait("sync", "vector", v_o)
            P.inc("sync", "out", lambda e, tt=tt: e.dma_start(
                out=dout[tt * 128:(tt + 1) * 128, :], in_=out_sb[tt % 2][:]), 16)

        P.wait("sync", "out", P.semv["out"])

        P.run(block)
    return nc


_NC_CACHE = [None]


def make_in_maps(x, W_in, conv_w, conv_b, W_xp, W_dt, b_dt, A_log, Dv, W_out,
                 wg, wu, wd):
    ident = np.eye(128, dtype=np.float32)
    in_maps = []
    for c in range(8):
        s, j = c // 2, c % 2
        ch = slice(j * HC, (j + 1) * HC)
        in_maps.append({
            "xT": np.ascontiguousarray(x[s].T),
            "xres": np.ascontiguousarray(x[s, j * TH:(j + 1) * TH, :]),
            "winT": np.ascontiguousarray(np.concatenate(
                [W_in[ch].T, W_in[H + j * HC:H + (j + 1) * HC].T], axis=1)),
            "convw": np.ascontiguousarray(conv_w[ch, 0, :]),
            "convb": np.ascontiguousarray(conv_b[ch].reshape(8, 128).T),
            "wxpT": np.ascontiguousarray(W_xp[:, ch].T),
            "wdtT": np.ascontiguousarray(W_dt[ch].T),
            "bdt": np.ascontiguousarray(b_dt[ch].reshape(8, 128).T),
            "alog": np.ascontiguousarray(A_log[None, :]),
            "dvec": np.ascontiguousarray(Dv[ch].reshape(8, 128).T),
            "woutT": np.ascontiguousarray(W_out[:, ch].T),
            "wg_s": np.ascontiguousarray(wg[c * ISL:(c + 1) * ISL]),
            "wu_s": np.ascontiguousarray(wu[c * ISL:(c + 1) * ISL]),
            "wd_s": np.ascontiguousarray(wd[c * DSL:(c + 1) * DSL]),
            "identin": ident,
        })
    return in_maps


def kernel(L0_attn_x, L0_attn_W_in, L0_attn_conv_w, L0_attn_conv_b,
           L0_attn_W_xproj, L0_attn_W_dt, L0_attn_b_dt, L0_attn_A_log,
           L0_attn_D, L0_attn_W_out, L0_ffn_w_gate, L0_ffn_w_up, L0_ffn_w_down):
    in_maps = make_in_maps(
        np.asarray(L0_attn_x, np.float32), np.asarray(L0_attn_W_in, np.float32),
        np.asarray(L0_attn_conv_w, np.float32),
        np.asarray(L0_attn_conv_b, np.float32),
        np.asarray(L0_attn_W_xproj, np.float32),
        np.asarray(L0_attn_W_dt, np.float32),
        np.asarray(L0_attn_b_dt, np.float32),
        np.asarray(L0_attn_A_log, np.float32),
        np.asarray(L0_attn_D, np.float32), np.asarray(L0_attn_W_out, np.float32),
        np.asarray(L0_ffn_w_gate, np.float32),
        np.asarray(L0_ffn_w_up, np.float32),
        np.asarray(L0_ffn_w_down, np.float32))
    if _NC_CACHE[0] is None:
        _NC_CACHE[0] = build()
    nc = _NC_CACHE[0]
    res = run_bass_kernel_spmd(nc, in_maps, core_ids=list(range(8)))
    out = np.empty((B, T, D), np.float32)
    for c in range(8):
        s, j = c // 2, c % 2
        out[s, j * TH:(j + 1) * TH, :] = res.results[c]["out"]
    return out


if __name__ == "__main__":
    print("building ...")
    nc = build()
    n = sum(len(b.instructions) for f in nc.m.functions for b in f.blocks)
    print("built, instructions:", n)


# version 16
# speedup vs baseline: 1.0183x; 1.0183x over previous
import contextlib
import numpy as np
import concourse.bass as bass
import concourse.mybir as mybir
from concourse.bass_utils import run_bass_kernel_spmd

F32 = mybir.dt.float32
BF16 = mybir.dt.bfloat16
F16 = mybir.dt.float16
I32 = mybir.dt.int32
U8 = mybir.dt.uint8
AF = mybir.ActivationFunctionType
ALU = mybir.AluOpType
AX = mybir.AxisListType

B, T, D = 4, 4096, 1024
H, HC, NST, DTR, FI = 2048, 1024, 16, 64, 2816
TC = 1024
NCH = T // TC
TH = 2048
LN_EPS = 1e-5
ISL = FI // 8
DSL = D // 8

ENGS = ("sync", "scalar", "vector", "tensor", "gpsimd")


class Prog:
    def __init__(self, nc, es):
        self.nc = nc
        self.q = {e: [] for e in ENGS}
        self.semh = {}
        self.semv = {}
        for s in ENGS + ("cc", "out"):
            self.semh[s] = es.enter_context(nc.semaphore(f"sem_{s}"))
            self.semv[s] = 0

    def emit(self, eng, fn):
        self.q[eng].append(fn)

    def wait(self, eng, sem, val):
        if val and val > 0:
            h = self.semh[sem]
            self.q[eng].append(lambda e, h=h, v=val: e.wait_ge(h, v))

    def inc(self, eng, sem, fn, k=1):
        self.semv[sem] += k
        tgt = self.semv[sem]
        h = self.semh[sem]
        self.q[eng].append(lambda e, f=fn, h=h, kk=k: f(e).then_inc(h, kk))
        return tgt

    def run(self, block):
        @block.sync
        def _(e):
            for f in self.q["sync"]:
                f(e)

        @block.scalar
        def _(e):
            for f in self.q["scalar"]:
                f(e)

        @block.vector
        def _(e):
            for f in self.q["vector"]:
                f(e)

        @block.tensor
        def _(e):
            for f in self.q["tensor"]:
                f(e)

        @block.gpsimd
        def _(e):
            for f in self.q["gpsimd"]:
                f(e)


def cdiv(a, b):
    return (a + b - 1) // b


ARENA_BYTES = 194688


class Arena:

    def __init__(self, ap):
        self.ap = ap
        self.off = 0

    def alloc(self, nbytes, align=512):
        o = (self.off + align - 1) // align * align
        self.off = o + nbytes
        assert self.off <= ARENA_BYTES, f"arena overflow {self.off}"
        return o

    def view(self, off, dtype, shape):
        esz = {F32: 4, BF16: 2, F16: 2, I32: 4}[dtype]
        n = int(np.prod(shape[1:])) if len(shape) > 1 else 1
        ap = self.ap[:, off:off + n * esz].bitcast(dtype)
        if len(shape) > 2:
            dims = " ".join(f"d{i}" for i in range(1, len(shape)))
            ap = ap.rearrange(f"p ({dims}) -> p {dims}",
                              **{f"d{i}": shape[i] for i in range(1, len(shape) - 1)})
        return ap


def build(debug=False):
    nc = bass.Bass("TRN2", target_bir_lowering=False,
                   detect_race_conditions=False)

    dx_T = nc.dram_tensor("xT", [D, T], F32, kind="ExternalInput").ap()
    dx_res = nc.dram_tensor("xres", [TH, D], F32, kind="ExternalInput").ap()
    dwinT = nc.dram_tensor("winT", [D, 2 * HC], F32, kind="ExternalInput").ap()
    dconvw = nc.dram_tensor("convw", [HC, 3], F32, kind="ExternalInput").ap()
    dconvb = nc.dram_tensor("convb", [128, 8], F32, kind="ExternalInput").ap()
    dwxpT = nc.dram_tensor("wxpT", [HC, 96], F32, kind="ExternalInput").ap()
    dwdtT = nc.dram_tensor("wdtT", [DTR, HC], F32, kind="ExternalInput").ap()
    dbdt = nc.dram_tensor("bdt", [128, 8], F32, kind="ExternalInput").ap()
    dalog = nc.dram_tensor("alog", [1, NST], F32, kind="ExternalInput").ap()
    ddvec = nc.dram_tensor("dvec", [128, 8], F32, kind="ExternalInput").ap()
    dwoutT = nc.dram_tensor("woutT", [HC, D], F32, kind="ExternalInput").ap()
    dwg_s = nc.dram_tensor("wg_s", [ISL, D], F32, kind="ExternalInput").ap()
    dwu_s = nc.dram_tensor("wu_s", [ISL, D], F32, kind="ExternalInput").ap()
    dwd_s = nc.dram_tensor("wd_s", [DSL, FI], F32, kind="ExternalInput").ap()
    dident = nc.dram_tensor("identin", [128, 128], F32, kind="ExternalInput").ap()

    dout = nc.dram_tensor("out", [TH, D], F32, kind="ExternalOutput").ap()

    if debug == 2:
        ddbg2 = {"ymine": nc.dram_tensor("dbg2_ymine", [128, 8 * TC], F32,
                                         kind="ExternalOutput").ap(),
                 "state": nc.dram_tensor("dbg2_state", [128, 8 * NST * NCH], F32,
                                         kind="ExternalOutput").ap(),
                 "dt": nc.dram_tensor("dbg2_dt", [128, 8 * TC], F32,
                                      kind="ExternalOutput").ap(),
                 "rsin": nc.dram_tensor("dbg2_rsin", [T, D], F32,
                                        kind="ExternalOutput").ap(),
                 "rsout": nc.dram_tensor("dbg2_rsout", [TH, D], F32,
                                         kind="ExternalOutput").ap(),
                 "ym3": nc.dram_tensor("dbg2_ym3", [128, 8 * TC], F32,
                                       kind="ExternalOutput").ap()}
    if debug is True:
        ddbg = {k: nc.dram_tensor(f"dbg_{k}", [128, 8 * TC], F32,
                                  kind="ExternalOutput").ap()
                for k in ("u", "uc", "sz", "dt", "ymine")}
        ddbg["xdbl"] = nc.dram_tensor("dbg_xdbl", [96, TC], F32,
                                      kind="ExternalOutput").ap()
        ddbg["rs"] = nc.dram_tensor("dbg_rs", [128, D], F32,
                                    kind="ExternalOutput").ap()
        ddbg["x2"] = nc.dram_tensor("dbg_x2", [128, D], F32,
                                    kind="ExternalOutput").ap()
        ddbg["x2q"] = nc.dram_tensor("dbg_x2q", [128, D], F32,
                                     kind="ExternalOutput").ap()
        ddbg["hq"] = nc.dram_tensor("dbg_hq", [128, FI], F32,
                                    kind="ExternalOutput").ap()
        ddbg["gate"] = nc.dram_tensor("dbg_gate", [128, 512], F32,
                                      kind="ExternalOutput").ap()
        ddbg["wgT"] = nc.dram_tensor("dbg_wgT", [128, FI], F32,
                                     kind="ExternalOutput").ap()
        ddbg["wgq"] = nc.dram_tensor("dbg_wgq", [FI, D], F32,
                                     kind="ExternalOutput").ap()
        ddbg["ln"] = nc.dram_tensor("dbg_ln", [128, 8], F32,
                                    kind="ExternalOutput").ap()
        ddbg["r"] = nc.dram_tensor("dbg_r", [128, D], F32,
                                   kind="ExternalOutput").ap()
    d_abounce = nc.dram_tensor("a_bounce", [1, NST], F32, kind="Internal").ap()
    d_xdbl_in = nc.dram_tensor("cc_xdbl_in", [96, TC], F32, kind="Internal").ap()
    d_xdbl_out = nc.dram_tensor("cc_xdbl_out", [96, TC], F32, kind="Internal").ap()
    d_rs_in = nc.dram_tensor("cc_rs_in", [T, D], F32, kind="Internal").ap()
    d_rs_out = nc.dram_tensor("cc_rs_out", [TH, D], F32, kind="Internal").ap()
    d_wgT_pre = nc.dram_tensor("cc_wgT_pre", [D, ISL], BF16, kind="Internal").ap()
    d_wgT_blk = nc.dram_tensor("cc_wgT_blk", [8 * D, ISL], BF16, kind="Internal").ap()
    d_wuT_pre = nc.dram_tensor("cc_wuT_pre", [D, ISL], BF16, kind="Internal").ap()
    d_wuT_blk = nc.dram_tensor("cc_wuT_blk", [8 * D, ISL], BF16, kind="Internal").ap()
    d_wdT_pre = nc.dram_tensor("cc_wdT_pre", [FI, DSL], BF16, kind="Internal").ap()
    d_wdT_blk = nc.dram_tensor("cc_wdT_blk", [8 * FI, DSL], BF16, kind="Internal").ap()
    d_x2 = nc.dram_tensor("x2_scr", [TH, D], F32, kind="Internal").ap()
    d_hq = nc.dram_tensor("hq_scr", [TH, FI], BF16, kind="Internal").ap()

    es = contextlib.ExitStack()
    with es:
        sb = lambda name, shape, dt: es.enter_context(nc.sbuf_tensor(name, shape, dt))

        abc = sb("sb_abc", [128, NST], F32)
        ident = sb("sb_ident", [128, 128], BF16)
        convw = sb("sb_convw", [128, 8, 3], F32)
        convb = sb("sb_convb", [128, 8], F32)
        bdt = sb("sb_bdt", [128, 8], F32)
        dvec = sb("sb_dvec", [128, 8], F32)
        state = sb("sb_state", [128, 8, NST], F32)
        state_rd = sb("sb_state_rd", [128, 8, NST], F32)
        wxpT_h = sb("sb_wxpT_h", [128, 8, 96], BF16)
        wxpT_l = sb("sb_wxpT_l", [128, 8, 96], BF16)
        wdtT_h = sb("sb_wdtT_h", [DTR, HC], BF16)
        wdtT_l = sb("sb_wdtT_l", [DTR, HC], BF16)
        dtp = sb("sb_dtp", [DTR, TC], BF16)
        xdbl = sb("sb_xdbl", [96, TC], F32)
        alog_t = sb("sb_alog_t", [1, NST], F32)
        aneg_t = sb("sb_aneg_t", [1, NST], F32)
        ln = {k: sb(f"sb_ln_{k}", [128, 1], F32)
              for k in ("sr", "ssq", "mu", "m2e", "mu2", "ve", "iv2", "rstd")}
        qsml = {k: sb(f"sb_q_{k}", [128, FI // 16], F32) for k in ("m", "sh", "rm", "rm2")}

        arena_t = sb("sb_arena", [128, ARENA_BYTES], U8)
        A = Arena(arena_t[:])

        o_winT = A.alloc(8 * 2048 * 2)
        o_xTc = A.alloc(8 * (TC + 2) * 2, align=64)
        o_u = A.alloc(8 * (TC + 2) * 2, align=64)
        o_uc = A.alloc(8 * TC * 2, align=64)
        o_sz = A.alloc(8 * TC * 2, align=64)
        o_dt = A.alloc(8 * TC * 2, align=64)
        o_dA = [A.alloc(2 * TC * 4, align=64) for _ in range(2)]
        o_bu = A.alloc(TC * 2, align=64)
        o_pb = [A.alloc(TC * 2, align=64) for _ in range(6)]
        o_hbuf = A.alloc(4 * TC * 2, align=64)
        o_ymh = A.alloc(8 * TC * 2, align=64)
        o_woutT = A.alloc(8 * D * 2, align=64)
        o_rse = [A.alloc(D * 4, align=64) for _ in range(2)]
        mix_end = A.off
        o_BC = o_xTc
        o_yacc = o_u
        o_yml = o_dt

        winT = A.view(o_winT, BF16, [128, 8, 2048])
        xTc = A.view(o_xTc, BF16, [128, 8, TC + 2])
        u_sb = A.view(o_u, F16, [128, 8, TC + 2])
        uc_sb = A.view(o_uc, BF16, [128, 8, TC])
        sz_sb = A.view(o_sz, F16, [128, 8, TC])
        dt_sb = A.view(o_dt, F16, [128, 8, TC])
        dA = [A.view(o, F32, [128, 2, TC]) for o in o_dA]
        bu = A.view(o_bu, F16, [128, TC])
        pb = [A.view(o, F16, [128, TC]) for o in o_pb]
        hbuf = A.view(o_hbuf, F16, [128, 4, TC])
        ymh = A.view(o_ymh, BF16, [128, 8, TC])
        yml = A.view(o_yml, BF16, [128, 8, TC])
        yacc = A.view(o_yacc, F16, [128, 2, 4, TC])
        woutT = A.view(o_woutT, BF16, [128, 8, D])
        rse = [A.view(o, F32, [128, D]) for o in o_rse]
        sp_tmp = A.view(o_rse[1], F32, [128, 2, 512])
        BC = A.view(o_BC, F16, [128, 4, 2, TC])
        BCflat = A.view(o_BC, F16, [128, 4 * 2 * TC])
        wxf = A.view(o_sz, F32, [128, 8, 96])
        wdf = A.view(o_sz + 4096, F32, [128, HC])

        q_big_offs = [o_dt, o_ymh, o_woutT, o_xTc]

        oB = Arena(arena_t[:])
        oB.off = 0
        b_x2qT = oB.alloc(8 * TH * 2)
        b_rs = [oB.alloc(D * 4) for _ in range(2)]
        b_r = oB.alloc(D * 4)
        b_x2 = oB.alloc(D * 4)
        b_x2q = oB.alloc(D * 2)
        b_junk = oB.alloc(D * 2)
        b_q = [oB.alloc(D * 4) for _ in range(4)]
        b_wld = [oB.alloc(FI * 2) for _ in range(2)]
        b_ev = oB.alloc(1024 * 2)
        b_ti = oB.alloc(D * 4)
        assert oB.off <= ARENA_BYTES

        x2qT = A.view(b_x2qT, BF16, [128, 8, TH])
        rsld = [A.view(o, F32, [128, D]) for o in b_rs]
        r_sb = A.view(b_r, F32, [128, D])
        x2_sb = A.view(b_x2, F32, [128, D])
        x2q_sb = A.view(b_x2q, BF16, [128, D])
        junk_sb = A.view(b_junk, BF16, [128, D])
        wld = [A.view(o, BF16, [128, FI]) for o in b_wld]
        ev_sb = A.view(b_ev, BF16, [128, 1024])

        oC = Arena(arena_t[:])
        oC.off = b_rs[0]
        c_wg = [oC.alloc(8 * 512 * 2) for _ in range(2)]
        c_wu = [oC.alloc(8 * 512 * 2) for _ in range(2)]
        c_h = oC.alloc(512 * 4)
        c_sg = oC.alloc(512 * 2)
        c_hq = oC.alloc(512 * 2)
        c_q = [oC.alloc(512 * 4) for _ in range(4)]
        c_wdqT = oC.alloc(22 * 1024 * 2)
        c_hqld = [oC.alloc(FI * 2) for _ in range(2)]
        c_hqT = oC.alloc(22 * 128 * 2)
        c_x2l = [oC.alloc(D * 4) for _ in range(2)]
        c_out = [oC.alloc(D * 4) for _ in range(2)]
        assert oC.off <= ARENA_BYTES, f"phaseC overflow {oC.off}"

        wgc = [A.view(o, BF16, [128, 8, 512]) for o in c_wg]
        wuc = [A.view(o, BF16, [128, 8, 512]) for o in c_wu]
        h_sb = A.view(c_h, F32, [128, 512])
        sg_sb = A.view(c_sg, F16, [128, 512])
        hq_sb = A.view(c_hq, BF16, [128, 512])
        wdqT = A.view(c_wdqT, BF16, [128, 22, 1024])
        hqld = [A.view(o, BF16, [128, FI]) for o in c_hqld]
        hqT = A.view(c_hqT, BF16, [128, 22, 128])
        x2l = [A.view(o, F32, [128, D]) for o in c_x2l]
        out_sb = [A.view(o, F32, [128, D]) for o in c_out]

        PB = [es.enter_context(nc.psum_tensor(f"pb{i}", [128, 512], F32))
              for i in range(8)]

        block = es.enter_context(nc.Block())
        P = Prog(nc, es)
        bank_free = {i: None for i in range(8)}

        def bank_wait(eng, i):
            if bank_free[i] is not None:
                s, v = bank_free[i]
                P.wait(eng, s, v)

        def vfence(v_val):
            P.wait("gpsimd", "vector", v_val)
            return P.inc("gpsimd", "gpsimd", lambda e: e.engine_nop(), 1)

        def sload(dst, src):
            return P.inc("sync", "sync",
                         lambda e, d=dst, s=src: e.dma_start(out=d, in_=s), 16)

        def gload(dst, src):
            return P.inc("gpsimd", "gpsimd",
                         lambda e, d=dst, s=src: e.dma_start(out=d, in_=s), 16)

        v_convw = sload(convw[:],
                        dconvw.rearrange("(a p) b -> p a b", p=128))
        v_convb = sload(convb[:], dconvb)
        v_bdt = sload(bdt[:], dbdt)
        v_dvec = sload(dvec[:], ddvec)
        v_alog = sload(alog_t[:], dalog)
        v_ident = gload(ident[:], dident)
        v_wxf = sload(wxf[:], dwxpT.rearrange("(a p) b -> p a b", p=128))
        v_wdf = sload(wdf[0:DTR, :], dwdtT)
        P.wait("vector", "sync", max(v_wxf, v_wdf))
        P.inc("vector", "vector", lambda e: e.tensor_copy(out=wxpT_h[:], in_=wxf[:]))
        P.inc("vector", "vector", lambda e: e.scalar_tensor_tensor(
            out=wxpT_l[:], in0=wxpT_h[:], scalar=-1.0, in1=wxf[:],
            op0=ALU.mult, op1=ALU.add))
        P.inc("vector", "vector",
              lambda e: e.tensor_copy(out=wdtT_h[:], in_=wdf[0:DTR, :]))
        v_wsplit = P.inc("vector", "vector", lambda e: e.scalar_tensor_tensor(
            out=wdtT_l[:], in0=wdtT_h[:], scalar=-1.0, in1=wdf[0:DTR, :],
            op0=ALU.mult, op1=ALU.add))
        v_f_wsplit = vfence(v_wsplit)

        P.wait("scalar", "sync", v_alog)
        v_ax = P.inc("scalar", "scalar",
                     lambda e: e.activation(out=alog_t[:], in_=alog_t[:], func=AF.Exp))
        P.wait("gpsimd", "scalar", v_ax)
        P.inc("gpsimd", "gpsimd",
              lambda e: e.tensor_copy(out=aneg_t[:], in_=alog_t[:]), 1)
        v_an = P.inc("gpsimd", "gpsimd",
                     lambda e: e.tensor_scalar_mul(aneg_t[:], aneg_t[:], -1.0), 1)
        P.wait("sync", "gpsimd", v_an)
        v_ab = sload(d_abounce, aneg_t[:])
        P.wait("sync", "sync", v_ab)
        v_abc = sload(abc[:], d_abounce.broadcast_to([128, NST]))

        v_st0 = P.inc("vector", "vector", lambda e: e.memset(state[:], 0.0))

        q_prev_use = [0]

        def quant_fp4(src_ap, dst_ap, F, scratch_f32, scratch_i32, pc=128,
                      src_ready=None):
            nb = F // 16
            m = qsml["m"][0:pc, 0:nb]
            sh = qsml["sh"][0:pc, 0:nb]
            rm = qsml["rm"][0:pc, 0:nb]
            rm2 = qsml["rm2"][0:pc, 0:nb]
            asrc, aw, t1, t2 = [sc[0:pc, 0:F] for sc in scratch_f32]
            src3 = src_ap.rearrange("p (b k) -> p b k", k=16)
            if src_ready is not None:
                P.wait("vector", src_ready[0], src_ready[1])
                P.wait("scalar", src_ready[0], src_ready[1])
            v_m = P.inc("vector", "vector", lambda e: e.tensor_reduce(
                out=m, in_=src3, axis=AX.X, op=ALU.max, apply_absolute_value=True))
            v_as = P.inc("scalar", "scalar",
                         lambda e: e.activation(out=asrc, in_=src_ap, func=AF.Abs))
            v_sg = P.inc("scalar", "scalar",
                         lambda e: e.activation(out=aw, in_=src_ap, func=AF.Sign))
            P.wait("gpsimd", "vector", v_m)
            P.wait("gpsimd", "scalar", v_sg)
            P.wait("gpsimd", "vector", q_prev_use[0])
            v_sh = P.inc("gpsimd", "gpsimd", lambda e: e.tensor_scalar(
                out=sh, in0=m, scalar1=1.0 / 12.0, scalar2=5e-13,
                op0=ALU.mult, op1=ALU.max), 1)
            P.wait("vector", "gpsimd", v_sh)
            v_rm = P.inc("vector", "vector", lambda e: e.reciprocal(out=rm, in_=sh))
            P.wait("scalar", "vector", v_rm)
            v_rm2 = P.inc("scalar", "scalar",
                          lambda e: e.copy(rm2, rm))
            P.wait("vector", "scalar", v_rm2)
            P.inc("vector", "vector", lambda e: e.tensor_tensor(
                out=t1.rearrange("p (b k) -> p b k", k=16),
                in0=asrc.rearrange("p (b k) -> p b k", k=16),
                in1=rm2.unsqueeze(-1).broadcast_to([pc, nb, 16]), op=ALU.mult))
            P.inc("vector", "vector", lambda e: e.tensor_scalar(
                out=asrc, in0=t1, scalar1=0.5, scalar2=1.0,
                op0=ALU.is_gt, op1=ALU.mult))
            for mj in (1.5, 2.5, 3.5):
                P.inc("vector", "vector", lambda e, mj=mj: e.scalar_tensor_tensor(
                    out=asrc, in0=t1, scalar=mj, in1=asrc,
                    op0=ALU.is_gt, op1=ALU.add))
            P.inc("vector", "vector", lambda e: e.tensor_scalar(
                out=t2, in0=t1, scalar1=10.0, scalar2=2.0,
                op0=ALU.is_gt, op1=ALU.mult))
            for mj in (5.0, 7.0):
                P.inc("vector", "vector", lambda e, mj=mj: e.scalar_tensor_tensor(
                    out=t2, in0=t1, scalar=mj, in1=t2,
                    op0=ALU.is_gt, op1=ALU.add))
            P.inc("vector", "vector", lambda e: e.scalar_tensor_tensor(
                out=asrc, in0=t2, scalar=2.0, in1=asrc,
                op0=ALU.mult, op1=ALU.add))
            P.wait("vector", "scalar", v_sg)
            P.inc("vector", "vector", lambda e: e.tensor_tensor(
                out=t2.rearrange("p (b k) -> p b k", k=16),
                in0=aw.rearrange("p (b k) -> p b k", k=16),
                in1=sh.unsqueeze(-1).broadcast_to([pc, nb, 16]), op=ALU.mult))
            v_q = P.inc("vector", "vector",
                        lambda e: e.tensor_mul(dst_ap, asrc, t2))
            q_prev_use[0] = v_q
            return v_q

        ln_prev_use = [0]

        def ln_tail(r_ap, out_ap, v_r):
            v_sr = P.inc("vector", "vector", lambda e: e.tensor_reduce(
                out=ln["sr"][:], in_=r_ap, axis=AX.X, op=ALU.add))
            P.wait("scalar", "vector", v_r)
            v_ssq = P.inc("scalar", "scalar", lambda e: e.activation(
                out=junk_sb[:], in_=r_ap, func=AF.Square, accum_out=ln["ssq"][:]))
            P.wait("gpsimd", "vector", v_sr)
            P.wait("gpsimd", "scalar", v_ssq)
            P.wait("gpsimd", "vector", ln_prev_use[0])
            P.inc("gpsimd", "gpsimd", lambda e: e.tensor_scalar(
                out=ln["mu"][:], in0=ln["sr"][:], scalar1=1.0 / D, scalar2=None,
                op0=ALU.mult), 1)
            P.inc("gpsimd", "gpsimd", lambda e: e.tensor_scalar(
                out=ln["m2e"][:], in0=ln["ssq"][:], scalar1=1.0 / D,
                scalar2=LN_EPS, op0=ALU.mult, op1=ALU.add), 1)
            P.inc("gpsimd", "gpsimd", lambda e: e.tensor_mul(
                ln["mu2"][:], ln["mu"][:], ln["mu"][:]), 1)
            v_ve = P.inc("gpsimd", "gpsimd", lambda e: e.tensor_sub(
                ln["ve"][:], ln["m2e"][:], ln["mu2"][:]), 1)
            P.wait("vector", "gpsimd", v_ve)
            v_iv = P.inc("vector", "vector", lambda e: e.reciprocal(
                out=ln["iv2"][:], in_=ln["ve"][:]))
            P.wait("scalar", "vector", v_iv)
            v_rstd = P.inc("scalar", "scalar", lambda e: e.activation(
                out=ln["rstd"][:], in_=ln["iv2"][:], func=AF.Sqrt))
            P.wait("vector", "scalar", v_rstd)
            v_out = P.inc("vector", "vector", lambda e: e.tensor_scalar(
                out=out_ap, in0=r_ap, scalar1=ln["mu"][:], scalar2=ln["rstd"][:],
                op0=ALU.subtract, op1=ALU.mult))
            ln_prev_use[0] = v_out
            return v_out, v_ssq

        wq_scr_f = [A.view(o, F32, [128, FI]) for o in q_big_offs]
        wq_ti = None
        wq_in_v = A.view(o_winT, F32, [128, 3, D])
        wq_in_d = A.view(o_winT, F32, [128, FI])
        wq_out_v = A.view(o_uc, BF16, [128, 3, D])
        wq_out_d = A.view(o_uc, BF16, [128, FI])

        wq_plan = [(dwg_s, d_wgT_pre, ISL, D, wq_in_v, wq_out_v),
                   (dwu_s, d_wuT_pre, ISL, D, wq_in_v, wq_out_v),
                   (dwd_s, d_wdT_pre, DSL, FI, wq_in_d, wq_out_d)]
        v_wq_store = 0
        wq_ev = [A.view(o_sz + 8192 + 256 * i, BF16, [128, 128]) for i in range(2)]
        P.wait("tensor", "gpsimd", v_ident)
        for (dsrc, ddstT, R, F, vin, vout) in wq_plan:
            nt = cdiv(R, 128)
            for rt in range(nt):
                p0, p1 = rt * 128, min((rt + 1) * 128, R)
                pc = p1 - p0
                vv = sload(vin[0:pc, rt, :] if nt > 1 else vin[0:pc, :],
                           dsrc[p0:p1, :])
                P.wait("vector", "sync", vv)
                v_q = quant_fp4(vin[0:pc, rt, :] if nt > 1 else vin[0:pc, :],
                                vout[0:pc, rt, :] if nt > 1 else vout[0:pc, :],
                                F, wq_scr_f, wq_ti, pc, src_ready=("sync", vv))
                v_f_wq = vfence(v_q)
                P.wait("tensor", "gpsimd", v_f_wq)
                for ft in range(F // 128):
                    bnk = 6 + (ft % 2)
                    bank_wait("tensor", bnk)
                    tin = (vout[0:pc, rt, ft * 128:(ft + 1) * 128] if nt > 1
                           else vout[0:pc, ft * 128:(ft + 1) * 128])
                    v_tr = P.inc("tensor", "tensor",
                                 lambda e, tin=tin, bnk=bnk, pc=pc: e.transpose(
                                     PB[bnk][:, 0:64].bitcast(BF16)[:, 0:pc],
                                     tin, ident[0:pc, 0:pc]))
                    P.wait("vector", "tensor", v_tr)
                    P.wait("vector", "sync", v_wq_store)
                    v_evt = P.inc("vector", "vector",
                                  lambda e, ft=ft, bnk=bnk, pc=pc: e.tensor_copy(
                                      out=wq_ev[ft % 2][:, 0:pc],
                                      in_=PB[bnk][:, 0:64].bitcast(BF16)[:, 0:pc]))
                    bank_free[bnk] = ("vector", v_evt)
                    v_f_ev = vfence(v_evt)
                    P.wait("sync", "gpsimd", v_f_ev)
                    P.wait("sync", "vector", v_evt)
                    v_wq_store = sload(ddstT[ft * 128:(ft + 1) * 128, p0:p1],
                                       wq_ev[ft % 2][:, 0:pc])
        v_wq_quant_done = P.semv["vector"]
        v_wq_sgn_done = P.semv["scalar"]

        v_bc_read_prev = 0
        v_xdo_prev = 0
        v_inproj_last_prev = 0
        v_uevac_read_prev = 0
        v_dA_last_prev = 0
        v_gate_prev = 0
        v_ucread_last_prev = 0
        v_ar_prev = 0
        v_rs_stores = 0
        v_wq_ag = None
        v_outproj_evac_prev = 0
        v_gy_last_prev = 0
        KT_G = (2, 3)
        PSLOT = {0: (0, 0), 1: (1, 1), 2: (2, 3), 3: (4, 5)}
        g_last = {2: [0, 0], 3: [0, 0]}

        for c in range(NCH):
            t0 = c * TC
            lo = max(t0 - 1, 0)
            hi = min(t0 + TC + 1, T)
            ofs = lo - (t0 - 1)

            if c == 0:
                P.wait("gpsimd", "vector", v_wq_quant_done)
                P.wait("gpsimd", "scalar", v_wq_sgn_done)
                P.wait("gpsimd", "sync", v_wq_store)
                P.wait("scalar", "vector", v_wq_quant_done)
                P.wait("scalar", "sync", v_wq_store)
            else:
                P.wait("gpsimd", "tensor", v_inproj_last_prev)
                P.wait("gpsimd", "vector", v_bc_read_prev)
            v_xc = 0
            for dt_ in range(8):
                v_xc = gload(xTc[:, dt_, ofs:ofs + (hi - lo)],
                             dx_T[dt_ * 128:(dt_ + 1) * 128, lo:hi])
            if c == 0:
                v_wl = gload(winT[:],
                             dwinT.rearrange("(a p) b -> p a b", p=128))
                v_wo = gload(woutT[:],
                             dwoutT.rearrange("(a p) b -> p a b", p=128))
                v_xc = max(v_xc, v_wl, v_wo)

            if c == 0:
                P.inc("vector", "vector", lambda e: e.memset(u_sb[:, :, 0:1], 0.0))
            if c == NCH - 1:
                P.inc("vector", "vector",
                      lambda e: e.memset(u_sb[:, :, TC + 1:TC + 2], 0.0))

            P.wait("tensor", "gpsimd", v_xc)
            if c > 0:
                P.wait("scalar", "vector", v_uevac_read_prev)
                P.wait("scalar", "vector", v_gate_prev)
                P.wait("scalar", "gpsimd", v_gy_last_prev)
            pbrot = [0]

            def inproj_tile(et, col0, width, dst_kind):
                nccw = cdiv(width, 512)
                for ccw in range(nccw):
                    w0 = ccw * 512
                    w1 = min(w0 + 512, width)
                    bi = pbrot[0] % 2
                    pbrot[0] += 1
                    bank_wait("tensor", bi)
                    for dt_ in range(8):
                        def mm(e, bi=bi, dt_=dt_, et=et, a=col0 + w0, b=col0 + w1,
                               st=(dt_ == 0), sp=(dt_ == 7)):
                            return e.matmul(PB[bi][:, 0:b - a],
                                            winT[:, dt_, et * 128:(et + 1) * 128],
                                            xTc[:, dt_, a:b], start=st, stop=sp)
                        if dt_ == 7:
                            v_mm = P.inc("tensor", "tensor", mm)
                        else:
                            P.emit("tensor", mm)
                    P.wait("scalar", "tensor", v_mm)
                    kind, kt = dst_kind
                    if kind == "u":
                        v_ev = P.inc("scalar", "scalar",
                                     lambda e, bi=bi, kt=kt, a=col0 + w0, b=col0 + w1:
                                     e.activation(out=u_sb[:, kt, a:b],
                                                  in_=PB[bi][:, 0:b - a],
                                                  func=AF.Copy))
                    else:
                        v_ev = P.inc("scalar", "scalar",
                                     lambda e, bi=bi, kt=kt, a=col0 + w0, b=col0 + w1:
                                     e.activation(out=sz_sb[:, kt, a - 1:b - 1],
                                                  in_=PB[bi][:, 0:b - a],
                                                  func=AF.Silu))
                    bank_free[bi] = ("scalar", v_ev)
                return P.semv["tensor"]

            v_u_evac_et = []
            for et in range(8):
                v_inproj_u = inproj_tile(et, ofs, hi - lo, ("u", et))
                v_u_evac_et.append(P.semv["scalar"])

            P.wait("vector", "sync", max(v_convw, v_convb, v_rs_stores))
            v_uc_silu_prev = [0]
            for kt in range(8):
                P.wait("vector", "scalar", v_u_evac_et[kt])
                P.wait("vector", "scalar", v_uc_silu_prev[0])
                P.inc("vector", "vector", lambda e, kt=kt: e.tensor_scalar_mul(
                    rse[0][0:128, 0:TC], u_sb[:, kt, 1:TC + 1], convw[:, kt, 1:2]))
                P.inc("vector", "vector", lambda e, kt=kt: e.scalar_tensor_tensor(
                    out=rse[0][0:128, 0:TC], in0=u_sb[:, kt, 0:TC],
                    scalar=convw[:, kt, 0:1], in1=rse[0][0:128, 0:TC],
                    op0=ALU.mult, op1=ALU.add))
                v_cv = P.inc("vector", "vector", lambda e, kt=kt: e.scalar_tensor_tensor(
                    out=rse[0][0:128, 0:TC], in0=u_sb[:, kt, 2:TC + 2],
                    scalar=convw[:, kt, 2:3], in1=rse[0][0:128, 0:TC],
                    op0=ALU.mult, op1=ALU.add))
                P.wait("scalar", "vector", v_cv)
                if c > 0 and kt == 0:
                    P.wait("scalar", "tensor", v_ucread_last_prev)
                v_uc_silu_prev[0] = P.inc("scalar", "scalar",
                    lambda e, kt=kt: e.activation(
                        out=uc_sb[:, kt, :], in_=rse[0][0:128, 0:TC], func=AF.Silu,
                        bias=convb[:, kt:kt + 1]))
            v_uc_done = P.semv["scalar"]

            P.wait("tensor", "scalar", v_uc_done)
            if c == 0:
                P.wait("tensor", "gpsimd", v_f_wsplit)
            P.wait("vector", "sync", v_xdo_prev)
            for ccw in range(TC // 512):
                bi = 2 + (ccw % 2)
                bank_wait("tensor", bi)
                for kt in range(8):
                    for si, wsb in enumerate((wxpT_h, wxpT_l)):
                        def mmx(e, bi=bi, kt=kt, ccw=ccw, wsb=wsb,
                                st=(kt == 0 and si == 0), sp=(kt == 7 and si == 1)):
                            return e.matmul(PB[bi][0:96, 0:512], wsb[:, kt, :],
                                            uc_sb[:, kt, ccw * 512:(ccw + 1) * 512],
                                            start=st, stop=sp)
                        if kt == 7 and si == 1:
                            v_mmx = P.inc("tensor", "tensor", mmx)
                        else:
                            P.emit("tensor", mmx)
                P.wait("vector", "tensor", v_mmx)
                v_xd = P.inc("vector", "vector", lambda e, bi=bi, ccw=ccw:
                             e.tensor_copy(out=xdbl[:, ccw * 512:(ccw + 1) * 512],
                                           in_=PB[bi][0:96, 0:512]))
                bank_free[bi] = ("vector", v_xd)
            v_f_xd = vfence(v_xd)
            P.wait("sync", "gpsimd", v_f_xd)
            P.wait("sync", "vector", v_xd)
            if c > 0:
                P.wait("sync", "cc", v_ar_prev)
            v_xdo = sload(d_xdbl_in, xdbl[:])
            v_xdo_prev = v_xdo
            P.wait("gpsimd", "sync", v_xdo)
            v_ar = P.inc("gpsimd", "cc", lambda e: e.collective_compute(
                "AllReduce", ALU.add,
                replica_groups=[[0, 1], [2, 3], [4, 5], [6, 7]],
                ins=[d_xdbl_in], outs=[d_xdbl_out]))
            v_ar_prev = v_ar

            for et in range(8, 16):
                v_inproj_z = inproj_tile(et, 1, TC, ("z", et - 8))
            v_inproj_last_prev = P.semv["tensor"]
            v_z_evac = P.semv["scalar"]

            P.wait("gpsimd", "cc", v_ar)
            v_dtp = gload(dtp[:], d_xdbl_out[0:DTR, :])
            P.wait("tensor", "gpsimd", v_dtp)
            v_dtadd_prev = [0]
            if c > 0:
                P.wait("vector", "scalar", v_dA_last_prev)
                P.wait("scalar", "vector", v_outproj_evac_prev)
                P.wait("scalar", "sync", v_rs_stores)
            for kt in range(8):
                for ccw in range(TC // 512):
                    bi = 4 + ((kt * 2 + ccw) % 2)
                    bank_wait("tensor", bi)
                    for si, wsb in enumerate((wdtT_h, wdtT_l)):
                        def mmd_(e, bi=bi, kt=kt, ccw=ccw, wsb=wsb,
                                 st=(si == 0), sp=(si == 1)):
                            return e.matmul(
                                PB[bi][:, 0:512],
                                wsb[:, kt * 128:(kt + 1) * 128],
                                dtp[:, ccw * 512:(ccw + 1) * 512],
                                start=st, stop=sp)
                        if si == 1:
                            v_mmd = P.inc("tensor", "tensor", mmd_)
                        else:
                            P.emit("tensor", mmd_)
                    P.wait("scalar", "tensor", v_mmd)
                    P.wait("scalar", "vector", v_dtadd_prev[0])
                    P.inc("scalar", "scalar", lambda e, bi=bi, kt=kt: e.activation(
                        out=sp_tmp[:, 0, :], in_=PB[bi][:, 0:512], func=AF.Abs,
                        bias=bdt[:, kt:kt + 1]))
                    P.inc("scalar", "scalar", lambda e: e.activation(
                        out=sp_tmp[:, 0, :], in_=sp_tmp[:, 0, :], func=AF.Exp,
                        scale=-1.0))
                    P.inc("scalar", "scalar", lambda e: e.activation(
                        out=sp_tmp[:, 0, :], in_=sp_tmp[:, 0, :], func=AF.Ln,
                        bias=1.0))
                    v_sp = P.inc("scalar", "scalar", lambda e, bi=bi, kt=kt:
                                 e.activation(out=sp_tmp[:, 1, :],
                                              in_=PB[bi][:, 0:512], func=AF.Relu,
                                              bias=bdt[:, kt:kt + 1]))
                    bank_free[bi] = ("scalar", v_sp)
                    P.wait("vector", "scalar", v_sp)
                    v_dtadd_prev[0] = P.inc("vector", "vector",
                          lambda e, kt=kt, ccw=ccw:
                          e.tensor_add(dt_sb[:, kt, ccw * 512:(ccw + 1) * 512],
                                       sp_tmp[:, 1, :], sp_tmp[:, 0, :]))
            v_dt_done = P.semv["vector"]

            P.wait("scalar", "vector", v_st0 if c == 0 else P.semv["vector"])
            v_strd = P.inc("scalar", "scalar", lambda e: e.copy(
                state_rd[:].rearrange("p a b -> p (a b)"),
                state[:].rearrange("p a b -> p (a b)")))
            P.wait("vector", "scalar", v_strd)

            P.wait("scalar", "vector", v_dt_done)
            P.wait("scalar", "sync", v_abc)
            v_scan_vec_prev = [0]
            v_bc_pair = [0, 0]
            v_bc_read = [0, 0]

            def emit_bc_load(k, first_of_chunk):
                ng, j = (k % 8) // 2, k % 2
                if first_of_chunk:
                    P.wait("gpsimd", "tensor", v_inproj_last_prev)
                if c > 0 or k >= 2:
                    P.wait("gpsimd", "vector",
                           v_bc_read[j] if k >= 2 else v_bc_read_prev)
                for i2 in range(2):
                    s = 2 * j + i2
                    src = bass.AP(d_xdbl_out.tensor, (DTR + 4 * ng + s) * TC,
                                  [[0, 128], [NST * TC, 2], [1, TC]])
                    v_bc_pair[j] = gload(
                        BCflat[:, s * 2 * TC:(s + 1) * 2 * TC], src)

            emit_bc_load(0, True)
            emit_bc_load(1, True)
            for ktg in range(2):
                kts = range(ktg * 4, ktg * 4 + 4)
                for ng in range(4):
                    for n4 in range(4):
                        n = 4 * ng + n4
                        if n4 % 2 == 0:
                            P.wait("vector", "gpsimd", v_bc_pair[n4 // 2])
                        P.wait("scalar", "vector", v_scan_vec_prev[0])
                        v_dA_half = []
                        for q in range(2):
                            ktp = ktg * 4 + q * 2
                            v_dA_half.append(P.inc("scalar", "scalar",
                                  lambda e, q=q, ktp=ktp, n=n: e.activation(
                                      out=dA[q][:].rearrange("p a b -> p (a b)"),
                                      in_=dt_sb[:, ktp:ktp + 2, :]
                                      .rearrange("p a b -> p (a b)"),
                                      func=AF.Exp, scale=abc[:, n:n + 1])))
                        for ki, kt in enumerate(kts):
                            if ki % 2 == 0:
                                P.wait("vector", "scalar", v_dA_half[ki // 2])
                            P.inc("vector", "vector",
                                  lambda e, kt=kt, n4=n4: e.tensor_tensor(
                                      out=bu[:], in0=uc_sb[:, kt, :],
                                      in1=BC[:, n4, 0, :], op=ALU.mult))
                            P.inc("vector", "vector",
                                  lambda e, kt=kt, ki=ki, n=n:
                                  e.tensor_tensor_scan(
                                      out=hbuf[:, ki, :],
                                      data0=dA[(kt % 4) // 2][:, kt % 2, :],
                                      data1=bu[:],
                                      initial=state_rd[:, kt, n:n + 1],
                                      op0=ALU.mult, op1=ALU.add))
                        v_scan_vec_prev[0] = P.semv["vector"]
                        gp_kts = () if n == NST - 1 else KT_G
                        for ki, kt in enumerate(kts):
                            slot = PSLOT[ki][n % 2]
                            if ki in KT_G:
                                P.wait("vector", "gpsimd", g_last[ki][n % 2])
                            P.inc("vector", "vector",
                                  lambda e, ki=ki, n4=n4, slot=slot:
                                  e.tensor_tensor(out=pb[slot][:],
                                                  in0=hbuf[:, ki, :],
                                                  in1=BC[:, n4, 1, :],
                                                  op=ALU.mult))
                        P.inc("vector", "vector", lambda e, ktg=ktg, n=n:
                              e.tensor_copy(
                                  out=state[:, ktg * 4:ktg * 4 + 4, n:n + 1],
                                  in_=hbuf[:, :, TC - 1:TC]))
                        for ki in range(4):
                            if ki in gp_kts:
                                continue
                            slot = PSLOT[ki][n % 2]
                            if ki in KT_G:
                                P.wait("vector", "gpsimd", g_last[ki][(n + 1) % 2])
                            if n == 0:
                                P.inc("vector", "vector",
                                      lambda e, ki=ki, slot=slot: e.tensor_copy(
                                          out=yacc[:, 0, ki, :], in_=pb[slot][:]))
                            else:
                                P.inc("vector", "vector",
                                      lambda e, ki=ki, n=n, slot=slot:
                                      e.tensor_tensor(
                                          out=yacc[:, n % 2, ki, :],
                                          in0=yacc[:, (n + 1) % 2, ki, :],
                                          in1=pb[slot][:], op=ALU.add))
                        v_pm_end = P.semv["vector"]
                        if gp_kts:
                            P.wait("gpsimd", "vector", v_pm_end)
                            P.emit("gpsimd", lambda e: e.engine_nop())
                        for ki in gp_kts:
                            slot = PSLOT[ki][n % 2]
                            if n == 0:
                                g_last[ki][0] = P.inc(
                                    "gpsimd", "gpsimd",
                                    lambda e, ki=ki, slot=slot: e.tensor_copy(
                                        out=yacc[:, 0, ki, :], in_=pb[slot][:]), 1)
                            else:
                                g_last[ki][n % 2] = P.inc(
                                    "gpsimd", "gpsimd",
                                    lambda e, ki=ki, n=n, slot=slot:
                                    e.tensor_tensor(
                                        out=yacc[:, n % 2, ki, :],
                                        in0=yacc[:, (n + 1) % 2, ki, :],
                                        in1=pb[slot][:], op=ALU.add), 1)
                        if n4 % 2 == 1:
                            jj = n4 // 2
                            v_bc_read[jj] = P.semv["vector"]
                            kk = ktg * 8 + ng * 2 + jj
                            if kk + 2 < 16:
                                emit_bc_load(kk + 2, False)
                    v_bc_read_prev = P.semv["vector"]
                v_dA_ktg_end = P.semv["scalar"]
                v_gy_end = P.semv["gpsimd"]
                P.wait("vector", "gpsimd", v_gy_end)
                P.wait("vector", "scalar", v_dA_ktg_end)
                P.wait("vector", "tensor", v_ucread_last_prev)
                P.wait("vector", "scalar", v_z_evac)
                P.wait("vector", "sync", v_dvec)
                for ki, kt in enumerate(kts):
                    P.inc("vector", "vector", lambda e, kt=kt, ki=ki:
                          e.scalar_tensor_tensor(
                              out=rse[0][0:128, 0:TC], in0=uc_sb[:, kt, :],
                              scalar=dvec[:, kt:kt + 1],
                              in1=yacc[:, 1, ki, :], op0=ALU.mult, op1=ALU.add))
                    P.inc("vector", "vector", lambda e, kt=kt: e.tensor_tensor(
                        out=rse[1][0:128, 0:TC], in0=rse[0][0:128, 0:TC],
                        in1=sz_sb[:, kt, :], op=ALU.mult))
                    P.inc("vector", "vector", lambda e, kt=kt: e.tensor_copy(
                        out=ymh[:, kt, :], in_=rse[1][0:128, 0:TC]))
                    P.inc("vector", "vector", lambda e, kt=kt:
                          e.scalar_tensor_tensor(
                              out=yml[:, kt, :], in0=ymh[:, kt, :], scalar=-1.0,
                              in1=rse[1][0:128, 0:TC],
                              op0=ALU.mult, op1=ALU.add))
            v_gate_prev = P.semv["vector"]
            v_dA_last_prev = P.semv["scalar"]
            v_gy_last_prev = P.semv["gpsimd"]

            P.wait("gpsimd", "vector", v_gate_prev)
            v_echo_g = P.inc("gpsimd", "gpsimd", lambda e: e.engine_nop(), 1)
            P.wait("tensor", "gpsimd", v_echo_g)
            P.wait("tensor", "vector", v_gate_prev)
            for ts in range(8):
                ob = (0 + 2 * (ts % 2), 1 + 2 * (ts % 2))
                for bnk in ob:
                    bank_wait("tensor", bnk)
                for kt in range(8):
                    for si, ysb in enumerate((ymh, yml)):
                        for half in range(2):
                            def mmo(e, kt=kt, ts=ts, half=half, bnk=ob[half],
                                    ysb=ysb, st=(kt == 0 and si == 0),
                                    sp=(kt == 7 and si == 1)):
                                return e.matmul(
                                    PB[bnk][:, 0:512],
                                    ysb[:, kt, ts * 128:(ts + 1) * 128],
                                    woutT[:, kt, half * 512:(half + 1) * 512],
                                    start=st, stop=sp)
                            if kt == 7 and si == 1 and half == 1:
                                v_mmo = P.inc("tensor", "tensor", mmo)
                            else:
                                P.emit("tensor", mmo)
                P.wait("vector", "tensor", v_mmo)
                reb = ts % 2
                v_ev = None
                for half in range(2):
                    v_ev = P.inc("vector", "vector",
                                 lambda e, reb=reb, half=half, bnk=ob[half]:
                                 e.tensor_copy(
                                     out=rse[reb][0:128, half * 512:(half + 1) * 512],
                                     in_=PB[bnk][:, 0:512]))
                for half in range(2):
                    bank_free[ob[half]] = ("vector", v_ev)
                v_f_rs = vfence(v_ev)
                P.wait("sync", "gpsimd", v_f_rs)
                P.wait("sync", "vector", v_ev)
                v_rs_stores = sload(d_rs_in[t0 + ts * 128:t0 + (ts + 1) * 128, :],
                                    rse[reb][0:128, 0:D])
            v_ucread_last_prev = P.semv["tensor"]
            v_outproj_evac_prev = P.semv["vector"]

            if debug == 2:
                P.wait("gpsimd", "vector", v_gate_prev)
                if c == NCH - 1:
                    P.inc("gpsimd", "gpsimd", lambda e: e.dma_start(
                        out=ddbg2["ym3"].rearrange("p (a b) -> p a b", a=8),
                        in_=ymh[:]), 16)
                if c == 0:
                    P.inc("gpsimd", "gpsimd", lambda e: e.dma_start(
                        out=ddbg2["ymine"].rearrange("p (a b) -> p a b", a=8),
                        in_=ymh[:]), 16)
                    P.inc("gpsimd", "gpsimd", lambda e: e.dma_start(
                        out=ddbg2["dt"].rearrange("p (a b) -> p a b", a=8),
                        in_=dt_sb[:]), 16)
                P.inc("gpsimd", "gpsimd", lambda e, c=c: e.dma_start(
                    out=ddbg2["state"].rearrange("p (c a b) -> p c a b", c=NCH, a=8)[:, c],
                    in_=state[:]), 16)
            if debug is True and c == 0:
                P.wait("gpsimd", "vector", P.semv["vector"])
                P.wait("gpsimd", "scalar", P.semv["scalar"])
                for key, src_ap in (("u", u_sb[:, :, 1:TC + 1]), ("uc", uc_sb[:]),
                                    ("sz", sz_sb[:]), ("dt", dt_sb[:]),
                                    ("ymine", ymh[:])):
                    P.inc("gpsimd", "gpsimd",
                          lambda e, d=ddbg[key], s=src_ap: e.dma_start(
                              out=d.rearrange("p (a b) -> p a b", a=8), in_=s), 16)
                P.wait("sync", "vector", P.semv["vector"])
                sload(ddbg["xdbl"], xdbl[:])
            if c == 0:
                v_wq_ag = []
                for a, b in ((d_wgT_pre, d_wgT_blk), (d_wuT_pre, d_wuT_blk),
                             (d_wdT_pre, d_wdT_blk)):
                    P.wait("gpsimd", "sync", v_wq_store)
                    v_wq_ag.append(P.inc("gpsimd", "cc",
                                         lambda e, a=a, b=b: e.collective_compute(
                                             "AllGather", ALU.bypass,
                                             replica_groups=[[0, 1, 2, 3, 4, 5, 6, 7]],
                                             ins=[a], outs=[b])))

        P.wait("gpsimd", "sync", v_rs_stores)
        v_rs = P.inc("gpsimd", "cc", lambda e: e.collective_compute(
            "ReduceScatter", ALU.add,
            replica_groups=[[0, 1], [2, 3], [4, 5], [6, 7]],
            ins=[d_rs_in], outs=[d_rs_out]))

        if debug == 2:
            P.wait("gpsimd", "sync", v_rs_stores)
            P.inc("gpsimd", "gpsimd",
                  lambda e: e.dma_start(out=ddbg2["rsin"], in_=d_rs_in), 16)
            P.wait("gpsimd", "cc", v_rs)
            P.inc("gpsimd", "gpsimd",
                  lambda e: e.dma_start(out=ddbg2["rsout"], in_=d_rs_out), 16)
        P.wait("sync", "cc", v_wq_ag[-1])
        P.wait("sync", "tensor", P.semv["tensor"])
        P.wait("sync", "vector", P.semv["vector"])
        P.wait("sync", "scalar", P.semv["scalar"])
        barrier_s = P.inc("sync", "sync", lambda e: e.dma_start(
            out=d_abounce, in_=alog_t[:]), 16)
        for eng in ("tensor", "vector", "scalar", "gpsimd"):
            P.wait(eng, "sync", barrier_s)

        v_wd_l = 0
        for it in range(22):
            wdsrc = bass.AP(d_wdT_blk.tensor, it * 128 * DSL,
                            [[DSL, 128], [FI * DSL, 8], [1, DSL]])
            v_wd_l = sload(wdqT[:, it, :], wdsrc)

        P.wait("sync", "cc", v_rs)
        b_qf = [A.view(o, F32, [128, D]) for o in b_q]
        b_qi = A.view(b_ti, F32, [128, D])
        v_b2_r_prev = [0]
        v_b2_x2store_prev = [0]
        v_b2_tr_prev = [0]
        v_b2_ssq_prev = [0]
        for tt in range(16):
            if tt >= 1:
                P.wait("sync", "vector", v_b2_r_prev[0])
            v_rl = sload(rsld[tt % 2][:], d_rs_out[tt * 128:(tt + 1) * 128, :])
            v_xr = sload(x2l[tt % 2][:], dx_res[tt * 128:(tt + 1) * 128, :])
            P.wait("vector", "sync", max(v_rl, v_xr))
            P.wait("vector", "scalar", v_b2_ssq_prev[0])
            v_r = P.inc("vector", "vector", lambda e, tt=tt: e.scalar_tensor_tensor(
                out=r_sb[:], in0=x2l[tt % 2][:], scalar=2.0,
                in1=rsld[tt % 2][:], op0=ALU.mult, op1=ALU.add))
            v_b2_r_prev[0] = v_r
            P.wait("vector", "sync", v_b2_x2store_prev[0])
            v_x2, v_ssq = ln_tail(r_sb[:], x2_sb[:], v_r)
            v_b2_ssq_prev[0] = v_ssq
            v_f_x2 = vfence(v_x2)
            P.wait("sync", "gpsimd", v_f_x2)
            P.wait("sync", "vector", v_x2)
            v_b2_x2store_prev[0] = sload(d_x2[tt * 128:(tt + 1) * 128, :], x2_sb[:])
            if debug is True and tt == 0:
                P.wait("sync", "vector", P.semv["vector"])
                P.wait("sync", "scalar", P.semv["scalar"])
                sload(ddbg["rs"], rsld[0][:])
                sload(ddbg["x2"], x2_sb[:])
                sload(ddbg["r"], r_sb[:])
                for li, lk in enumerate(("sr", "ssq", "mu", "m2e", "mu2", "ve",
                                         "iv2", "rstd")):
                    P.inc("sync", "sync",
                          lambda e, li=li, lk=lk: e.dma_start(
                              out=ddbg["ln"][:, li:li + 1], in_=ln[lk][:],
                              allow_slow_non_contiguous=True), 16)
                P.wait("vector", "sync", P.semv["sync"])
                P.wait("scalar", "sync", P.semv["sync"])
            P.wait("vector", "tensor", v_b2_tr_prev[0])
            v_x2q = quant_fp4(x2_sb[:], x2q_sb[:], D, b_qf, b_qi, 128,
                              src_ready=("vector", v_x2))
            if debug is True and tt == 0:
                P.wait("gpsimd", "vector", v_x2q)
                P.inc("gpsimd", "gpsimd", lambda e: e.dma_start(
                    out=ddbg["x2q"], in_=x2q_sb[:]), 16)
            P.wait("tensor", "vector", v_x2q)
            for dt_ in range(8):
                bnk = 6 + (dt_ % 2)
                bank_wait("tensor", bnk)
                v_tr = P.inc("tensor", "tensor", lambda e, dt_=dt_, bnk=bnk:
                             e.transpose(PB[bnk][:, 0:64].bitcast(BF16),
                                         x2q_sb[:, dt_ * 128:(dt_ + 1) * 128],
                                         ident[:]))
                P.wait("vector", "tensor", v_tr)
                v_evt = P.inc("vector", "vector",
                              lambda e, dt_=dt_, tt=tt, bnk=bnk: e.tensor_copy(
                                  out=x2qT[:, dt_, tt * 128:(tt + 1) * 128],
                                  in_=PB[bnk][:, 0:64].bitcast(BF16)))
                bank_free[bnk] = ("vector", v_evt)
            v_b2_tr_prev[0] = P.semv["tensor"]
        v_x2qT_done = P.semv["vector"]

        c_qf = [A.view(o, F32, [128, 512]) for o in c_q]
        c_qi = A.view(c_hqld[0], F32, [128, 512])
        P.wait("tensor", "vector", v_x2qT_done)
        ics = [(k * ISL, (k + 1) * ISL) for k in range(8)]
        v_hq_store = 0
        v_ffn_mm_prev = [0]
        v_ffn_h_prev = [0]
        for ici, (i0, i1) in enumerate(ics):
            iw = i1 - i0
            cb = ici % 2
            if ici == 0:
                P.wait("sync", "vector", v_x2qT_done)
            else:
                P.wait("sync", "tensor", v_ffn_mm_prev[0])
            v_wg_l = 0
            for dt_ in range(8):
                r0 = ici * D + dt_ * 128
                v_wg_l = sload(wgc[cb][:, dt_, 0:iw], d_wgT_blk[r0:r0 + 128, :])
                v_wg_l = max(v_wg_l, sload(wuc[cb][:, dt_, 0:iw],
                                           d_wuT_blk[r0:r0 + 128, :]))
            P.wait("tensor", "sync", v_wg_l)
            for tt in range(16):
                bank_wait("tensor", 2)
                bank_wait("tensor", 3)
                for dt_ in range(8):
                    def mmg(e, tt=tt, dt_=dt_, cb=cb, iw=iw, st=(dt_ == 0),
                            sp=(dt_ == 7)):
                        return e.matmul(PB[2][:, 0:iw],
                                        x2qT[:, dt_, tt * 128:(tt + 1) * 128],
                                        wgc[cb][:, dt_, 0:iw], start=st, stop=sp)
                    if dt_ == 7:
                        v_mmg = P.inc("tensor", "tensor", mmg)
                    else:
                        P.emit("tensor", mmg)
                for dt_ in range(8):
                    def mmu(e, tt=tt, dt_=dt_, cb=cb, iw=iw, st=(dt_ == 0),
                            sp=(dt_ == 7)):
                        return e.matmul(PB[3][:, 0:iw],
                                        x2qT[:, dt_, tt * 128:(tt + 1) * 128],
                                        wuc[cb][:, dt_, 0:iw], start=st, stop=sp)
                    if dt_ == 7:
                        v_mmu = P.inc("tensor", "tensor", mmu)
                    else:
                        P.emit("tensor", mmu)
                P.wait("scalar", "tensor", v_mmg)
                P.wait("scalar", "vector", v_ffn_h_prev[0])
                v_ffn_mm_prev[0] = v_mmu
                v_sg = P.inc("scalar", "scalar", lambda e, iw=iw: e.activation(
                    out=sg_sb[:, 0:iw], in_=PB[2][:, 0:iw], func=AF.Silu))
                bank_free[2] = ("scalar", v_sg)
                P.wait("vector", "scalar", v_sg)
                P.wait("vector", "tensor", v_mmu)
                v_h = P.inc("vector", "vector", lambda e, iw=iw: e.tensor_tensor(
                    out=h_sb[:, 0:iw], in0=sg_sb[:, 0:iw], in1=PB[3][:, 0:iw],
                    op=ALU.mult))
                bank_free[3] = ("vector", v_h)
                v_ffn_h_prev[0] = v_h
                if debug is True and ici == 0 and tt == 0:
                    P.wait("gpsimd", "scalar", v_sg)
                    P.inc("gpsimd", "gpsimd", lambda e: e.dma_start(
                        out=ddbg["gate"], in_=sg_sb[:, 0:512]), 16)
                P.wait("vector", "sync", v_hq_store)
                v_hq = quant_fp4(h_sb[:, 0:iw], hq_sb[:, 0:iw], iw, c_qf, c_qi,
                                 128, src_ready=("vector", v_h))
                v_f_hq = vfence(v_hq)
                P.wait("sync", "gpsimd", v_f_hq)
                P.wait("sync", "vector", v_hq)
                v_hq_store = sload(d_hq[tt * 128:(tt + 1) * 128, i0:i1],
                                   hq_sb[:, 0:iw])
                if debug is True and tt == 0:
                    P.wait("gpsimd", "vector", v_hq)
                    P.inc("gpsimd", "gpsimd",
                          lambda e, i0=i0, i1=i1: e.dma_start(
                              out=ddbg["hq"][:, i0:i1], in_=hq_sb[:, 0:i1 - i0]), 16)

        P.wait("sync", "sync", v_hq_store)
        v_mmd_prev = [0]
        v_tr_prev = [0]
        v_r2_prev = [0]
        v_b4_ssq_prev = [0]
        for tt in range(16):
            if tt >= 1:
                P.wait("sync", "tensor", v_tr_prev[0])
                P.wait("sync", "vector", v_r2_prev[0])
            v_hl = sload(hqld[tt % 2][:], d_hq[tt * 128:(tt + 1) * 128, :])
            v_x2r = sload(x2l[tt % 2][:], d_x2[tt * 128:(tt + 1) * 128, :])
            P.wait("tensor", "sync", v_hl)
            P.wait("vector", "tensor", v_mmd_prev[0])
            for it in range(22):
                bnk = 6 + (it % 2)
                bank_wait("tensor", bnk)
                v_tr = P.inc("tensor", "tensor", lambda e, tt=tt, it=it, bnk=bnk:
                             e.transpose(PB[bnk][:, 0:64].bitcast(BF16),
                                         hqld[tt % 2][:, it * 128:(it + 1) * 128],
                                         ident[:]))
                P.wait("vector", "tensor", v_tr)
                v_evt = P.inc("vector", "vector", lambda e, it=it, bnk=bnk:
                              e.tensor_copy(out=hqT[:, it, :], in_=PB[bnk][:, 0:64].bitcast(BF16)))
                bank_free[bnk] = ("vector", v_evt)
            v_hqT_done = P.semv["vector"]
            v_tr_prev[0] = P.semv["tensor"]
            P.wait("gpsimd", "vector", v_hqT_done)
            v_echo_h = P.inc("gpsimd", "gpsimd", lambda e: e.engine_nop(), 1)
            P.wait("tensor", "gpsimd", v_echo_h)
            P.wait("tensor", "vector", v_hqT_done)
            if tt == 0:
                P.wait("tensor", "sync", v_wd_l)
            db = (0 + 2 * (tt % 2), 1 + 2 * (tt % 2))
            for bnk in db:
                bank_wait("tensor", bnk)
            for it in range(22):
                for half in range(2):
                    def mmd(e, it=it, half=half, bnk=db[half], st=(it == 0),
                            sp=(it == 21)):
                        return e.matmul(PB[bnk][:, 0:512], hqT[:, it, :],
                                        wdqT[:, it, half * 512:(half + 1) * 512],
                                        start=st, stop=sp)
                    if it == 21 and half == 1:
                        v_mmd = P.inc("tensor", "tensor", mmd)
                    else:
                        P.emit("tensor", mmd)
            P.wait("vector", "tensor", v_mmd)
            P.wait("vector", "sync", v_x2r)
            P.wait("vector", "scalar", v_b4_ssq_prev[0])
            v_r2 = None
            for half in range(2):
                v_r2 = P.inc("vector", "vector",
                             lambda e, tt=tt, half=half, bnk=db[half]:
                             e.tensor_tensor(
                                 out=r_sb[:, half * 512:(half + 1) * 512],
                                 in0=PB[bnk][:, 0:512],
                                 in1=x2l[tt % 2][:, half * 512:(half + 1) * 512],
                                 op=ALU.add))
            for half in range(2):
                bank_free[db[half]] = ("vector", v_r2)
            v_mmd_prev[0] = v_mmd
            v_r2_prev[0] = v_r2
            if tt >= 2:
                P.wait("vector", "out", 16 * (tt - 1))
            v_o, v_ssq4 = ln_tail(r_sb[:], out_sb[tt % 2][:], v_r2)
            v_b4_ssq_prev[0] = v_ssq4
            v_f_o = vfence(v_o)
            P.wait("sync", "gpsimd", v_f_o)
            P.wait("sync", "vector", v_o)
            P.inc("sync", "out", lambda e, tt=tt: e.dma_start(
                out=dout[tt * 128:(tt + 1) * 128, :], in_=out_sb[tt % 2][:]), 16)

        P.wait("sync", "out", P.semv["out"])

        P.run(block)
    return nc


_NC_CACHE = [None]


def make_in_maps(x, W_in, conv_w, conv_b, W_xp, W_dt, b_dt, A_log, Dv, W_out,
                 wg, wu, wd):
    ident = np.eye(128, dtype=np.float32)
    in_maps = []
    for c in range(8):
        s, j = c // 2, c % 2
        ch = slice(j * HC, (j + 1) * HC)
        in_maps.append({
            "xT": np.ascontiguousarray(x[s].T),
            "xres": np.ascontiguousarray(x[s, j * TH:(j + 1) * TH, :]),
            "winT": np.ascontiguousarray(np.concatenate(
                [W_in[ch].T, W_in[H + j * HC:H + (j + 1) * HC].T], axis=1)),
            "convw": np.ascontiguousarray(conv_w[ch, 0, :]),
            "convb": np.ascontiguousarray(conv_b[ch].reshape(8, 128).T),
            "wxpT": np.ascontiguousarray(W_xp[:, ch].T),
            "wdtT": np.ascontiguousarray(W_dt[ch].T),
            "bdt": np.ascontiguousarray(b_dt[ch].reshape(8, 128).T),
            "alog": np.ascontiguousarray(A_log[None, :]),
            "dvec": np.ascontiguousarray(Dv[ch].reshape(8, 128).T),
            "woutT": np.ascontiguousarray(W_out[:, ch].T),
            "wg_s": np.ascontiguousarray(wg[c * ISL:(c + 1) * ISL]),
            "wu_s": np.ascontiguousarray(wu[c * ISL:(c + 1) * ISL]),
            "wd_s": np.ascontiguousarray(wd[c * DSL:(c + 1) * DSL]),
            "identin": ident,
        })
    return in_maps


def kernel(L0_attn_x, L0_attn_W_in, L0_attn_conv_w, L0_attn_conv_b,
           L0_attn_W_xproj, L0_attn_W_dt, L0_attn_b_dt, L0_attn_A_log,
           L0_attn_D, L0_attn_W_out, L0_ffn_w_gate, L0_ffn_w_up, L0_ffn_w_down):
    in_maps = make_in_maps(
        np.asarray(L0_attn_x, np.float32), np.asarray(L0_attn_W_in, np.float32),
        np.asarray(L0_attn_conv_w, np.float32),
        np.asarray(L0_attn_conv_b, np.float32),
        np.asarray(L0_attn_W_xproj, np.float32),
        np.asarray(L0_attn_W_dt, np.float32),
        np.asarray(L0_attn_b_dt, np.float32),
        np.asarray(L0_attn_A_log, np.float32),
        np.asarray(L0_attn_D, np.float32), np.asarray(L0_attn_W_out, np.float32),
        np.asarray(L0_ffn_w_gate, np.float32),
        np.asarray(L0_ffn_w_up, np.float32),
        np.asarray(L0_ffn_w_down, np.float32))
    if _NC_CACHE[0] is None:
        _NC_CACHE[0] = build()
    nc = _NC_CACHE[0]
    res = run_bass_kernel_spmd(nc, in_maps, core_ids=list(range(8)))
    out = np.empty((B, T, D), np.float32)
    for c in range(8):
        s, j = c // 2, c % 2
        out[s, j * TH:(j + 1) * TH, :] = res.results[c]["out"]
    return out


if __name__ == "__main__":
    print("building ...")
    nc = build()
    n = sum(len(b.instructions) for f in nc.m.functions for b in f.blocks)
    print("built, instructions:", n)


# version 18
# speedup vs baseline: 1.0685x; 1.0493x over previous
import contextlib
import numpy as np
import concourse.bass as bass
import concourse.mybir as mybir
from concourse.bass_utils import run_bass_kernel_spmd

F32 = mybir.dt.float32
BF16 = mybir.dt.bfloat16
F16 = mybir.dt.float16
I32 = mybir.dt.int32
U8 = mybir.dt.uint8
AF = mybir.ActivationFunctionType
ALU = mybir.AluOpType
AX = mybir.AxisListType

B, T, D = 4, 4096, 1024
H, HC, NST, DTR, FI = 2048, 1024, 16, 64, 2816
TC = 1024
NCH = T // TC
TH = 2048
LN_EPS = 1e-5
ISL = FI // 8
DSL = D // 8

ENGS = ("sync", "scalar", "vector", "tensor", "gpsimd")


class Prog:
    def __init__(self, nc, es):
        self.nc = nc
        self.q = {e: [] for e in ENGS}
        self.semh = {}
        self.semv = {}
        for s in ENGS + ("cc", "out"):
            self.semh[s] = es.enter_context(nc.semaphore(f"sem_{s}"))
            self.semv[s] = 0

    def emit(self, eng, fn):
        self.q[eng].append(fn)

    def wait(self, eng, sem, val):
        if val and val > 0:
            h = self.semh[sem]
            self.q[eng].append(lambda e, h=h, v=val: e.wait_ge(h, v))

    def inc(self, eng, sem, fn, k=1):
        self.semv[sem] += k
        tgt = self.semv[sem]
        h = self.semh[sem]
        self.q[eng].append(lambda e, f=fn, h=h, kk=k: f(e).then_inc(h, kk))
        return tgt

    def run(self, block):
        @block.sync
        def _(e):
            for f in self.q["sync"]:
                f(e)

        @block.scalar
        def _(e):
            for f in self.q["scalar"]:
                f(e)

        @block.vector
        def _(e):
            for f in self.q["vector"]:
                f(e)

        @block.tensor
        def _(e):
            for f in self.q["tensor"]:
                f(e)

        @block.gpsimd
        def _(e):
            for f in self.q["gpsimd"]:
                f(e)


def cdiv(a, b):
    return (a + b - 1) // b


ARENA_BYTES = 194688


class Arena:

    def __init__(self, ap):
        self.ap = ap
        self.off = 0

    def alloc(self, nbytes, align=512):
        o = (self.off + align - 1) // align * align
        self.off = o + nbytes
        assert self.off <= ARENA_BYTES, f"arena overflow {self.off}"
        return o

    def view(self, off, dtype, shape):
        esz = {F32: 4, BF16: 2, F16: 2, I32: 4}[dtype]
        n = int(np.prod(shape[1:])) if len(shape) > 1 else 1
        ap = self.ap[:, off:off + n * esz].bitcast(dtype)
        if len(shape) > 2:
            dims = " ".join(f"d{i}" for i in range(1, len(shape)))
            ap = ap.rearrange(f"p ({dims}) -> p {dims}",
                              **{f"d{i}": shape[i] for i in range(1, len(shape) - 1)})
        return ap


def build(debug=False):
    nc = bass.Bass("TRN2", target_bir_lowering=False,
                   detect_race_conditions=False)

    dx_T = nc.dram_tensor("xT", [D, T], F32, kind="ExternalInput").ap()
    dx_res = nc.dram_tensor("xres", [TH, D], F32, kind="ExternalInput").ap()
    dwinT = nc.dram_tensor("winT", [D, 2 * HC], F32, kind="ExternalInput").ap()
    dconvw = nc.dram_tensor("convw", [HC, 3], F32, kind="ExternalInput").ap()
    dconvb = nc.dram_tensor("convb", [128, 8], F32, kind="ExternalInput").ap()
    dwxpT = nc.dram_tensor("wxpT", [HC, 96], F32, kind="ExternalInput").ap()
    dwdtT = nc.dram_tensor("wdtT", [DTR, HC], F32, kind="ExternalInput").ap()
    dbdt = nc.dram_tensor("bdt", [128, 8], F32, kind="ExternalInput").ap()
    dalog = nc.dram_tensor("alog", [1, NST], F32, kind="ExternalInput").ap()
    ddvec = nc.dram_tensor("dvec", [128, 8], F32, kind="ExternalInput").ap()
    dwoutT = nc.dram_tensor("woutT", [HC, D], F32, kind="ExternalInput").ap()
    dwg_s = nc.dram_tensor("wg_s", [ISL, D], F32, kind="ExternalInput").ap()
    dwu_s = nc.dram_tensor("wu_s", [ISL, D], F32, kind="ExternalInput").ap()
    dwd_s = nc.dram_tensor("wd_s", [DSL, FI], F32, kind="ExternalInput").ap()
    dident = nc.dram_tensor("identin", [128, 128], F32, kind="ExternalInput").ap()

    dout = nc.dram_tensor("out", [TH, D], F32, kind="ExternalOutput").ap()

    if debug == 2:
        ddbg2 = {"ymine": nc.dram_tensor("dbg2_ymine", [128, 8 * TC], F32,
                                         kind="ExternalOutput").ap(),
                 "state": nc.dram_tensor("dbg2_state", [128, 8 * NST * NCH], F32,
                                         kind="ExternalOutput").ap(),
                 "dt": nc.dram_tensor("dbg2_dt", [128, 8 * TC], F32,
                                      kind="ExternalOutput").ap(),
                 "rsin": nc.dram_tensor("dbg2_rsin", [T, D], F32,
                                        kind="ExternalOutput").ap(),
                 "rsout": nc.dram_tensor("dbg2_rsout", [TH, D], F32,
                                         kind="ExternalOutput").ap(),
                 "ym3": nc.dram_tensor("dbg2_ym3", [128, 8 * TC], F32,
                                       kind="ExternalOutput").ap()}
    if debug is True:
        ddbg = {k: nc.dram_tensor(f"dbg_{k}", [128, 8 * TC], F32,
                                  kind="ExternalOutput").ap()
                for k in ("u", "uc", "sz", "dt", "ymine")}
        ddbg["xdbl"] = nc.dram_tensor("dbg_xdbl", [96, TC], F32,
                                      kind="ExternalOutput").ap()
        ddbg["rs"] = nc.dram_tensor("dbg_rs", [128, D], F32,
                                    kind="ExternalOutput").ap()
        ddbg["x2"] = nc.dram_tensor("dbg_x2", [128, D], F32,
                                    kind="ExternalOutput").ap()
        ddbg["x2q"] = nc.dram_tensor("dbg_x2q", [128, D], F32,
                                     kind="ExternalOutput").ap()
        ddbg["hq"] = nc.dram_tensor("dbg_hq", [128, FI], F32,
                                    kind="ExternalOutput").ap()
        ddbg["gate"] = nc.dram_tensor("dbg_gate", [128, 512], F32,
                                      kind="ExternalOutput").ap()
        ddbg["wgT"] = nc.dram_tensor("dbg_wgT", [128, FI], F32,
                                     kind="ExternalOutput").ap()
        ddbg["wgq"] = nc.dram_tensor("dbg_wgq", [FI, D], F32,
                                     kind="ExternalOutput").ap()
        ddbg["ln"] = nc.dram_tensor("dbg_ln", [128, 8], F32,
                                    kind="ExternalOutput").ap()
        ddbg["r"] = nc.dram_tensor("dbg_r", [128, D], F32,
                                   kind="ExternalOutput").ap()
    d_abounce = nc.dram_tensor("a_bounce", [1, NST], F32, kind="Internal").ap()
    d_xdbl_in = nc.dram_tensor("cc_xdbl_in", [96, TC], F32, kind="Internal").ap()
    d_xdbl_out = nc.dram_tensor("cc_xdbl_out", [96, TC], F32, kind="Internal").ap()
    d_rs_in = nc.dram_tensor("cc_rs_in", [T, D], F32, kind="Internal").ap()
    d_rs_out = nc.dram_tensor("cc_rs_out", [TH, D], F32, kind="Internal").ap()
    d_wgT_pre = nc.dram_tensor("cc_wgT_pre", [D, ISL], BF16, kind="Internal").ap()
    d_wgT_blk = nc.dram_tensor("cc_wgT_blk", [8 * D, ISL], BF16, kind="Internal").ap()
    d_wuT_pre = nc.dram_tensor("cc_wuT_pre", [D, ISL], BF16, kind="Internal").ap()
    d_wuT_blk = nc.dram_tensor("cc_wuT_blk", [8 * D, ISL], BF16, kind="Internal").ap()
    d_wdT_pre = nc.dram_tensor("cc_wdT_pre", [FI, DSL], BF16, kind="Internal").ap()
    d_wdT_blk = nc.dram_tensor("cc_wdT_blk", [8 * FI, DSL], BF16, kind="Internal").ap()
    d_x2 = nc.dram_tensor("x2_scr", [TH, D], F32, kind="Internal").ap()
    d_hq = nc.dram_tensor("hq_scr", [TH, FI], BF16, kind="Internal").ap()

    es = contextlib.ExitStack()
    with es:
        sb = lambda name, shape, dt: es.enter_context(nc.sbuf_tensor(name, shape, dt))

        abc = sb("sb_abc", [128, NST], F32)
        ident = sb("sb_ident", [128, 128], BF16)
        convw = sb("sb_convw", [128, 8, 3], F32)
        convb = sb("sb_convb", [128, 8], F32)
        bdt = sb("sb_bdt", [128, 8], F32)
        dvec = sb("sb_dvec", [128, 8], F32)
        state = sb("sb_state", [128, 8, NST], F32)
        state_rd = sb("sb_state_rd", [128, 8, NST], F32)
        wxpT_h = sb("sb_wxpT_h", [128, 8, 96], BF16)
        wxpT_l = sb("sb_wxpT_l", [128, 8, 96], BF16)
        wdtT_h = sb("sb_wdtT_h", [DTR, HC], BF16)
        wdtT_l = sb("sb_wdtT_l", [DTR, HC], BF16)
        dtp = sb("sb_dtp", [DTR, TC], BF16)
        xdbl = sb("sb_xdbl", [96, TC], F32)
        alog_t = sb("sb_alog_t", [1, NST], F32)
        aneg_t = sb("sb_aneg_t", [1, NST], F32)
        ln = {k: sb(f"sb_ln_{k}", [128, 1], F32)
              for k in ("sr", "ssq", "mu", "m2e", "mu2", "ve", "iv2", "rstd")}
        qsml = {k: sb(f"sb_q_{k}", [128, FI // 16], F32) for k in ("m", "sh", "rm", "rm2")}

        arena_t = sb("sb_arena", [128, ARENA_BYTES], U8)
        A = Arena(arena_t[:])

        o_winT = A.alloc(8 * 2048 * 2)
        o_xTc = A.alloc(8 * (TC + 2) * 2, align=64)
        o_u = A.alloc(8 * (TC + 2) * 2, align=64)
        o_uc = A.alloc(8 * TC * 2, align=64)
        o_sz = A.alloc(8 * TC * 2, align=64)
        o_dt = A.alloc(8 * TC * 2, align=64)
        o_dA = [A.alloc(2 * TC * 4, align=64) for _ in range(2)]
        o_bu = A.alloc(TC * 2, align=64)
        o_pb = [A.alloc(TC * 2, align=64) for _ in range(6)]
        o_hbuf = A.alloc(4 * TC * 2, align=64)
        o_ymh = A.alloc(8 * TC * 2, align=64)
        o_woutT = A.alloc(8 * D * 2, align=64)
        o_rse = [A.alloc(D * 4, align=64) for _ in range(2)]
        mix_end = A.off
        o_BC = o_xTc
        o_yacc = o_u
        o_yml = o_dt

        winT = A.view(o_winT, BF16, [128, 8, 2048])
        xTc = A.view(o_xTc, BF16, [128, 8, TC + 2])
        u_sb = A.view(o_u, F16, [128, 8, TC + 2])
        uc_sb = A.view(o_uc, BF16, [128, 8, TC])
        sz_sb = A.view(o_sz, F16, [128, 8, TC])
        dt_sb = A.view(o_dt, F16, [128, 8, TC])
        dA = [A.view(o, F32, [128, 2, TC]) for o in o_dA]
        bu = A.view(o_bu, F16, [128, TC])
        pb = [A.view(o, F16, [128, TC]) for o in o_pb]
        hbuf = A.view(o_hbuf, F16, [128, 4, TC])
        ymh = A.view(o_ymh, BF16, [128, 8, TC])
        yml = A.view(o_yml, BF16, [128, 8, TC])
        yacc = A.view(o_yacc, F16, [128, 2, 4, TC])
        woutT = A.view(o_woutT, BF16, [128, 8, D])
        rse = [A.view(o, F32, [128, D]) for o in o_rse]
        sp_tmp = A.view(o_rse[1], F32, [128, 2, 512])
        BC = A.view(o_BC, F16, [128, 4, 2, TC])
        BCflat = A.view(o_BC, F16, [128, 4 * 2 * TC])
        wxf = A.view(o_sz, F32, [128, 8, 96])
        wdf = A.view(o_sz + 4096, F32, [128, HC])

        q_big_offs = [o_dt, o_ymh, o_woutT, o_xTc]

        oB = Arena(arena_t[:])
        oB.off = 0
        b_x2qT = oB.alloc(8 * TH * 2)
        b_rs = [oB.alloc(D * 4) for _ in range(2)]
        b_r = oB.alloc(D * 4)
        b_x2 = oB.alloc(D * 4)
        b_x2q = oB.alloc(D * 2)
        b_junk = oB.alloc(D * 2)
        b_q = [oB.alloc(D * 4) for _ in range(4)]
        b_wld = [oB.alloc(FI * 2) for _ in range(2)]
        b_ev = oB.alloc(1024 * 2)
        b_ti = oB.alloc(D * 4)
        assert oB.off <= ARENA_BYTES

        x2qT = A.view(b_x2qT, BF16, [128, 8, TH])
        rsld = [A.view(o, F32, [128, D]) for o in b_rs]
        r_sb = A.view(b_r, F32, [128, D])
        x2_sb = A.view(b_x2, F32, [128, D])
        x2q_sb = A.view(b_x2q, BF16, [128, D])
        junk_sb = A.view(b_junk, BF16, [128, D])
        wld = [A.view(o, BF16, [128, FI]) for o in b_wld]
        ev_sb = A.view(b_ev, BF16, [128, 1024])

        oC = Arena(arena_t[:])
        oC.off = b_rs[0]
        c_wg = [oC.alloc(8 * 512 * 2) for _ in range(2)]
        c_wu = [oC.alloc(8 * 512 * 2) for _ in range(2)]
        c_h = oC.alloc(512 * 4)
        c_sg = oC.alloc(512 * 2)
        c_hq = oC.alloc(512 * 2)
        c_q = [oC.alloc(512 * 4) for _ in range(4)]
        c_wdqT = oC.alloc(22 * 1024 * 2)
        c_hqld = [oC.alloc(FI * 2) for _ in range(2)]
        c_hqT = oC.alloc(22 * 128 * 2)
        c_x2l = [oC.alloc(D * 4) for _ in range(2)]
        c_out = [oC.alloc(D * 4) for _ in range(2)]
        assert oC.off <= ARENA_BYTES, f"phaseC overflow {oC.off}"

        wgc = [A.view(o, BF16, [128, 8, 512]) for o in c_wg]
        wuc = [A.view(o, BF16, [128, 8, 512]) for o in c_wu]
        h_sb = A.view(c_h, F32, [128, 512])
        sg_sb = A.view(c_sg, F16, [128, 512])
        hq_sb = A.view(c_hq, BF16, [128, 512])
        wdqT = A.view(c_wdqT, BF16, [128, 22, 1024])
        hqld = [A.view(o, BF16, [128, FI]) for o in c_hqld]
        hqT = A.view(c_hqT, BF16, [128, 22, 128])
        x2l = [A.view(o, F32, [128, D]) for o in c_x2l]
        out_sb = [A.view(o, F32, [128, D]) for o in c_out]

        PB = [es.enter_context(nc.psum_tensor(f"pb{i}", [128, 512], F32))
              for i in range(8)]

        block = es.enter_context(nc.Block())
        P = Prog(nc, es)
        bank_free = {i: None for i in range(8)}

        def bank_wait(eng, i):
            if bank_free[i] is not None:
                s, v = bank_free[i]
                P.wait(eng, s, v)

        def vfence(v_val):
            P.wait("gpsimd", "vector", v_val)
            return P.inc("gpsimd", "gpsimd", lambda e: e.engine_nop(), 1)

        def sload(dst, src):
            return P.inc("sync", "sync",
                         lambda e, d=dst, s=src: e.dma_start(out=d, in_=s), 16)

        def gload(dst, src):
            return P.inc("gpsimd", "gpsimd",
                         lambda e, d=dst, s=src: e.dma_start(out=d, in_=s), 16)

        v_convw = sload(convw[:],
                        dconvw.rearrange("(a p) b -> p a b", p=128))
        v_convb = sload(convb[:], dconvb)
        v_bdt = sload(bdt[:], dbdt)
        v_dvec = sload(dvec[:], ddvec)
        v_alog = sload(alog_t[:], dalog)
        v_ident = gload(ident[:], dident)
        v_wxf = sload(wxf[:], dwxpT.rearrange("(a p) b -> p a b", p=128))
        v_wdf = sload(wdf[0:DTR, :], dwdtT)
        P.wait("vector", "sync", max(v_wxf, v_wdf))
        P.inc("vector", "vector", lambda e: e.tensor_copy(out=wxpT_h[:], in_=wxf[:]))
        P.inc("vector", "vector", lambda e: e.scalar_tensor_tensor(
            out=wxpT_l[:], in0=wxpT_h[:], scalar=-1.0, in1=wxf[:],
            op0=ALU.mult, op1=ALU.add))
        P.inc("vector", "vector",
              lambda e: e.tensor_copy(out=wdtT_h[:], in_=wdf[0:DTR, :]))
        v_wsplit = P.inc("vector", "vector", lambda e: e.scalar_tensor_tensor(
            out=wdtT_l[:], in0=wdtT_h[:], scalar=-1.0, in1=wdf[0:DTR, :],
            op0=ALU.mult, op1=ALU.add))
        v_f_wsplit = vfence(v_wsplit)

        P.wait("scalar", "sync", v_alog)
        v_ax = P.inc("scalar", "scalar",
                     lambda e: e.activation(out=alog_t[:], in_=alog_t[:], func=AF.Exp))
        P.wait("gpsimd", "scalar", v_ax)
        P.inc("gpsimd", "gpsimd",
              lambda e: e.tensor_copy(out=aneg_t[:], in_=alog_t[:]), 1)
        v_an = P.inc("gpsimd", "gpsimd",
                     lambda e: e.tensor_scalar_mul(aneg_t[:], aneg_t[:], -1.0), 1)
        P.wait("sync", "gpsimd", v_an)
        v_ab = sload(d_abounce, aneg_t[:])
        P.wait("sync", "sync", v_ab)
        v_abc = sload(abc[:], d_abounce.broadcast_to([128, NST]))

        v_st0 = P.inc("vector", "vector", lambda e: e.memset(state[:], 0.0))

        q_prev_use = [0]

        def quant_fp4(src_ap, dst_ap, F, scratch_f32, scratch_i32, pc=128,
                      src_ready=None):
            nb = F // 16
            m = qsml["m"][0:pc, 0:nb]
            sh = qsml["sh"][0:pc, 0:nb]
            rm = qsml["rm"][0:pc, 0:nb]
            rm2 = qsml["rm2"][0:pc, 0:nb]
            asrc, aw, t1, t2 = [sc[0:pc, 0:F] for sc in scratch_f32]
            src3 = src_ap.rearrange("p (b k) -> p b k", k=16)
            if src_ready is not None:
                P.wait("vector", src_ready[0], src_ready[1])
                P.wait("scalar", src_ready[0], src_ready[1])
            v_m = P.inc("vector", "vector", lambda e: e.tensor_reduce(
                out=m, in_=src3, axis=AX.X, op=ALU.max, apply_absolute_value=True))
            v_as = P.inc("scalar", "scalar",
                         lambda e: e.activation(out=asrc, in_=src_ap, func=AF.Abs))
            v_sg = P.inc("scalar", "scalar",
                         lambda e: e.activation(out=aw, in_=src_ap, func=AF.Sign))
            P.wait("gpsimd", "vector", v_m)
            P.wait("gpsimd", "scalar", v_sg)
            P.wait("gpsimd", "vector", q_prev_use[0])
            v_sh = P.inc("gpsimd", "gpsimd", lambda e: e.tensor_scalar(
                out=sh, in0=m, scalar1=1.0 / 12.0, scalar2=5e-13,
                op0=ALU.mult, op1=ALU.max), 1)
            P.wait("vector", "gpsimd", v_sh)
            v_rm = P.inc("vector", "vector", lambda e: e.reciprocal(out=rm, in_=sh))
            P.wait("scalar", "vector", v_rm)
            v_rm2 = P.inc("scalar", "scalar",
                          lambda e: e.copy(rm2, rm))
            P.wait("vector", "scalar", v_rm2)
            P.inc("vector", "vector", lambda e: e.tensor_tensor(
                out=t1.rearrange("p (b k) -> p b k", k=16),
                in0=asrc.rearrange("p (b k) -> p b k", k=16),
                in1=rm2.unsqueeze(-1).broadcast_to([pc, nb, 16]), op=ALU.mult))
            P.inc("vector", "vector", lambda e: e.tensor_scalar(
                out=asrc, in0=t1, scalar1=0.5, scalar2=1.0,
                op0=ALU.is_gt, op1=ALU.mult))
            for mj in (1.5, 2.5, 3.5):
                P.inc("vector", "vector", lambda e, mj=mj: e.scalar_tensor_tensor(
                    out=asrc, in0=t1, scalar=mj, in1=asrc,
                    op0=ALU.is_gt, op1=ALU.add))
            P.inc("vector", "vector", lambda e: e.tensor_scalar(
                out=t2, in0=t1, scalar1=10.0, scalar2=2.0,
                op0=ALU.is_gt, op1=ALU.mult))
            for mj in (5.0, 7.0):
                P.inc("vector", "vector", lambda e, mj=mj: e.scalar_tensor_tensor(
                    out=t2, in0=t1, scalar=mj, in1=t2,
                    op0=ALU.is_gt, op1=ALU.add))
            P.inc("vector", "vector", lambda e: e.scalar_tensor_tensor(
                out=asrc, in0=t2, scalar=2.0, in1=asrc,
                op0=ALU.mult, op1=ALU.add))
            P.wait("vector", "scalar", v_sg)
            P.inc("vector", "vector", lambda e: e.tensor_tensor(
                out=t2.rearrange("p (b k) -> p b k", k=16),
                in0=aw.rearrange("p (b k) -> p b k", k=16),
                in1=sh.unsqueeze(-1).broadcast_to([pc, nb, 16]), op=ALU.mult))
            v_q = P.inc("vector", "vector",
                        lambda e: e.tensor_mul(dst_ap, asrc, t2))
            q_prev_use[0] = v_q
            return v_q

        ln_prev_use = [0]

        def ln_tail(r_ap, out_ap, v_r):
            v_sr = P.inc("vector", "vector", lambda e: e.tensor_reduce(
                out=ln["sr"][:], in_=r_ap, axis=AX.X, op=ALU.add))
            P.wait("scalar", "vector", v_r)
            v_ssq = P.inc("scalar", "scalar", lambda e: e.activation(
                out=junk_sb[:], in_=r_ap, func=AF.Square, accum_out=ln["ssq"][:]))
            P.wait("gpsimd", "vector", v_sr)
            P.wait("gpsimd", "scalar", v_ssq)
            P.wait("gpsimd", "vector", ln_prev_use[0])
            P.inc("gpsimd", "gpsimd", lambda e: e.tensor_scalar(
                out=ln["mu"][:], in0=ln["sr"][:], scalar1=1.0 / D, scalar2=None,
                op0=ALU.mult), 1)
            P.inc("gpsimd", "gpsimd", lambda e: e.tensor_scalar(
                out=ln["m2e"][:], in0=ln["ssq"][:], scalar1=1.0 / D,
                scalar2=LN_EPS, op0=ALU.mult, op1=ALU.add), 1)
            P.inc("gpsimd", "gpsimd", lambda e: e.tensor_mul(
                ln["mu2"][:], ln["mu"][:], ln["mu"][:]), 1)
            v_ve = P.inc("gpsimd", "gpsimd", lambda e: e.tensor_sub(
                ln["ve"][:], ln["m2e"][:], ln["mu2"][:]), 1)
            P.wait("vector", "gpsimd", v_ve)
            v_iv = P.inc("vector", "vector", lambda e: e.reciprocal(
                out=ln["iv2"][:], in_=ln["ve"][:]))
            P.wait("scalar", "vector", v_iv)
            v_rstd = P.inc("scalar", "scalar", lambda e: e.activation(
                out=ln["rstd"][:], in_=ln["iv2"][:], func=AF.Sqrt))
            P.wait("vector", "scalar", v_rstd)
            v_out = P.inc("vector", "vector", lambda e: e.tensor_scalar(
                out=out_ap, in0=r_ap, scalar1=ln["mu"][:], scalar2=ln["rstd"][:],
                op0=ALU.subtract, op1=ALU.mult))
            ln_prev_use[0] = v_out
            return v_out, v_ssq

        wq_scr_f = [A.view(o, F32, [128, FI]) for o in q_big_offs]
        wq_ti = None
        wq_in_v = A.view(o_winT, F32, [128, 3, D])
        wq_in_d = A.view(o_winT, F32, [128, FI])
        wq_out_v = A.view(o_uc, BF16, [128, 3, D])
        wq_out_d = A.view(o_uc, BF16, [128, FI])

        wq_plan = [(dwg_s, d_wgT_pre, ISL, D, wq_in_v, wq_out_v),
                   (dwu_s, d_wuT_pre, ISL, D, wq_in_v, wq_out_v),
                   (dwd_s, d_wdT_pre, DSL, FI, wq_in_d, wq_out_d)]
        v_wq_store = 0
        wq_ev = [A.view(o_sz + 8192 + 256 * i, BF16, [128, 128]) for i in range(2)]
        P.wait("tensor", "gpsimd", v_ident)
        for (dsrc, ddstT, R, F, vin, vout) in wq_plan:
            nt = cdiv(R, 128)
            for rt in range(nt):
                p0, p1 = rt * 128, min((rt + 1) * 128, R)
                pc = p1 - p0
                vv = sload(vin[0:pc, rt, :] if nt > 1 else vin[0:pc, :],
                           dsrc[p0:p1, :])
                P.wait("vector", "sync", vv)
                v_q = quant_fp4(vin[0:pc, rt, :] if nt > 1 else vin[0:pc, :],
                                vout[0:pc, rt, :] if nt > 1 else vout[0:pc, :],
                                F, wq_scr_f, wq_ti, pc, src_ready=("sync", vv))
                v_f_wq = vfence(v_q)
                P.wait("tensor", "gpsimd", v_f_wq)
                for ft in range(F // 128):
                    bnk = 6 + (ft % 2)
                    bank_wait("tensor", bnk)
                    tin = (vout[0:pc, rt, ft * 128:(ft + 1) * 128] if nt > 1
                           else vout[0:pc, ft * 128:(ft + 1) * 128])
                    v_tr = P.inc("tensor", "tensor",
                                 lambda e, tin=tin, bnk=bnk, pc=pc: e.transpose(
                                     PB[bnk][:, 0:64].bitcast(BF16)[:, 0:pc],
                                     tin, ident[0:pc, 0:pc]))
                    P.wait("vector", "tensor", v_tr)
                    P.wait("vector", "sync", v_wq_store)
                    v_evt = P.inc("vector", "vector",
                                  lambda e, ft=ft, bnk=bnk, pc=pc: e.tensor_copy(
                                      out=wq_ev[ft % 2][:, 0:pc],
                                      in_=PB[bnk][:, 0:64].bitcast(BF16)[:, 0:pc]))
                    bank_free[bnk] = ("vector", v_evt)
                    v_f_ev = vfence(v_evt)
                    P.wait("sync", "gpsimd", v_f_ev)
                    P.wait("sync", "vector", v_evt)
                    v_wq_store = sload(ddstT[ft * 128:(ft + 1) * 128, p0:p1],
                                       wq_ev[ft % 2][:, 0:pc])
        v_wq_quant_done = P.semv["vector"]
        v_wq_sgn_done = P.semv["scalar"]

        v_bc_read_prev = 0
        v_xdo_prev = 0
        v_inproj_last_prev = 0
        v_uevac_read_prev = 0
        v_dA_last_prev = 0
        v_gate_prev = 0
        v_ucread_last_prev = 0
        v_ar_prev = 0
        v_rs_stores = 0
        v_wq_ag = None
        v_outproj_evac_prev = 0
        v_gy_last_prev = 0
        KT_G = ()
        PSLOT = {0: (0, 0), 1: (1, 1), 2: (2, 3), 3: (4, 5)}
        g_last = {2: [0, 0], 3: [0, 0]}

        for c in range(NCH):
            t0 = c * TC
            lo = max(t0 - 1, 0)
            hi = min(t0 + TC + 1, T)
            ofs = lo - (t0 - 1)

            if c == 0:
                P.wait("gpsimd", "vector", v_wq_quant_done)
                P.wait("gpsimd", "scalar", v_wq_sgn_done)
                P.wait("gpsimd", "sync", v_wq_store)
                P.wait("scalar", "vector", v_wq_quant_done)
                P.wait("scalar", "sync", v_wq_store)
            else:
                P.wait("gpsimd", "tensor", v_inproj_last_prev)
                P.wait("gpsimd", "vector", v_bc_read_prev)
            v_xc = 0
            for dt_ in range(8):
                v_xc = gload(xTc[:, dt_, ofs:ofs + (hi - lo)],
                             dx_T[dt_ * 128:(dt_ + 1) * 128, lo:hi])
            if c == 0:
                v_wl = gload(winT[:],
                             dwinT.rearrange("(a p) b -> p a b", p=128))
                v_wo = gload(woutT[:],
                             dwoutT.rearrange("(a p) b -> p a b", p=128))
                v_xc = max(v_xc, v_wl, v_wo)

            if c == 0:
                P.inc("vector", "vector", lambda e: e.memset(u_sb[:, :, 0:1], 0.0))
            if c == NCH - 1:
                P.inc("vector", "vector",
                      lambda e: e.memset(u_sb[:, :, TC + 1:TC + 2], 0.0))

            P.wait("tensor", "gpsimd", v_xc)
            if c > 0:
                P.wait("scalar", "vector", v_uevac_read_prev)
                P.wait("scalar", "vector", v_gate_prev)
                P.wait("scalar", "gpsimd", v_gy_last_prev)
            pbrot = [0]

            def inproj_tile(et, col0, width, dst_kind):
                nccw = cdiv(width, 512)
                for ccw in range(nccw):
                    w0 = ccw * 512
                    w1 = min(w0 + 512, width)
                    bi = pbrot[0] % 2
                    pbrot[0] += 1
                    bank_wait("tensor", bi)
                    for dt_ in range(8):
                        def mm(e, bi=bi, dt_=dt_, et=et, a=col0 + w0, b=col0 + w1,
                               st=(dt_ == 0), sp=(dt_ == 7)):
                            return e.matmul(PB[bi][:, 0:b - a],
                                            winT[:, dt_, et * 128:(et + 1) * 128],
                                            xTc[:, dt_, a:b], start=st, stop=sp)
                        if dt_ == 7:
                            v_mm = P.inc("tensor", "tensor", mm)
                        else:
                            P.emit("tensor", mm)
                    P.wait("scalar", "tensor", v_mm)
                    kind, kt = dst_kind
                    if kind == "u":
                        v_ev = P.inc("scalar", "scalar",
                                     lambda e, bi=bi, kt=kt, a=col0 + w0, b=col0 + w1:
                                     e.activation(out=u_sb[:, kt, a:b],
                                                  in_=PB[bi][:, 0:b - a],
                                                  func=AF.Copy))
                    else:
                        v_ev = P.inc("scalar", "scalar",
                                     lambda e, bi=bi, kt=kt, a=col0 + w0, b=col0 + w1:
                                     e.activation(out=sz_sb[:, kt, a - 1:b - 1],
                                                  in_=PB[bi][:, 0:b - a],
                                                  func=AF.Silu))
                    bank_free[bi] = ("scalar", v_ev)
                return P.semv["tensor"]

            v_u_evac_et = []
            for et in range(8):
                v_inproj_u = inproj_tile(et, ofs, hi - lo, ("u", et))
                v_u_evac_et.append(P.semv["scalar"])

            P.wait("vector", "sync", max(v_convw, v_convb, v_rs_stores))
            v_uc_silu_prev = [0]
            for kt in range(8):
                P.wait("vector", "scalar", v_u_evac_et[kt])
                P.wait("vector", "scalar", v_uc_silu_prev[0])
                P.inc("vector", "vector", lambda e, kt=kt: e.tensor_scalar_mul(
                    rse[0][0:128, 0:TC], u_sb[:, kt, 1:TC + 1], convw[:, kt, 1:2]))
                P.inc("vector", "vector", lambda e, kt=kt: e.scalar_tensor_tensor(
                    out=rse[0][0:128, 0:TC], in0=u_sb[:, kt, 0:TC],
                    scalar=convw[:, kt, 0:1], in1=rse[0][0:128, 0:TC],
                    op0=ALU.mult, op1=ALU.add))
                v_cv = P.inc("vector", "vector", lambda e, kt=kt: e.scalar_tensor_tensor(
                    out=rse[0][0:128, 0:TC], in0=u_sb[:, kt, 2:TC + 2],
                    scalar=convw[:, kt, 2:3], in1=rse[0][0:128, 0:TC],
                    op0=ALU.mult, op1=ALU.add))
                P.wait("scalar", "vector", v_cv)
                if c > 0 and kt == 0:
                    P.wait("scalar", "tensor", v_ucread_last_prev)
                v_uc_silu_prev[0] = P.inc("scalar", "scalar",
                    lambda e, kt=kt: e.activation(
                        out=uc_sb[:, kt, :], in_=rse[0][0:128, 0:TC], func=AF.Silu,
                        bias=convb[:, kt:kt + 1]))
            v_uc_done = P.semv["scalar"]

            P.wait("tensor", "scalar", v_uc_done)
            if c == 0:
                P.wait("tensor", "gpsimd", v_f_wsplit)
            P.wait("vector", "sync", v_xdo_prev)
            for ccw in range(TC // 512):
                bi = 2 + (ccw % 2)
                bank_wait("tensor", bi)
                for kt in range(8):
                    for si, wsb in enumerate((wxpT_h, wxpT_l)):
                        def mmx(e, bi=bi, kt=kt, ccw=ccw, wsb=wsb,
                                st=(kt == 0 and si == 0), sp=(kt == 7 and si == 1)):
                            return e.matmul(PB[bi][0:96, 0:512], wsb[:, kt, :],
                                            uc_sb[:, kt, ccw * 512:(ccw + 1) * 512],
                                            start=st, stop=sp)
                        if kt == 7 and si == 1:
                            v_mmx = P.inc("tensor", "tensor", mmx)
                        else:
                            P.emit("tensor", mmx)
                P.wait("vector", "tensor", v_mmx)
                v_xd = P.inc("vector", "vector", lambda e, bi=bi, ccw=ccw:
                             e.tensor_copy(out=xdbl[:, ccw * 512:(ccw + 1) * 512],
                                           in_=PB[bi][0:96, 0:512]))
                bank_free[bi] = ("vector", v_xd)
            v_f_xd = vfence(v_xd)
            P.wait("sync", "gpsimd", v_f_xd)
            P.wait("sync", "vector", v_xd)
            if c > 0:
                P.wait("sync", "cc", v_ar_prev)
            v_xdo = sload(d_xdbl_in, xdbl[:])
            v_xdo_prev = v_xdo
            P.wait("gpsimd", "sync", v_xdo)
            v_ar = P.inc("gpsimd", "cc", lambda e: e.collective_compute(
                "AllReduce", ALU.add,
                replica_groups=[[0, 1], [2, 3], [4, 5], [6, 7]],
                ins=[d_xdbl_in], outs=[d_xdbl_out]))
            v_ar_prev = v_ar

            for et in range(8, 16):
                v_inproj_z = inproj_tile(et, 1, TC, ("z", et - 8))
            v_inproj_last_prev = P.semv["tensor"]
            v_z_evac = P.semv["scalar"]

            P.wait("gpsimd", "cc", v_ar)
            v_dtp = gload(dtp[:], d_xdbl_out[0:DTR, :])
            P.wait("tensor", "gpsimd", v_dtp)
            v_dtadd_prev = [0]
            if c > 0:
                P.wait("vector", "scalar", v_dA_last_prev)
                P.wait("scalar", "vector", v_outproj_evac_prev)
                P.wait("scalar", "sync", v_rs_stores)
            for kt in range(8):
                for ccw in range(TC // 512):
                    bi = 4 + ((kt * 2 + ccw) % 2)
                    bank_wait("tensor", bi)
                    for si, wsb in enumerate((wdtT_h, wdtT_l)):
                        def mmd_(e, bi=bi, kt=kt, ccw=ccw, wsb=wsb,
                                 st=(si == 0), sp=(si == 1)):
                            return e.matmul(
                                PB[bi][:, 0:512],
                                wsb[:, kt * 128:(kt + 1) * 128],
                                dtp[:, ccw * 512:(ccw + 1) * 512],
                                start=st, stop=sp)
                        if si == 1:
                            v_mmd = P.inc("tensor", "tensor", mmd_)
                        else:
                            P.emit("tensor", mmd_)
                    P.wait("scalar", "tensor", v_mmd)
                    P.wait("scalar", "vector", v_dtadd_prev[0])
                    P.inc("scalar", "scalar", lambda e, bi=bi, kt=kt: e.activation(
                        out=sp_tmp[:, 0, :], in_=PB[bi][:, 0:512], func=AF.Abs,
                        bias=bdt[:, kt:kt + 1]))
                    P.inc("scalar", "scalar", lambda e: e.activation(
                        out=sp_tmp[:, 0, :], in_=sp_tmp[:, 0, :], func=AF.Exp,
                        scale=-1.0))
                    P.inc("scalar", "scalar", lambda e: e.activation(
                        out=sp_tmp[:, 0, :], in_=sp_tmp[:, 0, :], func=AF.Ln,
                        bias=1.0))
                    v_sp = P.inc("scalar", "scalar", lambda e, bi=bi, kt=kt:
                                 e.activation(out=sp_tmp[:, 1, :],
                                              in_=PB[bi][:, 0:512], func=AF.Relu,
                                              bias=bdt[:, kt:kt + 1]))
                    bank_free[bi] = ("scalar", v_sp)
                    P.wait("vector", "scalar", v_sp)
                    v_dtadd_prev[0] = P.inc("vector", "vector",
                          lambda e, kt=kt, ccw=ccw:
                          e.tensor_add(dt_sb[:, kt, ccw * 512:(ccw + 1) * 512],
                                       sp_tmp[:, 1, :], sp_tmp[:, 0, :]))
                if kt == 3:
                    v_dt_half0 = P.semv["vector"]
            v_dt_done = P.semv["vector"]
            v_dt_half = [v_dt_half0, v_dt_done]

            P.wait("scalar", "vector", v_st0 if c == 0 else v_gate_prev)
            v_strd = P.inc("scalar", "scalar", lambda e: e.copy(
                state_rd[:].rearrange("p a b -> p (a b)"),
                state[:].rearrange("p a b -> p (a b)")))
            P.wait("vector", "scalar", v_strd)

            P.wait("scalar", "sync", v_abc)
            v_scan_vec_prev = [0]
            v_bc_pair = [0, 0]
            v_bc_read = [0, 0]

            def emit_bc_load(k, first_of_chunk):
                ng, j = (k % 8) // 2, k % 2
                if first_of_chunk:
                    P.wait("gpsimd", "tensor", v_inproj_last_prev)
                if c > 0 or k >= 2:
                    P.wait("gpsimd", "vector",
                           v_bc_read[j] if k >= 2 else v_bc_read_prev)
                for i2 in range(2):
                    s = 2 * j + i2
                    src = bass.AP(d_xdbl_out.tensor, (DTR + 4 * ng + s) * TC,
                                  [[0, 128], [NST * TC, 2], [1, TC]])
                    v_bc_pair[j] = gload(
                        BCflat[:, s * 2 * TC:(s + 1) * 2 * TC], src)

            emit_bc_load(0, True)
            emit_bc_load(1, True)
            for ktg in range(2):
                kts = range(ktg * 4, ktg * 4 + 4)
                P.wait("scalar", "vector", v_dt_half[ktg])
                for ng in range(4):
                    for n4 in range(4):
                        n = 4 * ng + n4
                        if n4 % 2 == 0:
                            P.wait("vector", "gpsimd", v_bc_pair[n4 // 2])
                        P.wait("scalar", "vector", v_scan_vec_prev[0])
                        v_dA_half = []
                        for q in range(2):
                            ktp = ktg * 4 + q * 2
                            v_dA_half.append(P.inc("scalar", "scalar",
                                  lambda e, q=q, ktp=ktp, n=n: e.activation(
                                      out=dA[q][:].rearrange("p a b -> p (a b)"),
                                      in_=dt_sb[:, ktp:ktp + 2, :]
                                      .rearrange("p a b -> p (a b)"),
                                      func=AF.Exp, scale=abc[:, n:n + 1])))
                        for ki, kt in enumerate(kts):
                            if ki % 2 == 0:
                                P.wait("vector", "scalar", v_dA_half[ki // 2])
                            P.inc("vector", "vector",
                                  lambda e, kt=kt, n4=n4: e.tensor_tensor(
                                      out=bu[:], in0=uc_sb[:, kt, :],
                                      in1=BC[:, n4, 0, :], op=ALU.mult))
                            P.inc("vector", "vector",
                                  lambda e, kt=kt, ki=ki, n=n:
                                  e.tensor_tensor_scan(
                                      out=hbuf[:, ki, :],
                                      data0=dA[(kt % 4) // 2][:, kt % 2, :],
                                      data1=bu[:],
                                      initial=state_rd[:, kt, n:n + 1],
                                      op0=ALU.mult, op1=ALU.add))
                        v_scan_vec_prev[0] = P.semv["vector"]
                        gp_kts = () if n == NST - 1 else KT_G
                        for ki, kt in enumerate(kts):
                            slot = PSLOT[ki][n % 2]
                            if ki in KT_G:
                                P.wait("vector", "gpsimd", g_last[ki][n % 2])
                            P.inc("vector", "vector",
                                  lambda e, ki=ki, n4=n4, slot=slot:
                                  e.tensor_tensor(out=pb[slot][:],
                                                  in0=hbuf[:, ki, :],
                                                  in1=BC[:, n4, 1, :],
                                                  op=ALU.mult))
                        P.inc("vector", "vector", lambda e, ktg=ktg, n=n:
                              e.tensor_copy(
                                  out=state[:, ktg * 4:ktg * 4 + 4, n:n + 1],
                                  in_=hbuf[:, :, TC - 1:TC]))
                        for ki in range(4):
                            if ki in gp_kts:
                                continue
                            slot = PSLOT[ki][n % 2]
                            if ki in KT_G:
                                P.wait("vector", "gpsimd", g_last[ki][(n + 1) % 2])
                            if n == 0:
                                P.inc("vector", "vector",
                                      lambda e, ki=ki, slot=slot: e.tensor_copy(
                                          out=yacc[:, 0, ki, :], in_=pb[slot][:]))
                            else:
                                P.inc("vector", "vector",
                                      lambda e, ki=ki, n=n, slot=slot:
                                      e.tensor_tensor(
                                          out=yacc[:, n % 2, ki, :],
                                          in0=yacc[:, (n + 1) % 2, ki, :],
                                          in1=pb[slot][:], op=ALU.add))
                        v_pm_end = P.semv["vector"]
                        if gp_kts:
                            P.wait("gpsimd", "vector", v_pm_end)
                            P.emit("gpsimd", lambda e: e.engine_nop())
                        for ki in gp_kts:
                            slot = PSLOT[ki][n % 2]
                            if n == 0:
                                g_last[ki][0] = P.inc(
                                    "gpsimd", "gpsimd",
                                    lambda e, ki=ki, slot=slot: e.tensor_copy(
                                        out=yacc[:, 0, ki, :], in_=pb[slot][:]), 1)
                            else:
                                g_last[ki][n % 2] = P.inc(
                                    "gpsimd", "gpsimd",
                                    lambda e, ki=ki, n=n, slot=slot:
                                    e.tensor_tensor(
                                        out=yacc[:, n % 2, ki, :],
                                        in0=yacc[:, (n + 1) % 2, ki, :],
                                        in1=pb[slot][:], op=ALU.add), 1)
                        if n4 % 2 == 1:
                            jj = n4 // 2
                            v_bc_read[jj] = P.semv["vector"]
                            kk = ktg * 8 + ng * 2 + jj
                            if kk + 2 < 16:
                                emit_bc_load(kk + 2, False)
                    v_bc_read_prev = P.semv["vector"]
                v_dA_ktg_end = P.semv["scalar"]
                v_gy_end = P.semv["gpsimd"]
                P.wait("vector", "gpsimd", v_gy_end)
                P.wait("vector", "scalar", v_dA_ktg_end)
                P.wait("vector", "tensor", v_ucread_last_prev)
                P.wait("vector", "scalar", v_z_evac)
                P.wait("vector", "sync", v_dvec)
                for ki, kt in enumerate(kts):
                    P.inc("vector", "vector", lambda e, kt=kt, ki=ki:
                          e.scalar_tensor_tensor(
                              out=rse[0][0:128, 0:TC], in0=uc_sb[:, kt, :],
                              scalar=dvec[:, kt:kt + 1],
                              in1=yacc[:, 1, ki, :], op0=ALU.mult, op1=ALU.add))
                    P.inc("vector", "vector", lambda e, kt=kt: e.tensor_tensor(
                        out=rse[1][0:128, 0:TC], in0=rse[0][0:128, 0:TC],
                        in1=sz_sb[:, kt, :], op=ALU.mult))
                    P.inc("vector", "vector", lambda e, kt=kt: e.tensor_copy(
                        out=ymh[:, kt, :], in_=rse[1][0:128, 0:TC]))
                    P.inc("vector", "vector", lambda e, kt=kt:
                          e.scalar_tensor_tensor(
                              out=yml[:, kt, :], in0=ymh[:, kt, :], scalar=-1.0,
                              in1=rse[1][0:128, 0:TC],
                              op0=ALU.mult, op1=ALU.add))
            v_gate_prev = P.semv["vector"]
            v_dA_last_prev = P.semv["scalar"]
            v_gy_last_prev = P.semv["gpsimd"]

            P.wait("gpsimd", "vector", v_gate_prev)
            v_echo_g = P.inc("gpsimd", "gpsimd", lambda e: e.engine_nop(), 1)
            P.wait("tensor", "gpsimd", v_echo_g)
            P.wait("tensor", "vector", v_gate_prev)
            for ts in range(8):
                ob = (0 + 2 * (ts % 2), 1 + 2 * (ts % 2))
                for bnk in ob:
                    bank_wait("tensor", bnk)
                for kt in range(8):
                    for si, ysb in enumerate((ymh, yml)):
                        for half in range(2):
                            def mmo(e, kt=kt, ts=ts, half=half, bnk=ob[half],
                                    ysb=ysb, st=(kt == 0 and si == 0),
                                    sp=(kt == 7 and si == 1)):
                                return e.matmul(
                                    PB[bnk][:, 0:512],
                                    ysb[:, kt, ts * 128:(ts + 1) * 128],
                                    woutT[:, kt, half * 512:(half + 1) * 512],
                                    start=st, stop=sp)
                            if kt == 7 and si == 1 and half == 1:
                                v_mmo = P.inc("tensor", "tensor", mmo)
                            else:
                                P.emit("tensor", mmo)
                P.wait("vector", "tensor", v_mmo)
                reb = ts % 2
                v_ev = None
                for half in range(2):
                    v_ev = P.inc("vector", "vector",
                                 lambda e, reb=reb, half=half, bnk=ob[half]:
                                 e.tensor_copy(
                                     out=rse[reb][0:128, half * 512:(half + 1) * 512],
                                     in_=PB[bnk][:, 0:512]))
                for half in range(2):
                    bank_free[ob[half]] = ("vector", v_ev)
                v_f_rs = vfence(v_ev)
                P.wait("sync", "gpsimd", v_f_rs)
                P.wait("sync", "vector", v_ev)
                v_rs_stores = sload(d_rs_in[t0 + ts * 128:t0 + (ts + 1) * 128, :],
                                    rse[reb][0:128, 0:D])
            v_ucread_last_prev = P.semv["tensor"]
            v_outproj_evac_prev = P.semv["vector"]

            if debug == 2:
                P.wait("gpsimd", "vector", v_gate_prev)
                if c == NCH - 1:
                    P.inc("gpsimd", "gpsimd", lambda e: e.dma_start(
                        out=ddbg2["ym3"].rearrange("p (a b) -> p a b", a=8),
                        in_=ymh[:]), 16)
                if c == 0:
                    P.inc("gpsimd", "gpsimd", lambda e: e.dma_start(
                        out=ddbg2["ymine"].rearrange("p (a b) -> p a b", a=8),
                        in_=ymh[:]), 16)
                    P.inc("gpsimd", "gpsimd", lambda e: e.dma_start(
                        out=ddbg2["dt"].rearrange("p (a b) -> p a b", a=8),
                        in_=dt_sb[:]), 16)
                P.inc("gpsimd", "gpsimd", lambda e, c=c: e.dma_start(
                    out=ddbg2["state"].rearrange("p (c a b) -> p c a b", c=NCH, a=8)[:, c],
                    in_=state[:]), 16)
            if debug is True and c == 0:
                P.wait("gpsimd", "vector", P.semv["vector"])
                P.wait("gpsimd", "scalar", P.semv["scalar"])
                for key, src_ap in (("u", u_sb[:, :, 1:TC + 1]), ("uc", uc_sb[:]),
                                    ("sz", sz_sb[:]), ("dt", dt_sb[:]),
                                    ("ymine", ymh[:])):
                    P.inc("gpsimd", "gpsimd",
                          lambda e, d=ddbg[key], s=src_ap: e.dma_start(
                              out=d.rearrange("p (a b) -> p a b", a=8), in_=s), 16)
                P.wait("sync", "vector", P.semv["vector"])
                sload(ddbg["xdbl"], xdbl[:])
            if c == 0:
                v_wq_ag = []
                for a, b in ((d_wgT_pre, d_wgT_blk), (d_wuT_pre, d_wuT_blk),
                             (d_wdT_pre, d_wdT_blk)):
                    P.wait("gpsimd", "sync", v_wq_store)
                    v_wq_ag.append(P.inc("gpsimd", "cc",
                                         lambda e, a=a, b=b: e.collective_compute(
                                             "AllGather", ALU.bypass,
                                             replica_groups=[[0, 1, 2, 3, 4, 5, 6, 7]],
                                             ins=[a], outs=[b])))

        P.wait("gpsimd", "sync", v_rs_stores)
        v_rs = P.inc("gpsimd", "cc", lambda e: e.collective_compute(
            "ReduceScatter", ALU.add,
            replica_groups=[[0, 1], [2, 3], [4, 5], [6, 7]],
            ins=[d_rs_in], outs=[d_rs_out]))

        if debug == 2:
            P.wait("gpsimd", "sync", v_rs_stores)
            P.inc("gpsimd", "gpsimd",
                  lambda e: e.dma_start(out=ddbg2["rsin"], in_=d_rs_in), 16)
            P.wait("gpsimd", "cc", v_rs)
            P.inc("gpsimd", "gpsimd",
                  lambda e: e.dma_start(out=ddbg2["rsout"], in_=d_rs_out), 16)
        P.wait("sync", "cc", v_wq_ag[-1])
        P.wait("sync", "tensor", P.semv["tensor"])
        P.wait("sync", "vector", P.semv["vector"])
        P.wait("sync", "scalar", P.semv["scalar"])
        barrier_s = P.inc("sync", "sync", lambda e: e.dma_start(
            out=d_abounce, in_=alog_t[:]), 16)
        for eng in ("tensor", "vector", "scalar", "gpsimd"):
            P.wait(eng, "sync", barrier_s)

        v_wd_l = 0
        for it in range(22):
            wdsrc = bass.AP(d_wdT_blk.tensor, it * 128 * DSL,
                            [[DSL, 128], [FI * DSL, 8], [1, DSL]])
            v_wd_l = sload(wdqT[:, it, :], wdsrc)

        P.wait("sync", "cc", v_rs)
        b_qf = [A.view(o, F32, [128, D]) for o in b_q]
        b_qi = A.view(b_ti, F32, [128, D])
        v_b2_r_prev = [0]
        v_b2_x2store_prev = [0]
        v_b2_tr_prev = [0]
        v_b2_ssq_prev = [0]
        for tt in range(16):
            if tt >= 1:
                P.wait("sync", "vector", v_b2_r_prev[0])
            v_rl = sload(rsld[tt % 2][:], d_rs_out[tt * 128:(tt + 1) * 128, :])
            v_xr = sload(x2l[tt % 2][:], dx_res[tt * 128:(tt + 1) * 128, :])
            P.wait("vector", "sync", max(v_rl, v_xr))
            P.wait("vector", "scalar", v_b2_ssq_prev[0])
            v_r = P.inc("vector", "vector", lambda e, tt=tt: e.scalar_tensor_tensor(
                out=r_sb[:], in0=x2l[tt % 2][:], scalar=2.0,
                in1=rsld[tt % 2][:], op0=ALU.mult, op1=ALU.add))
            v_b2_r_prev[0] = v_r
            P.wait("vector", "sync", v_b2_x2store_prev[0])
            v_x2, v_ssq = ln_tail(r_sb[:], x2_sb[:], v_r)
            v_b2_ssq_prev[0] = v_ssq
            v_f_x2 = vfence(v_x2)
            P.wait("sync", "gpsimd", v_f_x2)
            P.wait("sync", "vector", v_x2)
            v_b2_x2store_prev[0] = sload(d_x2[tt * 128:(tt + 1) * 128, :], x2_sb[:])
            if debug is True and tt == 0:
                P.wait("sync", "vector", P.semv["vector"])
                P.wait("sync", "scalar", P.semv["scalar"])
                sload(ddbg["rs"], rsld[0][:])
                sload(ddbg["x2"], x2_sb[:])
                sload(ddbg["r"], r_sb[:])
                for li, lk in enumerate(("sr", "ssq", "mu", "m2e", "mu2", "ve",
                                         "iv2", "rstd")):
                    P.inc("sync", "sync",
                          lambda e, li=li, lk=lk: e.dma_start(
                              out=ddbg["ln"][:, li:li + 1], in_=ln[lk][:],
                              allow_slow_non_contiguous=True), 16)
                P.wait("vector", "sync", P.semv["sync"])
                P.wait("scalar", "sync", P.semv["sync"])
            P.wait("vector", "tensor", v_b2_tr_prev[0])
            v_x2q = quant_fp4(x2_sb[:], x2q_sb[:], D, b_qf, b_qi, 128,
                              src_ready=("vector", v_x2))
            if debug is True and tt == 0:
                P.wait("gpsimd", "vector", v_x2q)
                P.inc("gpsimd", "gpsimd", lambda e: e.dma_start(
                    out=ddbg["x2q"], in_=x2q_sb[:]), 16)
            P.wait("tensor", "vector", v_x2q)
            for dt_ in range(8):
                bnk = 6 + (dt_ % 2)
                bank_wait("tensor", bnk)
                v_tr = P.inc("tensor", "tensor", lambda e, dt_=dt_, bnk=bnk:
                             e.transpose(PB[bnk][:, 0:64].bitcast(BF16),
                                         x2q_sb[:, dt_ * 128:(dt_ + 1) * 128],
                                         ident[:]))
                P.wait("vector", "tensor", v_tr)
                v_evt = P.inc("vector", "vector",
                              lambda e, dt_=dt_, tt=tt, bnk=bnk: e.tensor_copy(
                                  out=x2qT[:, dt_, tt * 128:(tt + 1) * 128],
                                  in_=PB[bnk][:, 0:64].bitcast(BF16)))
                bank_free[bnk] = ("vector", v_evt)
            v_b2_tr_prev[0] = P.semv["tensor"]
        v_x2qT_done = P.semv["vector"]

        c_qf = [A.view(o, F32, [128, 512]) for o in c_q]
        c_qi = A.view(c_hqld[0], F32, [128, 512])
        P.wait("tensor", "vector", v_x2qT_done)
        ics = [(k * ISL, (k + 1) * ISL) for k in range(8)]
        v_hq_store = 0
        v_ffn_mm_prev = [0]
        v_ffn_h_prev = [0]
        for ici, (i0, i1) in enumerate(ics):
            iw = i1 - i0
            cb = ici % 2
            if ici == 0:
                P.wait("sync", "vector", v_x2qT_done)
            else:
                P.wait("sync", "tensor", v_ffn_mm_prev[0])
            v_wg_l = 0
            for dt_ in range(8):
                r0 = ici * D + dt_ * 128
                v_wg_l = sload(wgc[cb][:, dt_, 0:iw], d_wgT_blk[r0:r0 + 128, :])
                v_wg_l = max(v_wg_l, sload(wuc[cb][:, dt_, 0:iw],
                                           d_wuT_blk[r0:r0 + 128, :]))
            P.wait("tensor", "sync", v_wg_l)
            for tt in range(16):
                bank_wait("tensor", 2)
                bank_wait("tensor", 3)
                for dt_ in range(8):
                    def mmg(e, tt=tt, dt_=dt_, cb=cb, iw=iw, st=(dt_ == 0),
                            sp=(dt_ == 7)):
                        return e.matmul(PB[2][:, 0:iw],
                                        x2qT[:, dt_, tt * 128:(tt + 1) * 128],
                                        wgc[cb][:, dt_, 0:iw], start=st, stop=sp)
                    if dt_ == 7:
                        v_mmg = P.inc("tensor", "tensor", mmg)
                    else:
                        P.emit("tensor", mmg)
                for dt_ in range(8):
                    def mmu(e, tt=tt, dt_=dt_, cb=cb, iw=iw, st=(dt_ == 0),
                            sp=(dt_ == 7)):
                        return e.matmul(PB[3][:, 0:iw],
                                        x2qT[:, dt_, tt * 128:(tt + 1) * 128],
                                        wuc[cb][:, dt_, 0:iw], start=st, stop=sp)
                    if dt_ == 7:
                        v_mmu = P.inc("tensor", "tensor", mmu)
                    else:
                        P.emit("tensor", mmu)
                P.wait("scalar", "tensor", v_mmg)
                P.wait("scalar", "vector", v_ffn_h_prev[0])
                v_ffn_mm_prev[0] = v_mmu
                v_sg = P.inc("scalar", "scalar", lambda e, iw=iw: e.activation(
                    out=sg_sb[:, 0:iw], in_=PB[2][:, 0:iw], func=AF.Silu))
                bank_free[2] = ("scalar", v_sg)
                P.wait("vector", "scalar", v_sg)
                P.wait("vector", "tensor", v_mmu)
                v_h = P.inc("vector", "vector", lambda e, iw=iw: e.tensor_tensor(
                    out=h_sb[:, 0:iw], in0=sg_sb[:, 0:iw], in1=PB[3][:, 0:iw],
                    op=ALU.mult))
                bank_free[3] = ("vector", v_h)
                v_ffn_h_prev[0] = v_h
                if debug is True and ici == 0 and tt == 0:
                    P.wait("gpsimd", "scalar", v_sg)
                    P.inc("gpsimd", "gpsimd", lambda e: e.dma_start(
                        out=ddbg["gate"], in_=sg_sb[:, 0:512]), 16)
                P.wait("vector", "sync", v_hq_store)
                v_hq = quant_fp4(h_sb[:, 0:iw], hq_sb[:, 0:iw], iw, c_qf, c_qi,
                                 128, src_ready=("vector", v_h))
                v_f_hq = vfence(v_hq)
                P.wait("sync", "gpsimd", v_f_hq)
                P.wait("sync", "vector", v_hq)
                v_hq_store = sload(d_hq[tt * 128:(tt + 1) * 128, i0:i1],
                                   hq_sb[:, 0:iw])
                if debug is True and tt == 0:
                    P.wait("gpsimd", "vector", v_hq)
                    P.inc("gpsimd", "gpsimd",
                          lambda e, i0=i0, i1=i1: e.dma_start(
                              out=ddbg["hq"][:, i0:i1], in_=hq_sb[:, 0:i1 - i0]), 16)

        P.wait("sync", "sync", v_hq_store)
        v_mmd_prev = [0]
        v_tr_prev = [0]
        v_r2_prev = [0]
        v_b4_ssq_prev = [0]
        for tt in range(16):
            if tt >= 1:
                P.wait("sync", "tensor", v_tr_prev[0])
                P.wait("sync", "vector", v_r2_prev[0])
            v_hl = sload(hqld[tt % 2][:], d_hq[tt * 128:(tt + 1) * 128, :])
            v_x2r = sload(x2l[tt % 2][:], d_x2[tt * 128:(tt + 1) * 128, :])
            P.wait("tensor", "sync", v_hl)
            P.wait("vector", "tensor", v_mmd_prev[0])
            for it in range(22):
                bnk = 6 + (it % 2)
                bank_wait("tensor", bnk)
                v_tr = P.inc("tensor", "tensor", lambda e, tt=tt, it=it, bnk=bnk:
                             e.transpose(PB[bnk][:, 0:64].bitcast(BF16),
                                         hqld[tt % 2][:, it * 128:(it + 1) * 128],
                                         ident[:]))
                P.wait("vector", "tensor", v_tr)
                v_evt = P.inc("vector", "vector", lambda e, it=it, bnk=bnk:
                              e.tensor_copy(out=hqT[:, it, :], in_=PB[bnk][:, 0:64].bitcast(BF16)))
                bank_free[bnk] = ("vector", v_evt)
            v_hqT_done = P.semv["vector"]
            v_tr_prev[0] = P.semv["tensor"]
            P.wait("gpsimd", "vector", v_hqT_done)
            v_echo_h = P.inc("gpsimd", "gpsimd", lambda e: e.engine_nop(), 1)
            P.wait("tensor", "gpsimd", v_echo_h)
            P.wait("tensor", "vector", v_hqT_done)
            if tt == 0:
                P.wait("tensor", "sync", v_wd_l)
            db = (0 + 2 * (tt % 2), 1 + 2 * (tt % 2))
            for bnk in db:
                bank_wait("tensor", bnk)
            for it in range(22):
                for half in range(2):
                    def mmd(e, it=it, half=half, bnk=db[half], st=(it == 0),
                            sp=(it == 21)):
                        return e.matmul(PB[bnk][:, 0:512], hqT[:, it, :],
                                        wdqT[:, it, half * 512:(half + 1) * 512],
                                        start=st, stop=sp)
                    if it == 21 and half == 1:
                        v_mmd = P.inc("tensor", "tensor", mmd)
                    else:
                        P.emit("tensor", mmd)
            P.wait("vector", "tensor", v_mmd)
            P.wait("vector", "sync", v_x2r)
            P.wait("vector", "scalar", v_b4_ssq_prev[0])
            v_r2 = None
            for half in range(2):
                v_r2 = P.inc("vector", "vector",
                             lambda e, tt=tt, half=half, bnk=db[half]:
                             e.tensor_tensor(
                                 out=r_sb[:, half * 512:(half + 1) * 512],
                                 in0=PB[bnk][:, 0:512],
                                 in1=x2l[tt % 2][:, half * 512:(half + 1) * 512],
                                 op=ALU.add))
            for half in range(2):
                bank_free[db[half]] = ("vector", v_r2)
            v_mmd_prev[0] = v_mmd
            v_r2_prev[0] = v_r2
            if tt >= 2:
                P.wait("vector", "out", 16 * (tt - 1))
            v_o, v_ssq4 = ln_tail(r_sb[:], out_sb[tt % 2][:], v_r2)
            v_b4_ssq_prev[0] = v_ssq4
            v_f_o = vfence(v_o)
            P.wait("sync", "gpsimd", v_f_o)
            P.wait("sync", "vector", v_o)
            P.inc("sync", "out", lambda e, tt=tt: e.dma_start(
                out=dout[tt * 128:(tt + 1) * 128, :], in_=out_sb[tt % 2][:]), 16)

        P.wait("sync", "out", P.semv["out"])

        P.run(block)
    return nc


_NC_CACHE = [None]


def make_in_maps(x, W_in, conv_w, conv_b, W_xp, W_dt, b_dt, A_log, Dv, W_out,
                 wg, wu, wd):
    ident = np.eye(128, dtype=np.float32)
    in_maps = []
    for c in range(8):
        s, j = c // 2, c % 2
        ch = slice(j * HC, (j + 1) * HC)
        in_maps.append({
            "xT": np.ascontiguousarray(x[s].T),
            "xres": np.ascontiguousarray(x[s, j * TH:(j + 1) * TH, :]),
            "winT": np.ascontiguousarray(np.concatenate(
                [W_in[ch].T, W_in[H + j * HC:H + (j + 1) * HC].T], axis=1)),
            "convw": np.ascontiguousarray(conv_w[ch, 0, :]),
            "convb": np.ascontiguousarray(conv_b[ch].reshape(8, 128).T),
            "wxpT": np.ascontiguousarray(W_xp[:, ch].T),
            "wdtT": np.ascontiguousarray(W_dt[ch].T),
            "bdt": np.ascontiguousarray(b_dt[ch].reshape(8, 128).T),
            "alog": np.ascontiguousarray(A_log[None, :]),
            "dvec": np.ascontiguousarray(Dv[ch].reshape(8, 128).T),
            "woutT": np.ascontiguousarray(W_out[:, ch].T),
            "wg_s": np.ascontiguousarray(wg[c * ISL:(c + 1) * ISL]),
            "wu_s": np.ascontiguousarray(wu[c * ISL:(c + 1) * ISL]),
            "wd_s": np.ascontiguousarray(wd[c * DSL:(c + 1) * DSL]),
            "identin": ident,
        })
    return in_maps


def kernel(L0_attn_x, L0_attn_W_in, L0_attn_conv_w, L0_attn_conv_b,
           L0_attn_W_xproj, L0_attn_W_dt, L0_attn_b_dt, L0_attn_A_log,
           L0_attn_D, L0_attn_W_out, L0_ffn_w_gate, L0_ffn_w_up, L0_ffn_w_down):
    in_maps = make_in_maps(
        np.asarray(L0_attn_x, np.float32), np.asarray(L0_attn_W_in, np.float32),
        np.asarray(L0_attn_conv_w, np.float32),
        np.asarray(L0_attn_conv_b, np.float32),
        np.asarray(L0_attn_W_xproj, np.float32),
        np.asarray(L0_attn_W_dt, np.float32),
        np.asarray(L0_attn_b_dt, np.float32),
        np.asarray(L0_attn_A_log, np.float32),
        np.asarray(L0_attn_D, np.float32), np.asarray(L0_attn_W_out, np.float32),
        np.asarray(L0_ffn_w_gate, np.float32),
        np.asarray(L0_ffn_w_up, np.float32),
        np.asarray(L0_ffn_w_down, np.float32))
    if _NC_CACHE[0] is None:
        _NC_CACHE[0] = build()
    nc = _NC_CACHE[0]
    res = run_bass_kernel_spmd(nc, in_maps, core_ids=list(range(8)))
    out = np.empty((B, T, D), np.float32)
    for c in range(8):
        s, j = c // 2, c % 2
        out[s, j * TH:(j + 1) * TH, :] = res.results[c]["out"]
    return out


if __name__ == "__main__":
    print("building ...")
    nc = build()
    n = sum(len(b.instructions) for f in nc.m.functions for b in f.blocks)
    print("built, instructions:", n)


# version 19
# speedup vs baseline: 1.0689x; 1.0003x over previous
import contextlib
import numpy as np
import concourse.bass as bass
import concourse.mybir as mybir
from concourse.bass_utils import run_bass_kernel_spmd

F32 = mybir.dt.float32
BF16 = mybir.dt.bfloat16
F16 = mybir.dt.float16
I32 = mybir.dt.int32
U8 = mybir.dt.uint8
AF = mybir.ActivationFunctionType
ALU = mybir.AluOpType
AX = mybir.AxisListType

B, T, D = 4, 4096, 1024
H, HC, NST, DTR, FI = 2048, 1024, 16, 64, 2816
TC = 1024
NCH = T // TC
TH = 2048
LN_EPS = 1e-5
ISL = FI // 8
DSL = D // 8

ENGS = ("sync", "scalar", "vector", "tensor", "gpsimd")


class Prog:
    def __init__(self, nc, es):
        self.nc = nc
        self.q = {e: [] for e in ENGS}
        self.semh = {}
        self.semv = {}
        for s in ENGS + ("cc", "out"):
            self.semh[s] = es.enter_context(nc.semaphore(f"sem_{s}"))
            self.semv[s] = 0

    def emit(self, eng, fn):
        self.q[eng].append(fn)

    def wait(self, eng, sem, val):
        if val and val > 0:
            h = self.semh[sem]
            self.q[eng].append(lambda e, h=h, v=val: e.wait_ge(h, v))

    def inc(self, eng, sem, fn, k=1):
        self.semv[sem] += k
        tgt = self.semv[sem]
        h = self.semh[sem]
        self.q[eng].append(lambda e, f=fn, h=h, kk=k: f(e).then_inc(h, kk))
        return tgt

    def run(self, block):
        @block.sync
        def _(e):
            for f in self.q["sync"]:
                f(e)

        @block.scalar
        def _(e):
            for f in self.q["scalar"]:
                f(e)

        @block.vector
        def _(e):
            for f in self.q["vector"]:
                f(e)

        @block.tensor
        def _(e):
            for f in self.q["tensor"]:
                f(e)

        @block.gpsimd
        def _(e):
            for f in self.q["gpsimd"]:
                f(e)


def cdiv(a, b):
    return (a + b - 1) // b


ARENA_BYTES = 194688


class Arena:

    def __init__(self, ap):
        self.ap = ap
        self.off = 0

    def alloc(self, nbytes, align=512):
        o = (self.off + align - 1) // align * align
        self.off = o + nbytes
        assert self.off <= ARENA_BYTES, f"arena overflow {self.off}"
        return o

    def view(self, off, dtype, shape):
        esz = {F32: 4, BF16: 2, F16: 2, I32: 4}[dtype]
        n = int(np.prod(shape[1:])) if len(shape) > 1 else 1
        ap = self.ap[:, off:off + n * esz].bitcast(dtype)
        if len(shape) > 2:
            dims = " ".join(f"d{i}" for i in range(1, len(shape)))
            ap = ap.rearrange(f"p ({dims}) -> p {dims}",
                              **{f"d{i}": shape[i] for i in range(1, len(shape) - 1)})
        return ap


def build(debug=False):
    nc = bass.Bass("TRN2", target_bir_lowering=False,
                   detect_race_conditions=False)

    dx_T = nc.dram_tensor("xT", [D, T], F32, kind="ExternalInput").ap()
    dx_res = nc.dram_tensor("xres", [TH, D], F32, kind="ExternalInput").ap()
    dwinT = nc.dram_tensor("winT", [D, 2 * HC], F32, kind="ExternalInput").ap()
    dconvw = nc.dram_tensor("convw", [HC, 3], F32, kind="ExternalInput").ap()
    dconvb = nc.dram_tensor("convb", [128, 8], F32, kind="ExternalInput").ap()
    dwxpT = nc.dram_tensor("wxpT", [HC, 96], F32, kind="ExternalInput").ap()
    dwdtT = nc.dram_tensor("wdtT", [DTR, HC], F32, kind="ExternalInput").ap()
    dbdt = nc.dram_tensor("bdt", [128, 8], F32, kind="ExternalInput").ap()
    dalog = nc.dram_tensor("alog", [1, NST], F32, kind="ExternalInput").ap()
    ddvec = nc.dram_tensor("dvec", [128, 8], F32, kind="ExternalInput").ap()
    dwoutT = nc.dram_tensor("woutT", [HC, D], F32, kind="ExternalInput").ap()
    dwg_s = nc.dram_tensor("wg_s", [ISL, D], F32, kind="ExternalInput").ap()
    dwu_s = nc.dram_tensor("wu_s", [ISL, D], F32, kind="ExternalInput").ap()
    dwd_s = nc.dram_tensor("wd_s", [DSL, FI], F32, kind="ExternalInput").ap()
    dident = nc.dram_tensor("identin", [128, 128], F32, kind="ExternalInput").ap()

    dout = nc.dram_tensor("out", [TH, D], F32, kind="ExternalOutput").ap()

    if debug == 2:
        ddbg2 = {"ymine": nc.dram_tensor("dbg2_ymine", [128, 8 * TC], F32,
                                         kind="ExternalOutput").ap(),
                 "state": nc.dram_tensor("dbg2_state", [128, 8 * NST * NCH], F32,
                                         kind="ExternalOutput").ap(),
                 "dt": nc.dram_tensor("dbg2_dt", [128, 8 * TC], F32,
                                      kind="ExternalOutput").ap(),
                 "rsin": nc.dram_tensor("dbg2_rsin", [T, D], F32,
                                        kind="ExternalOutput").ap(),
                 "rsout": nc.dram_tensor("dbg2_rsout", [TH, D], F32,
                                         kind="ExternalOutput").ap(),
                 "ym3": nc.dram_tensor("dbg2_ym3", [128, 8 * TC], F32,
                                       kind="ExternalOutput").ap()}
    if debug is True:
        ddbg = {k: nc.dram_tensor(f"dbg_{k}", [128, 8 * TC], F32,
                                  kind="ExternalOutput").ap()
                for k in ("u", "uc", "sz", "dt", "ymine")}
        ddbg["xdbl"] = nc.dram_tensor("dbg_xdbl", [96, TC], F32,
                                      kind="ExternalOutput").ap()
        ddbg["rs"] = nc.dram_tensor("dbg_rs", [128, D], F32,
                                    kind="ExternalOutput").ap()
        ddbg["x2"] = nc.dram_tensor("dbg_x2", [128, D], F32,
                                    kind="ExternalOutput").ap()
        ddbg["x2q"] = nc.dram_tensor("dbg_x2q", [128, D], F32,
                                     kind="ExternalOutput").ap()
        ddbg["hq"] = nc.dram_tensor("dbg_hq", [128, FI], F32,
                                    kind="ExternalOutput").ap()
        ddbg["gate"] = nc.dram_tensor("dbg_gate", [128, 512], F32,
                                      kind="ExternalOutput").ap()
        ddbg["wgT"] = nc.dram_tensor("dbg_wgT", [128, FI], F32,
                                     kind="ExternalOutput").ap()
        ddbg["wgq"] = nc.dram_tensor("dbg_wgq", [FI, D], F32,
                                     kind="ExternalOutput").ap()
        ddbg["ln"] = nc.dram_tensor("dbg_ln", [128, 8], F32,
                                    kind="ExternalOutput").ap()
        ddbg["r"] = nc.dram_tensor("dbg_r", [128, D], F32,
                                   kind="ExternalOutput").ap()
    d_abounce = nc.dram_tensor("a_bounce", [1, NST], F32, kind="Internal").ap()
    d_xdbl_in = nc.dram_tensor("cc_xdbl_in", [96, TC], F32, kind="Internal").ap()
    d_xdbl_out = nc.dram_tensor("cc_xdbl_out", [96, TC], F32, kind="Internal").ap()
    d_rs_in = nc.dram_tensor("cc_rs_in", [T, D], F32, kind="Internal").ap()
    d_rs_out = nc.dram_tensor("cc_rs_out", [TH, D], F32, kind="Internal").ap()
    d_wgT_pre = nc.dram_tensor("cc_wgT_pre", [D, ISL], BF16, kind="Internal").ap()
    d_wgT_blk = nc.dram_tensor("cc_wgT_blk", [8 * D, ISL], BF16, kind="Internal").ap()
    d_wuT_pre = nc.dram_tensor("cc_wuT_pre", [D, ISL], BF16, kind="Internal").ap()
    d_wuT_blk = nc.dram_tensor("cc_wuT_blk", [8 * D, ISL], BF16, kind="Internal").ap()
    d_wdT_pre = nc.dram_tensor("cc_wdT_pre", [FI, DSL], BF16, kind="Internal").ap()
    d_wdT_blk = nc.dram_tensor("cc_wdT_blk", [8 * FI, DSL], BF16, kind="Internal").ap()
    d_x2 = nc.dram_tensor("x2_scr", [TH, D], F32, kind="Internal").ap()
    d_hq = nc.dram_tensor("hq_scr", [TH, FI], BF16, kind="Internal").ap()

    es = contextlib.ExitStack()
    with es:
        sb = lambda name, shape, dt: es.enter_context(nc.sbuf_tensor(name, shape, dt))

        abc = sb("sb_abc", [128, NST], F32)
        ident = sb("sb_ident", [128, 128], BF16)
        convw = sb("sb_convw", [128, 8, 3], F32)
        convb = sb("sb_convb", [128, 8], F32)
        bdt = sb("sb_bdt", [128, 8], F32)
        dvec = sb("sb_dvec", [128, 8], F32)
        state = sb("sb_state", [128, 8, NST], F32)
        state_rd = sb("sb_state_rd", [128, 8, NST], F32)
        wxpT_h = sb("sb_wxpT_h", [128, 8, 96], BF16)
        wxpT_l = sb("sb_wxpT_l", [128, 8, 96], BF16)
        wdtT_h = sb("sb_wdtT_h", [DTR, HC], BF16)
        wdtT_l = sb("sb_wdtT_l", [DTR, HC], BF16)
        dtp = sb("sb_dtp", [DTR, TC], BF16)
        xdbl = sb("sb_xdbl", [96, TC], F32)
        alog_t = sb("sb_alog_t", [1, NST], F32)
        aneg_t = sb("sb_aneg_t", [1, NST], F32)
        ln = {k: sb(f"sb_ln_{k}", [128, 1], F32)
              for k in ("sr", "ssq", "mu", "m2e", "mu2", "ve", "iv2", "rstd")}
        qsml = {k: sb(f"sb_q_{k}", [128, FI // 16], F32) for k in ("m", "sh", "rm", "rm2")}

        arena_t = sb("sb_arena", [128, ARENA_BYTES], U8)
        A = Arena(arena_t[:])

        o_winT = A.alloc(8 * 2048 * 2)
        o_xTc = A.alloc(8 * (TC + 2) * 2, align=64)
        o_u = A.alloc(8 * (TC + 2) * 2, align=64)
        o_uc = A.alloc(8 * TC * 2, align=64)
        o_sz = A.alloc(8 * TC * 2, align=64)
        o_dt = A.alloc(8 * TC * 2, align=64)
        o_dA = [A.alloc(2 * TC * 4, align=64) for _ in range(2)]
        o_bu = A.alloc(TC * 2, align=64)
        o_pb = [A.alloc(TC * 2, align=64) for _ in range(6)]
        o_hbuf = A.alloc(4 * TC * 2, align=64)
        o_ymh = A.alloc(8 * TC * 2, align=64)
        o_woutT = A.alloc(8 * D * 2, align=64)
        o_rse = [A.alloc(D * 4, align=64) for _ in range(2)]
        mix_end = A.off
        o_BC = o_xTc
        o_yacc = o_u
        o_yml = o_dt

        winT = A.view(o_winT, BF16, [128, 8, 2048])
        xTc = A.view(o_xTc, BF16, [128, 8, TC + 2])
        u_sb = A.view(o_u, F16, [128, 8, TC + 2])
        uc_sb = A.view(o_uc, BF16, [128, 8, TC])
        sz_sb = A.view(o_sz, F16, [128, 8, TC])
        dt_sb = A.view(o_dt, F16, [128, 8, TC])
        dA = [A.view(o, F32, [128, 2, TC]) for o in o_dA]
        bu = A.view(o_bu, F16, [128, TC])
        pb = [A.view(o, F16, [128, TC]) for o in o_pb]
        hbuf = A.view(o_hbuf, F16, [128, 4, TC])
        ymh = A.view(o_ymh, BF16, [128, 8, TC])
        yml = A.view(o_yml, BF16, [128, 8, TC])
        yacc = A.view(o_yacc, F16, [128, 2, 4, TC])
        woutT = A.view(o_woutT, BF16, [128, 8, D])
        rse = [A.view(o, F32, [128, D]) for o in o_rse]
        sp_tmp = A.view(o_rse[1], F32, [128, 2, 512])
        BC = A.view(o_BC, F16, [128, 4, 2, TC])
        BCflat = A.view(o_BC, F16, [128, 4 * 2 * TC])
        wxf = A.view(o_sz, F32, [128, 8, 96])
        wdf = A.view(o_sz + 4096, F32, [128, HC])

        q_big_offs = [o_dt, o_ymh, o_woutT, o_xTc]

        oB = Arena(arena_t[:])
        oB.off = 0
        b_x2qT = oB.alloc(8 * TH * 2)
        b_rs = [oB.alloc(D * 4) for _ in range(2)]
        b_r = oB.alloc(D * 4)
        b_x2 = oB.alloc(D * 4)
        b_x2q = oB.alloc(D * 2)
        b_junk = oB.alloc(D * 2)
        b_q = [oB.alloc(D * 4) for _ in range(4)]
        b_wld = [oB.alloc(FI * 2) for _ in range(2)]
        b_ev = oB.alloc(1024 * 2)
        b_ti = oB.alloc(D * 4)
        assert oB.off <= ARENA_BYTES

        x2qT = A.view(b_x2qT, BF16, [128, 8, TH])
        rsld = [A.view(o, F32, [128, D]) for o in b_rs]
        r_sb = A.view(b_r, F32, [128, D])
        x2_sb = A.view(b_x2, F32, [128, D])
        x2q_sb = A.view(b_x2q, BF16, [128, D])
        junk_sb = A.view(b_junk, BF16, [128, D])
        wld = [A.view(o, BF16, [128, FI]) for o in b_wld]
        ev_sb = A.view(b_ev, BF16, [128, 1024])

        oC = Arena(arena_t[:])
        oC.off = b_rs[0]
        c_wg = [oC.alloc(8 * 512 * 2) for _ in range(2)]
        c_wu = [oC.alloc(8 * 512 * 2) for _ in range(2)]
        c_h = oC.alloc(512 * 4)
        c_sg = oC.alloc(512 * 2)
        c_hq = oC.alloc(512 * 2)
        c_q = [oC.alloc(512 * 4) for _ in range(4)]
        c_wdqT = oC.alloc(22 * 1024 * 2)
        c_hqld = [oC.alloc(FI * 2) for _ in range(2)]
        c_hqT = oC.alloc(22 * 128 * 2)
        c_x2l = [oC.alloc(D * 4) for _ in range(2)]
        c_out = [oC.alloc(D * 4) for _ in range(2)]
        assert oC.off <= ARENA_BYTES, f"phaseC overflow {oC.off}"

        wgc = [A.view(o, BF16, [128, 8, 512]) for o in c_wg]
        wuc = [A.view(o, BF16, [128, 8, 512]) for o in c_wu]
        h_sb = A.view(c_h, F32, [128, 512])
        sg_sb = A.view(c_sg, F16, [128, 512])
        hq_sb = A.view(c_hq, BF16, [128, 512])
        wdqT = A.view(c_wdqT, BF16, [128, 22, 1024])
        hqld = [A.view(o, BF16, [128, FI]) for o in c_hqld]
        hqT = A.view(c_hqT, BF16, [128, 22, 128])
        x2l = [A.view(o, F32, [128, D]) for o in c_x2l]
        out_sb = [A.view(o, F32, [128, D]) for o in c_out]

        PB = [es.enter_context(nc.psum_tensor(f"pb{i}", [128, 512], F32))
              for i in range(8)]

        block = es.enter_context(nc.Block())
        P = Prog(nc, es)
        bank_free = {i: None for i in range(8)}

        def bank_wait(eng, i):
            if bank_free[i] is not None:
                s, v = bank_free[i]
                P.wait(eng, s, v)

        def vfence(v_val):
            P.wait("gpsimd", "vector", v_val)
            return P.inc("gpsimd", "gpsimd", lambda e: e.engine_nop(), 1)

        def sload(dst, src):
            return P.inc("sync", "sync",
                         lambda e, d=dst, s=src: e.dma_start(out=d, in_=s), 16)

        def gload(dst, src):
            return P.inc("gpsimd", "gpsimd",
                         lambda e, d=dst, s=src: e.dma_start(out=d, in_=s), 16)

        v_convw = sload(convw[:],
                        dconvw.rearrange("(a p) b -> p a b", p=128))
        v_convb = sload(convb[:], dconvb)
        v_bdt = sload(bdt[:], dbdt)
        v_dvec = sload(dvec[:], ddvec)
        v_alog = sload(alog_t[:], dalog)
        v_ident = gload(ident[:], dident)
        v_wxf = sload(wxf[:], dwxpT.rearrange("(a p) b -> p a b", p=128))
        v_wdf = sload(wdf[0:DTR, :], dwdtT)
        P.wait("vector", "sync", max(v_wxf, v_wdf))
        P.inc("vector", "vector", lambda e: e.tensor_copy(out=wxpT_h[:], in_=wxf[:]))
        P.inc("vector", "vector", lambda e: e.scalar_tensor_tensor(
            out=wxpT_l[:], in0=wxpT_h[:], scalar=-1.0, in1=wxf[:],
            op0=ALU.mult, op1=ALU.add))
        P.inc("vector", "vector",
              lambda e: e.tensor_copy(out=wdtT_h[:], in_=wdf[0:DTR, :]))
        v_wsplit = P.inc("vector", "vector", lambda e: e.scalar_tensor_tensor(
            out=wdtT_l[:], in0=wdtT_h[:], scalar=-1.0, in1=wdf[0:DTR, :],
            op0=ALU.mult, op1=ALU.add))
        v_f_wsplit = vfence(v_wsplit)

        P.wait("scalar", "sync", v_alog)
        v_ax = P.inc("scalar", "scalar",
                     lambda e: e.activation(out=alog_t[:], in_=alog_t[:], func=AF.Exp))
        P.wait("gpsimd", "scalar", v_ax)
        P.inc("gpsimd", "gpsimd",
              lambda e: e.tensor_copy(out=aneg_t[:], in_=alog_t[:]), 1)
        v_an = P.inc("gpsimd", "gpsimd",
                     lambda e: e.tensor_scalar_mul(aneg_t[:], aneg_t[:], -1.0), 1)
        P.wait("sync", "gpsimd", v_an)
        v_ab = sload(d_abounce, aneg_t[:])
        P.wait("sync", "sync", v_ab)
        v_abc = sload(abc[:], d_abounce.broadcast_to([128, NST]))

        v_st0 = P.inc("vector", "vector", lambda e: e.memset(state[:], 0.0))

        q_prev_use = [0]

        def quant_fp4(src_ap, dst_ap, F, scratch_f32, scratch_i32, pc=128,
                      src_ready=None):
            nb = F // 16
            m = qsml["m"][0:pc, 0:nb]
            sh = qsml["sh"][0:pc, 0:nb]
            rm = qsml["rm"][0:pc, 0:nb]
            rm2 = qsml["rm2"][0:pc, 0:nb]
            asrc, aw, t1, t2 = [sc[0:pc, 0:F] for sc in scratch_f32]
            src3 = src_ap.rearrange("p (b k) -> p b k", k=16)
            if src_ready is not None:
                P.wait("vector", src_ready[0], src_ready[1])
                P.wait("scalar", src_ready[0], src_ready[1])
            v_m = P.inc("vector", "vector", lambda e: e.tensor_reduce(
                out=m, in_=src3, axis=AX.X, op=ALU.max, apply_absolute_value=True))
            v_as = P.inc("scalar", "scalar",
                         lambda e: e.activation(out=asrc, in_=src_ap, func=AF.Abs))
            v_sg = P.inc("scalar", "scalar",
                         lambda e: e.activation(out=aw, in_=src_ap, func=AF.Sign))
            P.wait("gpsimd", "vector", v_m)
            P.wait("gpsimd", "scalar", v_sg)
            P.wait("gpsimd", "vector", q_prev_use[0])
            v_sh = P.inc("gpsimd", "gpsimd", lambda e: e.tensor_scalar(
                out=sh, in0=m, scalar1=1.0 / 12.0, scalar2=5e-13,
                op0=ALU.mult, op1=ALU.max), 1)
            P.wait("vector", "gpsimd", v_sh)
            v_rm = P.inc("vector", "vector", lambda e: e.reciprocal(out=rm, in_=sh))
            P.wait("scalar", "vector", v_rm)
            v_rm2 = P.inc("scalar", "scalar",
                          lambda e: e.copy(rm2, rm))
            P.wait("vector", "scalar", v_rm2)
            P.inc("vector", "vector", lambda e: e.tensor_tensor(
                out=t1.rearrange("p (b k) -> p b k", k=16),
                in0=asrc.rearrange("p (b k) -> p b k", k=16),
                in1=rm2.unsqueeze(-1).broadcast_to([pc, nb, 16]), op=ALU.mult))
            P.inc("vector", "vector", lambda e: e.tensor_scalar(
                out=asrc, in0=t1, scalar1=0.5, scalar2=1.0,
                op0=ALU.is_gt, op1=ALU.mult))
            for mj in (1.5, 2.5, 3.5):
                P.inc("vector", "vector", lambda e, mj=mj: e.scalar_tensor_tensor(
                    out=asrc, in0=t1, scalar=mj, in1=asrc,
                    op0=ALU.is_gt, op1=ALU.add))
            P.inc("vector", "vector", lambda e: e.tensor_scalar(
                out=t2, in0=t1, scalar1=10.0, scalar2=2.0,
                op0=ALU.is_gt, op1=ALU.mult))
            for mj in (5.0, 7.0):
                P.inc("vector", "vector", lambda e, mj=mj: e.scalar_tensor_tensor(
                    out=t2, in0=t1, scalar=mj, in1=t2,
                    op0=ALU.is_gt, op1=ALU.add))
            P.inc("vector", "vector", lambda e: e.scalar_tensor_tensor(
                out=asrc, in0=t2, scalar=2.0, in1=asrc,
                op0=ALU.mult, op1=ALU.add))
            P.wait("vector", "scalar", v_sg)
            P.inc("vector", "vector", lambda e: e.tensor_tensor(
                out=t2.rearrange("p (b k) -> p b k", k=16),
                in0=aw.rearrange("p (b k) -> p b k", k=16),
                in1=sh.unsqueeze(-1).broadcast_to([pc, nb, 16]), op=ALU.mult))
            v_q = P.inc("vector", "vector",
                        lambda e: e.tensor_mul(dst_ap, asrc, t2))
            q_prev_use[0] = v_q
            return v_q

        ln_prev_use = [0]

        def ln_tail(r_ap, out_ap, v_r):
            v_sr = P.inc("vector", "vector", lambda e: e.tensor_reduce(
                out=ln["sr"][:], in_=r_ap, axis=AX.X, op=ALU.add))
            P.wait("scalar", "vector", v_r)
            v_ssq = P.inc("scalar", "scalar", lambda e: e.activation(
                out=junk_sb[:], in_=r_ap, func=AF.Square, accum_out=ln["ssq"][:]))
            P.wait("gpsimd", "vector", v_sr)
            P.wait("gpsimd", "scalar", v_ssq)
            P.wait("gpsimd", "vector", ln_prev_use[0])
            P.inc("gpsimd", "gpsimd", lambda e: e.tensor_scalar(
                out=ln["mu"][:], in0=ln["sr"][:], scalar1=1.0 / D, scalar2=None,
                op0=ALU.mult), 1)
            P.inc("gpsimd", "gpsimd", lambda e: e.tensor_scalar(
                out=ln["m2e"][:], in0=ln["ssq"][:], scalar1=1.0 / D,
                scalar2=LN_EPS, op0=ALU.mult, op1=ALU.add), 1)
            P.inc("gpsimd", "gpsimd", lambda e: e.tensor_mul(
                ln["mu2"][:], ln["mu"][:], ln["mu"][:]), 1)
            v_ve = P.inc("gpsimd", "gpsimd", lambda e: e.tensor_sub(
                ln["ve"][:], ln["m2e"][:], ln["mu2"][:]), 1)
            P.wait("vector", "gpsimd", v_ve)
            v_iv = P.inc("vector", "vector", lambda e: e.reciprocal(
                out=ln["iv2"][:], in_=ln["ve"][:]))
            P.wait("scalar", "vector", v_iv)
            v_rstd = P.inc("scalar", "scalar", lambda e: e.activation(
                out=ln["rstd"][:], in_=ln["iv2"][:], func=AF.Sqrt))
            P.wait("vector", "scalar", v_rstd)
            v_out = P.inc("vector", "vector", lambda e: e.tensor_scalar(
                out=out_ap, in0=r_ap, scalar1=ln["mu"][:], scalar2=ln["rstd"][:],
                op0=ALU.subtract, op1=ALU.mult))
            ln_prev_use[0] = v_out
            return v_out, v_ssq

        wq_scr_f = [A.view(o, F32, [128, FI]) for o in q_big_offs]
        wq_ti = None
        wq_in_v = A.view(o_winT, F32, [128, 3, D])
        wq_in_d = A.view(o_winT, F32, [128, FI])
        wq_out_v = A.view(o_uc, BF16, [128, 3, D])
        wq_out_d = A.view(o_uc, BF16, [128, FI])

        wq_plan = [(dwg_s, d_wgT_pre, ISL, D, wq_in_v, wq_out_v),
                   (dwu_s, d_wuT_pre, ISL, D, wq_in_v, wq_out_v),
                   (dwd_s, d_wdT_pre, DSL, FI, wq_in_d, wq_out_d)]
        v_wq_store = 0
        wq_ev = [A.view(o_sz + 8192 + 256 * i, BF16, [128, 128]) for i in range(2)]
        P.wait("tensor", "gpsimd", v_ident)
        for (dsrc, ddstT, R, F, vin, vout) in wq_plan:
            nt = cdiv(R, 128)
            for rt in range(nt):
                p0, p1 = rt * 128, min((rt + 1) * 128, R)
                pc = p1 - p0
                vv = sload(vin[0:pc, rt, :] if nt > 1 else vin[0:pc, :],
                           dsrc[p0:p1, :])
                P.wait("vector", "sync", vv)
                v_q = quant_fp4(vin[0:pc, rt, :] if nt > 1 else vin[0:pc, :],
                                vout[0:pc, rt, :] if nt > 1 else vout[0:pc, :],
                                F, wq_scr_f, wq_ti, pc, src_ready=("sync", vv))
                v_f_wq = vfence(v_q)
                P.wait("tensor", "gpsimd", v_f_wq)
                for ft in range(F // 128):
                    bnk = 6 + (ft % 2)
                    bank_wait("tensor", bnk)
                    tin = (vout[0:pc, rt, ft * 128:(ft + 1) * 128] if nt > 1
                           else vout[0:pc, ft * 128:(ft + 1) * 128])
                    v_tr = P.inc("tensor", "tensor",
                                 lambda e, tin=tin, bnk=bnk, pc=pc: e.transpose(
                                     PB[bnk][:, 0:64].bitcast(BF16)[:, 0:pc],
                                     tin, ident[0:pc, 0:pc]))
                    P.wait("vector", "tensor", v_tr)
                    P.wait("vector", "sync", v_wq_store)
                    v_evt = P.inc("vector", "vector",
                                  lambda e, ft=ft, bnk=bnk, pc=pc: e.tensor_copy(
                                      out=wq_ev[ft % 2][:, 0:pc],
                                      in_=PB[bnk][:, 0:64].bitcast(BF16)[:, 0:pc]))
                    bank_free[bnk] = ("vector", v_evt)
                    v_f_ev = vfence(v_evt)
                    P.wait("sync", "gpsimd", v_f_ev)
                    P.wait("sync", "vector", v_evt)
                    v_wq_store = sload(ddstT[ft * 128:(ft + 1) * 128, p0:p1],
                                       wq_ev[ft % 2][:, 0:pc])
        v_wq_quant_done = P.semv["vector"]
        v_wq_sgn_done = P.semv["scalar"]

        v_bc_read_prev = 0
        v_xdo_prev = 0
        v_inproj_last_prev = 0
        v_uevac_read_prev = 0
        v_dA_last_prev = 0
        v_gate_prev = 0
        v_ucread_last_prev = 0
        v_ar_prev = 0
        v_rs_stores = 0
        v_wq_ag = None
        v_outproj_evac_prev = 0
        v_gy_last_prev = 0
        KT_G = ()
        PSLOT = {0: (0, 0), 1: (1, 1), 2: (2, 3), 3: (4, 5)}
        g_last = {2: [0, 0], 3: [0, 0]}

        for c in range(NCH):
            t0 = c * TC
            lo = max(t0 - 1, 0)
            hi = min(t0 + TC + 1, T)
            ofs = lo - (t0 - 1)

            if c == 0:
                P.wait("gpsimd", "vector", v_wq_quant_done)
                P.wait("gpsimd", "scalar", v_wq_sgn_done)
                P.wait("gpsimd", "sync", v_wq_store)
                P.wait("scalar", "vector", v_wq_quant_done)
                P.wait("scalar", "sync", v_wq_store)
            else:
                P.wait("gpsimd", "tensor", v_inproj_last_prev)
                P.wait("gpsimd", "vector", v_bc_read_prev)
            v_xc = 0
            for dt_ in range(8):
                v_xc = gload(xTc[:, dt_, ofs:ofs + (hi - lo)],
                             dx_T[dt_ * 128:(dt_ + 1) * 128, lo:hi])
            if c == 0:
                v_wl = gload(winT[:],
                             dwinT.rearrange("(a p) b -> p a b", p=128))
                v_wo = gload(woutT[:],
                             dwoutT.rearrange("(a p) b -> p a b", p=128))
                v_xc = max(v_xc, v_wl, v_wo)

            if c == 0:
                P.inc("vector", "vector", lambda e: e.memset(u_sb[:, :, 0:1], 0.0))
            if c == NCH - 1:
                P.inc("vector", "vector",
                      lambda e: e.memset(u_sb[:, :, TC + 1:TC + 2], 0.0))

            P.wait("tensor", "gpsimd", v_xc)
            if c > 0:
                P.wait("scalar", "vector", v_uevac_read_prev)
                P.wait("scalar", "vector", v_gate_prev)
                P.wait("scalar", "gpsimd", v_gy_last_prev)
            pbrot = [0]

            def inproj_tile(et, col0, width, dst_kind):
                nccw = cdiv(width, 512)
                for ccw in range(nccw):
                    w0 = ccw * 512
                    w1 = min(w0 + 512, width)
                    bi = pbrot[0] % 2
                    pbrot[0] += 1
                    bank_wait("tensor", bi)
                    for dt_ in range(8):
                        def mm(e, bi=bi, dt_=dt_, et=et, a=col0 + w0, b=col0 + w1,
                               st=(dt_ == 0), sp=(dt_ == 7)):
                            return e.matmul(PB[bi][:, 0:b - a],
                                            winT[:, dt_, et * 128:(et + 1) * 128],
                                            xTc[:, dt_, a:b], start=st, stop=sp)
                        if dt_ == 7:
                            v_mm = P.inc("tensor", "tensor", mm)
                        else:
                            P.emit("tensor", mm)
                    P.wait("scalar", "tensor", v_mm)
                    kind, kt = dst_kind
                    if kind == "u":
                        v_ev = P.inc("scalar", "scalar",
                                     lambda e, bi=bi, kt=kt, a=col0 + w0, b=col0 + w1:
                                     e.activation(out=u_sb[:, kt, a:b],
                                                  in_=PB[bi][:, 0:b - a],
                                                  func=AF.Copy))
                    else:
                        v_ev = P.inc("scalar", "scalar",
                                     lambda e, bi=bi, kt=kt, a=col0 + w0, b=col0 + w1:
                                     e.activation(out=sz_sb[:, kt, a - 1:b - 1],
                                                  in_=PB[bi][:, 0:b - a],
                                                  func=AF.Silu))
                    bank_free[bi] = ("scalar", v_ev)
                return P.semv["tensor"]

            v_u_evac_et = []
            for et in range(8):
                v_inproj_u = inproj_tile(et, ofs, hi - lo, ("u", et))
                v_u_evac_et.append(P.semv["scalar"])

            P.wait("vector", "sync", max(v_convw, v_convb, v_rs_stores))
            v_uc_silu_prev = [0]
            for kt in range(8):
                P.wait("vector", "scalar", v_u_evac_et[kt])
                P.wait("vector", "scalar", v_uc_silu_prev[0])
                P.inc("vector", "vector", lambda e, kt=kt: e.tensor_scalar_mul(
                    rse[0][0:128, 0:TC], u_sb[:, kt, 1:TC + 1], convw[:, kt, 1:2]))
                P.inc("vector", "vector", lambda e, kt=kt: e.scalar_tensor_tensor(
                    out=rse[0][0:128, 0:TC], in0=u_sb[:, kt, 0:TC],
                    scalar=convw[:, kt, 0:1], in1=rse[0][0:128, 0:TC],
                    op0=ALU.mult, op1=ALU.add))
                v_cv = P.inc("vector", "vector", lambda e, kt=kt: e.scalar_tensor_tensor(
                    out=rse[0][0:128, 0:TC], in0=u_sb[:, kt, 2:TC + 2],
                    scalar=convw[:, kt, 2:3], in1=rse[0][0:128, 0:TC],
                    op0=ALU.mult, op1=ALU.add))
                P.wait("scalar", "vector", v_cv)
                if c > 0 and kt == 0:
                    P.wait("scalar", "tensor", v_ucread_last_prev)
                v_uc_silu_prev[0] = P.inc("scalar", "scalar",
                    lambda e, kt=kt: e.activation(
                        out=uc_sb[:, kt, :], in_=rse[0][0:128, 0:TC], func=AF.Silu,
                        bias=convb[:, kt:kt + 1]))
            v_uc_done = P.semv["scalar"]

            P.wait("tensor", "scalar", v_uc_done)
            if c == 0:
                P.wait("tensor", "gpsimd", v_f_wsplit)
            P.wait("vector", "sync", v_xdo_prev)
            for ccw in range(TC // 512):
                bi = 2 + (ccw % 2)
                bank_wait("tensor", bi)
                for kt in range(8):
                    for si, wsb in enumerate((wxpT_h, wxpT_l)):
                        def mmx(e, bi=bi, kt=kt, ccw=ccw, wsb=wsb,
                                st=(kt == 0 and si == 0), sp=(kt == 7 and si == 1)):
                            return e.matmul(PB[bi][0:96, 0:512], wsb[:, kt, :],
                                            uc_sb[:, kt, ccw * 512:(ccw + 1) * 512],
                                            start=st, stop=sp)
                        if kt == 7 and si == 1:
                            v_mmx = P.inc("tensor", "tensor", mmx)
                        else:
                            P.emit("tensor", mmx)
                P.wait("vector", "tensor", v_mmx)
                v_xd = P.inc("vector", "vector", lambda e, bi=bi, ccw=ccw:
                             e.tensor_copy(out=xdbl[:, ccw * 512:(ccw + 1) * 512],
                                           in_=PB[bi][0:96, 0:512]))
                bank_free[bi] = ("vector", v_xd)
            v_f_xd = vfence(v_xd)
            P.wait("sync", "gpsimd", v_f_xd)
            P.wait("sync", "vector", v_xd)
            if c > 0:
                P.wait("sync", "cc", v_ar_prev)
            v_xdo = sload(d_xdbl_in, xdbl[:])
            v_xdo_prev = v_xdo
            P.wait("gpsimd", "sync", v_xdo)
            v_ar = P.inc("gpsimd", "cc", lambda e: e.collective_compute(
                "AllReduce", ALU.add,
                replica_groups=[[0, 1], [2, 3], [4, 5], [6, 7]],
                ins=[d_xdbl_in], outs=[d_xdbl_out]))
            v_ar_prev = v_ar

            for et in range(8, 16):
                v_inproj_z = inproj_tile(et, 1, TC, ("z", et - 8))
            v_inproj_last_prev = P.semv["tensor"]
            v_z_evac = P.semv["scalar"]

            P.wait("gpsimd", "cc", v_ar)
            v_dtp = gload(dtp[:], d_xdbl_out[0:DTR, :])
            P.wait("tensor", "gpsimd", v_dtp)
            v_dtadd_prev = [0]
            if c > 0:
                P.wait("vector", "scalar", v_dA_last_prev)
                P.wait("scalar", "vector", v_outproj_evac_prev)
                P.wait("scalar", "sync", v_rs_stores)
            for kt in range(8):
                for ccw in range(TC // 512):
                    bi = 4 + ((kt * 2 + ccw) % 2)
                    bank_wait("tensor", bi)
                    for si, wsb in enumerate((wdtT_h, wdtT_l)):
                        def mmd_(e, bi=bi, kt=kt, ccw=ccw, wsb=wsb,
                                 st=(si == 0), sp=(si == 1)):
                            return e.matmul(
                                PB[bi][:, 0:512],
                                wsb[:, kt * 128:(kt + 1) * 128],
                                dtp[:, ccw * 512:(ccw + 1) * 512],
                                start=st, stop=sp)
                        if si == 1:
                            v_mmd = P.inc("tensor", "tensor", mmd_)
                        else:
                            P.emit("tensor", mmd_)
                    P.wait("scalar", "tensor", v_mmd)
                    P.wait("scalar", "vector", v_dtadd_prev[0])
                    P.inc("scalar", "scalar", lambda e, bi=bi, kt=kt: e.activation(
                        out=sp_tmp[:, 0, :], in_=PB[bi][:, 0:512], func=AF.Abs,
                        bias=bdt[:, kt:kt + 1]))
                    P.inc("scalar", "scalar", lambda e: e.activation(
                        out=sp_tmp[:, 0, :], in_=sp_tmp[:, 0, :], func=AF.Exp,
                        scale=-1.0))
                    P.inc("scalar", "scalar", lambda e: e.activation(
                        out=sp_tmp[:, 0, :], in_=sp_tmp[:, 0, :], func=AF.Ln,
                        bias=1.0))
                    v_sp = P.inc("scalar", "scalar", lambda e, bi=bi, kt=kt:
                                 e.activation(out=sp_tmp[:, 1, :],
                                              in_=PB[bi][:, 0:512], func=AF.Relu,
                                              bias=bdt[:, kt:kt + 1]))
                    bank_free[bi] = ("scalar", v_sp)
                    P.wait("vector", "scalar", v_sp)
                    v_dtadd_prev[0] = P.inc("vector", "vector",
                          lambda e, kt=kt, ccw=ccw:
                          e.tensor_add(dt_sb[:, kt, ccw * 512:(ccw + 1) * 512],
                                       sp_tmp[:, 1, :], sp_tmp[:, 0, :]))
                if kt == 3:
                    v_dt_half0 = P.semv["vector"]
            v_dt_done = P.semv["vector"]
            v_dt_half = [v_dt_half0, v_dt_done]

            P.wait("scalar", "vector", v_st0 if c == 0 else v_gate_prev)
            v_strd = P.inc("scalar", "scalar", lambda e: e.copy(
                state_rd[:].rearrange("p a b -> p (a b)"),
                state[:].rearrange("p a b -> p (a b)")))
            P.wait("vector", "scalar", v_strd)

            P.wait("scalar", "sync", v_abc)
            v_scan_vec_prev = [0]
            v_bc_pair = [0, 0]
            v_bc_read = [0, 0]

            def emit_bc_load(k, first_of_chunk):
                ng, j = (k % 8) // 2, k % 2
                if first_of_chunk:
                    P.wait("gpsimd", "tensor", v_inproj_last_prev)
                if c > 0 or k >= 2:
                    P.wait("gpsimd", "vector",
                           v_bc_read[j] if k >= 2 else v_bc_read_prev)
                for i2 in range(2):
                    s = 2 * j + i2
                    src = bass.AP(d_xdbl_out.tensor, (DTR + 4 * ng + s) * TC,
                                  [[0, 128], [NST * TC, 2], [1, TC]])
                    v_bc_pair[j] = gload(
                        BCflat[:, s * 2 * TC:(s + 1) * 2 * TC], src)

            emit_bc_load(0, True)
            emit_bc_load(1, True)
            for ktg in range(2):
                kts = range(ktg * 4, ktg * 4 + 4)
                P.wait("scalar", "vector", v_dt_half[ktg])
                for ng in range(4):
                    for n4 in range(4):
                        n = 4 * ng + n4
                        if n4 % 2 == 0:
                            P.wait("vector", "gpsimd", v_bc_pair[n4 // 2])
                        P.wait("scalar", "vector", v_scan_vec_prev[0])
                        v_dA_half = []
                        for q in range(2):
                            ktp = ktg * 4 + q * 2
                            v_dA_half.append(P.inc("scalar", "scalar",
                                  lambda e, q=q, ktp=ktp, n=n: e.activation(
                                      out=dA[q][:].rearrange("p a b -> p (a b)"),
                                      in_=dt_sb[:, ktp:ktp + 2, :]
                                      .rearrange("p a b -> p (a b)"),
                                      func=AF.Exp, scale=abc[:, n:n + 1])))
                        for ki, kt in enumerate(kts):
                            if ki % 2 == 0:
                                P.wait("vector", "scalar", v_dA_half[ki // 2])
                            P.inc("vector", "vector",
                                  lambda e, kt=kt, n4=n4: e.tensor_tensor(
                                      out=bu[:], in0=uc_sb[:, kt, :],
                                      in1=BC[:, n4, 0, :], op=ALU.mult))
                            P.inc("vector", "vector",
                                  lambda e, kt=kt, ki=ki, n=n:
                                  e.tensor_tensor_scan(
                                      out=hbuf[:, ki, :],
                                      data0=dA[(kt % 4) // 2][:, kt % 2, :],
                                      data1=bu[:],
                                      initial=state_rd[:, kt, n:n + 1],
                                      op0=ALU.mult, op1=ALU.add))
                        v_scan_vec_prev[0] = P.semv["vector"]
                        gp_kts = () if n == NST - 1 else KT_G
                        for ki, kt in enumerate(kts):
                            slot = PSLOT[ki][n % 2]
                            if ki in KT_G:
                                P.wait("vector", "gpsimd", g_last[ki][n % 2])
                            P.inc("vector", "vector",
                                  lambda e, ki=ki, n4=n4, slot=slot:
                                  e.tensor_tensor(out=pb[slot][:],
                                                  in0=hbuf[:, ki, :],
                                                  in1=BC[:, n4, 1, :],
                                                  op=ALU.mult))
                        P.inc("vector", "vector", lambda e, ktg=ktg, n=n:
                              e.tensor_copy(
                                  out=state[:, ktg * 4:ktg * 4 + 4, n:n + 1],
                                  in_=hbuf[:, :, TC - 1:TC]))
                        for ki in range(4):
                            if ki in gp_kts:
                                continue
                            slot = PSLOT[ki][n % 2]
                            if ki in KT_G:
                                P.wait("vector", "gpsimd", g_last[ki][(n + 1) % 2])
                            if n == 0:
                                P.inc("vector", "vector",
                                      lambda e, ki=ki, slot=slot: e.tensor_copy(
                                          out=yacc[:, 0, ki, :], in_=pb[slot][:]))
                            else:
                                P.inc("vector", "vector",
                                      lambda e, ki=ki, n=n, slot=slot:
                                      e.tensor_tensor(
                                          out=yacc[:, n % 2, ki, :],
                                          in0=yacc[:, (n + 1) % 2, ki, :],
                                          in1=pb[slot][:], op=ALU.add))
                        v_pm_end = P.semv["vector"]
                        if gp_kts:
                            P.wait("gpsimd", "vector", v_pm_end)
                            P.emit("gpsimd", lambda e: e.engine_nop())
                        for ki in gp_kts:
                            slot = PSLOT[ki][n % 2]
                            if n == 0:
                                g_last[ki][0] = P.inc(
                                    "gpsimd", "gpsimd",
                                    lambda e, ki=ki, slot=slot: e.tensor_copy(
                                        out=yacc[:, 0, ki, :], in_=pb[slot][:]), 1)
                            else:
                                g_last[ki][n % 2] = P.inc(
                                    "gpsimd", "gpsimd",
                                    lambda e, ki=ki, n=n, slot=slot:
                                    e.tensor_tensor(
                                        out=yacc[:, n % 2, ki, :],
                                        in0=yacc[:, (n + 1) % 2, ki, :],
                                        in1=pb[slot][:], op=ALU.add), 1)
                        if n4 % 2 == 1:
                            jj = n4 // 2
                            v_bc_read[jj] = P.semv["vector"]
                            kk = ktg * 8 + ng * 2 + jj
                            if kk + 2 < 16:
                                emit_bc_load(kk + 2, False)
                    v_bc_read_prev = P.semv["vector"]
                v_dA_ktg_end = P.semv["scalar"]
                v_gy_end = P.semv["gpsimd"]
                P.wait("vector", "gpsimd", v_gy_end)
                P.wait("vector", "scalar", v_dA_ktg_end)
                P.wait("vector", "tensor", v_ucread_last_prev)
                P.wait("vector", "scalar", v_z_evac)
                P.wait("vector", "sync", v_dvec)
                for ki, kt in enumerate(kts):
                    P.inc("vector", "vector", lambda e, kt=kt, ki=ki:
                          e.scalar_tensor_tensor(
                              out=rse[0][0:128, 0:TC], in0=uc_sb[:, kt, :],
                              scalar=dvec[:, kt:kt + 1],
                              in1=yacc[:, 1, ki, :], op0=ALU.mult, op1=ALU.add))
                    v_ymf = P.inc("vector", "vector",
                                  lambda e, kt=kt: e.tensor_tensor(
                                      out=rse[1][0:128, 0:TC],
                                      in0=rse[0][0:128, 0:TC],
                                      in1=sz_sb[:, kt, :], op=ALU.mult))
                    P.wait("scalar", "vector", v_ymf)
                    v_ymh_c = P.inc("scalar", "scalar", lambda e, kt=kt: e.copy(
                        ymh[:, kt, :], rse[1][0:128, 0:TC]))
                    P.wait("vector", "scalar", v_ymh_c)
                    P.inc("vector", "vector", lambda e, kt=kt:
                          e.scalar_tensor_tensor(
                              out=yml[:, kt, :], in0=ymh[:, kt, :], scalar=-1.0,
                              in1=rse[1][0:128, 0:TC],
                              op0=ALU.mult, op1=ALU.add))
            v_gate_prev = P.semv["vector"]
            v_dA_last_prev = P.semv["scalar"]
            v_gy_last_prev = P.semv["gpsimd"]

            P.wait("gpsimd", "vector", v_gate_prev)
            P.wait("gpsimd", "scalar", P.semv["scalar"])
            v_echo_g = P.inc("gpsimd", "gpsimd", lambda e: e.engine_nop(), 1)
            P.wait("tensor", "gpsimd", v_echo_g)
            P.wait("tensor", "vector", v_gate_prev)
            P.wait("tensor", "scalar", P.semv["scalar"])
            for ts in range(8):
                ob = (0 + 2 * (ts % 2), 1 + 2 * (ts % 2))
                for bnk in ob:
                    bank_wait("tensor", bnk)
                for kt in range(8):
                    for si, ysb in enumerate((ymh, yml)):
                        for half in range(2):
                            def mmo(e, kt=kt, ts=ts, half=half, bnk=ob[half],
                                    ysb=ysb, st=(kt == 0 and si == 0),
                                    sp=(kt == 7 and si == 1)):
                                return e.matmul(
                                    PB[bnk][:, 0:512],
                                    ysb[:, kt, ts * 128:(ts + 1) * 128],
                                    woutT[:, kt, half * 512:(half + 1) * 512],
                                    start=st, stop=sp)
                            if kt == 7 and si == 1 and half == 1:
                                v_mmo = P.inc("tensor", "tensor", mmo)
                            else:
                                P.emit("tensor", mmo)
                P.wait("scalar", "tensor", v_mmo)
                reb = ts % 2
                v_ev = None
                for half in range(2):
                    v_ev = P.inc("scalar", "scalar",
                                 lambda e, reb=reb, half=half, bnk=ob[half]:
                                 e.copy(rse[reb][0:128, half * 512:(half + 1) * 512],
                                        PB[bnk][:, 0:512]))
                for half in range(2):
                    bank_free[ob[half]] = ("scalar", v_ev)
                P.wait("gpsimd", "scalar", v_ev)
                v_f_rs = P.inc("gpsimd", "gpsimd", lambda e: e.engine_nop(), 1)
                P.wait("sync", "gpsimd", v_f_rs)
                P.wait("sync", "scalar", v_ev)
                v_rs_stores = sload(d_rs_in[t0 + ts * 128:t0 + (ts + 1) * 128, :],
                                    rse[reb][0:128, 0:D])
            v_ucread_last_prev = P.semv["tensor"]
            v_outproj_evac_prev = P.semv["vector"]

            if debug == 2:
                P.wait("gpsimd", "vector", v_gate_prev)
                if c == NCH - 1:
                    P.inc("gpsimd", "gpsimd", lambda e: e.dma_start(
                        out=ddbg2["ym3"].rearrange("p (a b) -> p a b", a=8),
                        in_=ymh[:]), 16)
                if c == 0:
                    P.inc("gpsimd", "gpsimd", lambda e: e.dma_start(
                        out=ddbg2["ymine"].rearrange("p (a b) -> p a b", a=8),
                        in_=ymh[:]), 16)
                    P.inc("gpsimd", "gpsimd", lambda e: e.dma_start(
                        out=ddbg2["dt"].rearrange("p (a b) -> p a b", a=8),
                        in_=dt_sb[:]), 16)
                P.inc("gpsimd", "gpsimd", lambda e, c=c: e.dma_start(
                    out=ddbg2["state"].rearrange("p (c a b) -> p c a b", c=NCH, a=8)[:, c],
                    in_=state[:]), 16)
            if debug is True and c == 0:
                P.wait("gpsimd", "vector", P.semv["vector"])
                P.wait("gpsimd", "scalar", P.semv["scalar"])
                for key, src_ap in (("u", u_sb[:, :, 1:TC + 1]), ("uc", uc_sb[:]),
                                    ("sz", sz_sb[:]), ("dt", dt_sb[:]),
                                    ("ymine", ymh[:])):
                    P.inc("gpsimd", "gpsimd",
                          lambda e, d=ddbg[key], s=src_ap: e.dma_start(
                              out=d.rearrange("p (a b) -> p a b", a=8), in_=s), 16)
                P.wait("sync", "vector", P.semv["vector"])
                sload(ddbg["xdbl"], xdbl[:])
            if c == 0:
                v_wq_ag = []
                for a, b in ((d_wgT_pre, d_wgT_blk), (d_wuT_pre, d_wuT_blk),
                             (d_wdT_pre, d_wdT_blk)):
                    P.wait("gpsimd", "sync", v_wq_store)
                    v_wq_ag.append(P.inc("gpsimd", "cc",
                                         lambda e, a=a, b=b: e.collective_compute(
                                             "AllGather", ALU.bypass,
                                             replica_groups=[[0, 1, 2, 3, 4, 5, 6, 7]],
                                             ins=[a], outs=[b])))

        P.wait("gpsimd", "sync", v_rs_stores)
        v_rs = P.inc("gpsimd", "cc", lambda e: e.collective_compute(
            "ReduceScatter", ALU.add,
            replica_groups=[[0, 1], [2, 3], [4, 5], [6, 7]],
            ins=[d_rs_in], outs=[d_rs_out]))

        if debug == 2:
            P.wait("gpsimd", "sync", v_rs_stores)
            P.inc("gpsimd", "gpsimd",
                  lambda e: e.dma_start(out=ddbg2["rsin"], in_=d_rs_in), 16)
            P.wait("gpsimd", "cc", v_rs)
            P.inc("gpsimd", "gpsimd",
                  lambda e: e.dma_start(out=ddbg2["rsout"], in_=d_rs_out), 16)
        P.wait("sync", "cc", v_wq_ag[-1])
        P.wait("sync", "tensor", P.semv["tensor"])
        P.wait("sync", "vector", P.semv["vector"])
        P.wait("sync", "scalar", P.semv["scalar"])
        barrier_s = P.inc("sync", "sync", lambda e: e.dma_start(
            out=d_abounce, in_=alog_t[:]), 16)
        for eng in ("tensor", "vector", "scalar", "gpsimd"):
            P.wait(eng, "sync", barrier_s)

        v_wd_l = 0
        for it in range(22):
            wdsrc = bass.AP(d_wdT_blk.tensor, it * 128 * DSL,
                            [[DSL, 128], [FI * DSL, 8], [1, DSL]])
            v_wd_l = sload(wdqT[:, it, :], wdsrc)

        P.wait("sync", "cc", v_rs)
        b_qf = [A.view(o, F32, [128, D]) for o in b_q]
        b_qi = A.view(b_ti, F32, [128, D])
        v_b2_r_prev = [0]
        v_b2_x2store_prev = [0]
        v_b2_tr_prev = [0]
        v_b2_ssq_prev = [0]
        for tt in range(16):
            if tt >= 1:
                P.wait("sync", "vector", v_b2_r_prev[0])
            v_rl = sload(rsld[tt % 2][:], d_rs_out[tt * 128:(tt + 1) * 128, :])
            v_xr = sload(x2l[tt % 2][:], dx_res[tt * 128:(tt + 1) * 128, :])
            P.wait("vector", "sync", max(v_rl, v_xr))
            P.wait("vector", "scalar", v_b2_ssq_prev[0])
            v_r = P.inc("vector", "vector", lambda e, tt=tt: e.scalar_tensor_tensor(
                out=r_sb[:], in0=x2l[tt % 2][:], scalar=2.0,
                in1=rsld[tt % 2][:], op0=ALU.mult, op1=ALU.add))
            v_b2_r_prev[0] = v_r
            P.wait("vector", "sync", v_b2_x2store_prev[0])
            v_x2, v_ssq = ln_tail(r_sb[:], x2_sb[:], v_r)
            v_b2_ssq_prev[0] = v_ssq
            v_f_x2 = vfence(v_x2)
            P.wait("sync", "gpsimd", v_f_x2)
            P.wait("sync", "vector", v_x2)
            v_b2_x2store_prev[0] = sload(d_x2[tt * 128:(tt + 1) * 128, :], x2_sb[:])
            if debug is True and tt == 0:
                P.wait("sync", "vector", P.semv["vector"])
                P.wait("sync", "scalar", P.semv["scalar"])
                sload(ddbg["rs"], rsld[0][:])
                sload(ddbg["x2"], x2_sb[:])
                sload(ddbg["r"], r_sb[:])
                for li, lk in enumerate(("sr", "ssq", "mu", "m2e", "mu2", "ve",
                                         "iv2", "rstd")):
                    P.inc("sync", "sync",
                          lambda e, li=li, lk=lk: e.dma_start(
                              out=ddbg["ln"][:, li:li + 1], in_=ln[lk][:],
                              allow_slow_non_contiguous=True), 16)
                P.wait("vector", "sync", P.semv["sync"])
                P.wait("scalar", "sync", P.semv["sync"])
            P.wait("vector", "tensor", v_b2_tr_prev[0])
            v_x2q = quant_fp4(x2_sb[:], x2q_sb[:], D, b_qf, b_qi, 128,
                              src_ready=("vector", v_x2))
            if debug is True and tt == 0:
                P.wait("gpsimd", "vector", v_x2q)
                P.inc("gpsimd", "gpsimd", lambda e: e.dma_start(
                    out=ddbg["x2q"], in_=x2q_sb[:]), 16)
            P.wait("tensor", "vector", v_x2q)
            for dt_ in range(8):
                bnk = 6 + (dt_ % 2)
                bank_wait("tensor", bnk)
                v_tr = P.inc("tensor", "tensor", lambda e, dt_=dt_, bnk=bnk:
                             e.transpose(PB[bnk][:, 0:64].bitcast(BF16),
                                         x2q_sb[:, dt_ * 128:(dt_ + 1) * 128],
                                         ident[:]))
                P.wait("vector", "tensor", v_tr)
                v_evt = P.inc("vector", "vector",
                              lambda e, dt_=dt_, tt=tt, bnk=bnk: e.tensor_copy(
                                  out=x2qT[:, dt_, tt * 128:(tt + 1) * 128],
                                  in_=PB[bnk][:, 0:64].bitcast(BF16)))
                bank_free[bnk] = ("vector", v_evt)
            v_b2_tr_prev[0] = P.semv["tensor"]
        v_x2qT_done = P.semv["vector"]

        c_qf = [A.view(o, F32, [128, 512]) for o in c_q]
        c_qi = A.view(c_hqld[0], F32, [128, 512])
        P.wait("tensor", "vector", v_x2qT_done)
        ics = [(k * ISL, (k + 1) * ISL) for k in range(8)]
        v_hq_store = 0
        v_ffn_mm_prev = [0]
        v_ffn_h_prev = [0]
        for ici, (i0, i1) in enumerate(ics):
            iw = i1 - i0
            cb = ici % 2
            if ici == 0:
                P.wait("sync", "vector", v_x2qT_done)
            else:
                P.wait("sync", "tensor", v_ffn_mm_prev[0])
            v_wg_l = 0
            for dt_ in range(8):
                r0 = ici * D + dt_ * 128
                v_wg_l = sload(wgc[cb][:, dt_, 0:iw], d_wgT_blk[r0:r0 + 128, :])
                v_wg_l = max(v_wg_l, sload(wuc[cb][:, dt_, 0:iw],
                                           d_wuT_blk[r0:r0 + 128, :]))
            P.wait("tensor", "sync", v_wg_l)
            for tt in range(16):
                bank_wait("tensor", 2)
                bank_wait("tensor", 3)
                for dt_ in range(8):
                    def mmg(e, tt=tt, dt_=dt_, cb=cb, iw=iw, st=(dt_ == 0),
                            sp=(dt_ == 7)):
                        return e.matmul(PB[2][:, 0:iw],
                                        x2qT[:, dt_, tt * 128:(tt + 1) * 128],
                                        wgc[cb][:, dt_, 0:iw], start=st, stop=sp)
                    if dt_ == 7:
                        v_mmg = P.inc("tensor", "tensor", mmg)
                    else:
                        P.emit("tensor", mmg)
                for dt_ in range(8):
                    def mmu(e, tt=tt, dt_=dt_, cb=cb, iw=iw, st=(dt_ == 0),
                            sp=(dt_ == 7)):
                        return e.matmul(PB[3][:, 0:iw],
                                        x2qT[:, dt_, tt * 128:(tt + 1) * 128],
                                        wuc[cb][:, dt_, 0:iw], start=st, stop=sp)
                    if dt_ == 7:
                        v_mmu = P.inc("tensor", "tensor", mmu)
                    else:
                        P.emit("tensor", mmu)
                P.wait("scalar", "tensor", v_mmg)
                P.wait("scalar", "vector", v_ffn_h_prev[0])
                v_ffn_mm_prev[0] = v_mmu
                v_sg = P.inc("scalar", "scalar", lambda e, iw=iw: e.activation(
                    out=sg_sb[:, 0:iw], in_=PB[2][:, 0:iw], func=AF.Silu))
                bank_free[2] = ("scalar", v_sg)
                P.wait("vector", "scalar", v_sg)
                P.wait("vector", "tensor", v_mmu)
                v_h = P.inc("vector", "vector", lambda e, iw=iw: e.tensor_tensor(
                    out=h_sb[:, 0:iw], in0=sg_sb[:, 0:iw], in1=PB[3][:, 0:iw],
                    op=ALU.mult))
                bank_free[3] = ("vector", v_h)
                v_ffn_h_prev[0] = v_h
                if debug is True and ici == 0 and tt == 0:
                    P.wait("gpsimd", "scalar", v_sg)
                    P.inc("gpsimd", "gpsimd", lambda e: e.dma_start(
                        out=ddbg["gate"], in_=sg_sb[:, 0:512]), 16)
                P.wait("vector", "sync", v_hq_store)
                v_hq = quant_fp4(h_sb[:, 0:iw], hq_sb[:, 0:iw], iw, c_qf, c_qi,
                                 128, src_ready=("vector", v_h))
                v_f_hq = vfence(v_hq)
                P.wait("sync", "gpsimd", v_f_hq)
                P.wait("sync", "vector", v_hq)
                v_hq_store = sload(d_hq[tt * 128:(tt + 1) * 128, i0:i1],
                                   hq_sb[:, 0:iw])
                if debug is True and tt == 0:
                    P.wait("gpsimd", "vector", v_hq)
                    P.inc("gpsimd", "gpsimd",
                          lambda e, i0=i0, i1=i1: e.dma_start(
                              out=ddbg["hq"][:, i0:i1], in_=hq_sb[:, 0:i1 - i0]), 16)

        P.wait("sync", "sync", v_hq_store)
        v_mmd_prev = [0]
        v_tr_prev = [0]
        v_r2_prev = [0]
        v_b4_ssq_prev = [0]
        for tt in range(16):
            if tt >= 1:
                P.wait("sync", "tensor", v_tr_prev[0])
                P.wait("sync", "vector", v_r2_prev[0])
            v_hl = sload(hqld[tt % 2][:], d_hq[tt * 128:(tt + 1) * 128, :])
            v_x2r = sload(x2l[tt % 2][:], d_x2[tt * 128:(tt + 1) * 128, :])
            P.wait("tensor", "sync", v_hl)
            P.wait("vector", "tensor", v_mmd_prev[0])
            for it in range(22):
                bnk = 6 + (it % 2)
                bank_wait("tensor", bnk)
                v_tr = P.inc("tensor", "tensor", lambda e, tt=tt, it=it, bnk=bnk:
                             e.transpose(PB[bnk][:, 0:64].bitcast(BF16),
                                         hqld[tt % 2][:, it * 128:(it + 1) * 128],
                                         ident[:]))
                P.wait("vector", "tensor", v_tr)
                v_evt = P.inc("vector", "vector", lambda e, it=it, bnk=bnk:
                              e.tensor_copy(out=hqT[:, it, :], in_=PB[bnk][:, 0:64].bitcast(BF16)))
                bank_free[bnk] = ("vector", v_evt)
            v_hqT_done = P.semv["vector"]
            v_tr_prev[0] = P.semv["tensor"]
            P.wait("gpsimd", "vector", v_hqT_done)
            v_echo_h = P.inc("gpsimd", "gpsimd", lambda e: e.engine_nop(), 1)
            P.wait("tensor", "gpsimd", v_echo_h)
            P.wait("tensor", "vector", v_hqT_done)
            if tt == 0:
                P.wait("tensor", "sync", v_wd_l)
            db = (0 + 2 * (tt % 2), 1 + 2 * (tt % 2))
            for bnk in db:
                bank_wait("tensor", bnk)
            for it in range(22):
                for half in range(2):
                    def mmd(e, it=it, half=half, bnk=db[half], st=(it == 0),
                            sp=(it == 21)):
                        return e.matmul(PB[bnk][:, 0:512], hqT[:, it, :],
                                        wdqT[:, it, half * 512:(half + 1) * 512],
                                        start=st, stop=sp)
                    if it == 21 and half == 1:
                        v_mmd = P.inc("tensor", "tensor", mmd)
                    else:
                        P.emit("tensor", mmd)
            P.wait("vector", "tensor", v_mmd)
            P.wait("vector", "sync", v_x2r)
            P.wait("vector", "scalar", v_b4_ssq_prev[0])
            v_r2 = None
            for half in range(2):
                v_r2 = P.inc("vector", "vector",
                             lambda e, tt=tt, half=half, bnk=db[half]:
                             e.tensor_tensor(
                                 out=r_sb[:, half * 512:(half + 1) * 512],
                                 in0=PB[bnk][:, 0:512],
                                 in1=x2l[tt % 2][:, half * 512:(half + 1) * 512],
                                 op=ALU.add))
            for half in range(2):
                bank_free[db[half]] = ("vector", v_r2)
            v_mmd_prev[0] = v_mmd
            v_r2_prev[0] = v_r2
            if tt >= 2:
                P.wait("vector", "out", 16 * (tt - 1))
            v_o, v_ssq4 = ln_tail(r_sb[:], out_sb[tt % 2][:], v_r2)
            v_b4_ssq_prev[0] = v_ssq4
            v_f_o = vfence(v_o)
            P.wait("sync", "gpsimd", v_f_o)
            P.wait("sync", "vector", v_o)
            P.inc("sync", "out", lambda e, tt=tt: e.dma_start(
                out=dout[tt * 128:(tt + 1) * 128, :], in_=out_sb[tt % 2][:]), 16)

        P.wait("sync", "out", P.semv["out"])

        P.run(block)
    return nc


_NC_CACHE = [None]


def make_in_maps(x, W_in, conv_w, conv_b, W_xp, W_dt, b_dt, A_log, Dv, W_out,
                 wg, wu, wd):
    ident = np.eye(128, dtype=np.float32)
    in_maps = []
    for c in range(8):
        s, j = c // 2, c % 2
        ch = slice(j * HC, (j + 1) * HC)
        in_maps.append({
            "xT": np.ascontiguousarray(x[s].T),
            "xres": np.ascontiguousarray(x[s, j * TH:(j + 1) * TH, :]),
            "winT": np.ascontiguousarray(np.concatenate(
                [W_in[ch].T, W_in[H + j * HC:H + (j + 1) * HC].T], axis=1)),
            "convw": np.ascontiguousarray(conv_w[ch, 0, :]),
            "convb": np.ascontiguousarray(conv_b[ch].reshape(8, 128).T),
            "wxpT": np.ascontiguousarray(W_xp[:, ch].T),
            "wdtT": np.ascontiguousarray(W_dt[ch].T),
            "bdt": np.ascontiguousarray(b_dt[ch].reshape(8, 128).T),
            "alog": np.ascontiguousarray(A_log[None, :]),
            "dvec": np.ascontiguousarray(Dv[ch].reshape(8, 128).T),
            "woutT": np.ascontiguousarray(W_out[:, ch].T),
            "wg_s": np.ascontiguousarray(wg[c * ISL:(c + 1) * ISL]),
            "wu_s": np.ascontiguousarray(wu[c * ISL:(c + 1) * ISL]),
            "wd_s": np.ascontiguousarray(wd[c * DSL:(c + 1) * DSL]),
            "identin": ident,
        })
    return in_maps


def kernel(L0_attn_x, L0_attn_W_in, L0_attn_conv_w, L0_attn_conv_b,
           L0_attn_W_xproj, L0_attn_W_dt, L0_attn_b_dt, L0_attn_A_log,
           L0_attn_D, L0_attn_W_out, L0_ffn_w_gate, L0_ffn_w_up, L0_ffn_w_down):
    in_maps = make_in_maps(
        np.asarray(L0_attn_x, np.float32), np.asarray(L0_attn_W_in, np.float32),
        np.asarray(L0_attn_conv_w, np.float32),
        np.asarray(L0_attn_conv_b, np.float32),
        np.asarray(L0_attn_W_xproj, np.float32),
        np.asarray(L0_attn_W_dt, np.float32),
        np.asarray(L0_attn_b_dt, np.float32),
        np.asarray(L0_attn_A_log, np.float32),
        np.asarray(L0_attn_D, np.float32), np.asarray(L0_attn_W_out, np.float32),
        np.asarray(L0_ffn_w_gate, np.float32),
        np.asarray(L0_ffn_w_up, np.float32),
        np.asarray(L0_ffn_w_down, np.float32))
    if _NC_CACHE[0] is None:
        _NC_CACHE[0] = build()
    nc = _NC_CACHE[0]
    res = run_bass_kernel_spmd(nc, in_maps, core_ids=list(range(8)))
    out = np.empty((B, T, D), np.float32)
    for c in range(8):
        s, j = c // 2, c % 2
        out[s, j * TH:(j + 1) * TH, :] = res.results[c]["out"]
    return out


if __name__ == "__main__":
    print("building ...")
    nc = build()
    n = sum(len(b.instructions) for f in nc.m.functions for b in f.blocks)
    print("built, instructions:", n)


# version 20 (final)
# speedup vs baseline: 1.0713x; 1.0023x over previous
import contextlib
import numpy as np
import concourse.bass as bass
import concourse.mybir as mybir
from concourse.bass_utils import run_bass_kernel_spmd

F32 = mybir.dt.float32
BF16 = mybir.dt.bfloat16
F16 = mybir.dt.float16
I32 = mybir.dt.int32
U8 = mybir.dt.uint8
AF = mybir.ActivationFunctionType
ALU = mybir.AluOpType
AX = mybir.AxisListType

B, T, D = 4, 4096, 1024
H, HC, NST, DTR, FI = 2048, 1024, 16, 64, 2816
TC = 1024
NCH = T // TC
TH = 2048
LN_EPS = 1e-5
ISL = FI // 8
DSL = D // 8

ENGS = ("sync", "scalar", "vector", "tensor", "gpsimd")


class Prog:
    def __init__(self, nc, es):
        self.nc = nc
        self.q = {e: [] for e in ENGS}
        self.semh = {}
        self.semv = {}
        for s in ENGS + ("cc", "out"):
            self.semh[s] = es.enter_context(nc.semaphore(f"sem_{s}"))
            self.semv[s] = 0

    def emit(self, eng, fn):
        self.q[eng].append(fn)

    def wait(self, eng, sem, val):
        if val and val > 0:
            h = self.semh[sem]
            self.q[eng].append(lambda e, h=h, v=val: e.wait_ge(h, v))

    def inc(self, eng, sem, fn, k=1):
        self.semv[sem] += k
        tgt = self.semv[sem]
        h = self.semh[sem]
        self.q[eng].append(lambda e, f=fn, h=h, kk=k: f(e).then_inc(h, kk))
        return tgt

    def run(self, block):
        @block.sync
        def _(e):
            for f in self.q["sync"]:
                f(e)

        @block.scalar
        def _(e):
            for f in self.q["scalar"]:
                f(e)

        @block.vector
        def _(e):
            for f in self.q["vector"]:
                f(e)

        @block.tensor
        def _(e):
            for f in self.q["tensor"]:
                f(e)

        @block.gpsimd
        def _(e):
            for f in self.q["gpsimd"]:
                f(e)


def cdiv(a, b):
    return (a + b - 1) // b


ARENA_BYTES = 194688


class Arena:

    def __init__(self, ap):
        self.ap = ap
        self.off = 0

    def alloc(self, nbytes, align=512):
        o = (self.off + align - 1) // align * align
        self.off = o + nbytes
        assert self.off <= ARENA_BYTES, f"arena overflow {self.off}"
        return o

    def view(self, off, dtype, shape):
        esz = {F32: 4, BF16: 2, F16: 2, I32: 4}[dtype]
        n = int(np.prod(shape[1:])) if len(shape) > 1 else 1
        ap = self.ap[:, off:off + n * esz].bitcast(dtype)
        if len(shape) > 2:
            dims = " ".join(f"d{i}" for i in range(1, len(shape)))
            ap = ap.rearrange(f"p ({dims}) -> p {dims}",
                              **{f"d{i}": shape[i] for i in range(1, len(shape) - 1)})
        return ap


def build(debug=False):
    nc = bass.Bass("TRN2", target_bir_lowering=False,
                   detect_race_conditions=False)

    dx_T = nc.dram_tensor("xT", [D, T], F32, kind="ExternalInput").ap()
    dx_res = nc.dram_tensor("xres", [TH, D], F32, kind="ExternalInput").ap()
    dwinT = nc.dram_tensor("winT", [D, 2 * HC], F32, kind="ExternalInput").ap()
    dconvw = nc.dram_tensor("convw", [HC, 3], F32, kind="ExternalInput").ap()
    dconvb = nc.dram_tensor("convb", [128, 8], F32, kind="ExternalInput").ap()
    dwxpT = nc.dram_tensor("wxpT", [HC, 96], F32, kind="ExternalInput").ap()
    dwdtT = nc.dram_tensor("wdtT", [DTR, HC], F32, kind="ExternalInput").ap()
    dbdt = nc.dram_tensor("bdt", [128, 8], F32, kind="ExternalInput").ap()
    dalog = nc.dram_tensor("alog", [1, NST], F32, kind="ExternalInput").ap()
    ddvec = nc.dram_tensor("dvec", [128, 8], F32, kind="ExternalInput").ap()
    dwoutT = nc.dram_tensor("woutT", [HC, D], F32, kind="ExternalInput").ap()
    dwg_s = nc.dram_tensor("wg_s", [ISL, D], F32, kind="ExternalInput").ap()
    dwu_s = nc.dram_tensor("wu_s", [ISL, D], F32, kind="ExternalInput").ap()
    dwd_s = nc.dram_tensor("wd_s", [DSL, FI], F32, kind="ExternalInput").ap()
    dident = nc.dram_tensor("identin", [128, 128], F32, kind="ExternalInput").ap()

    dout = nc.dram_tensor("out", [TH, D], F32, kind="ExternalOutput").ap()

    if debug == 2:
        ddbg2 = {"ymine": nc.dram_tensor("dbg2_ymine", [128, 8 * TC], F32,
                                         kind="ExternalOutput").ap(),
                 "state": nc.dram_tensor("dbg2_state", [128, 8 * NST * NCH], F32,
                                         kind="ExternalOutput").ap(),
                 "dt": nc.dram_tensor("dbg2_dt", [128, 8 * TC], F32,
                                      kind="ExternalOutput").ap(),
                 "rsin": nc.dram_tensor("dbg2_rsin", [T, D], F32,
                                        kind="ExternalOutput").ap(),
                 "rsout": nc.dram_tensor("dbg2_rsout", [TH, D], F32,
                                         kind="ExternalOutput").ap(),
                 "ym3": nc.dram_tensor("dbg2_ym3", [128, 8 * TC], F32,
                                       kind="ExternalOutput").ap()}
    if debug is True:
        ddbg = {k: nc.dram_tensor(f"dbg_{k}", [128, 8 * TC], F32,
                                  kind="ExternalOutput").ap()
                for k in ("u", "uc", "sz", "dt", "ymine")}
        ddbg["xdbl"] = nc.dram_tensor("dbg_xdbl", [96, TC], F32,
                                      kind="ExternalOutput").ap()
        ddbg["rs"] = nc.dram_tensor("dbg_rs", [128, D], F32,
                                    kind="ExternalOutput").ap()
        ddbg["x2"] = nc.dram_tensor("dbg_x2", [128, D], F32,
                                    kind="ExternalOutput").ap()
        ddbg["x2q"] = nc.dram_tensor("dbg_x2q", [128, D], F32,
                                     kind="ExternalOutput").ap()
        ddbg["hq"] = nc.dram_tensor("dbg_hq", [128, FI], F32,
                                    kind="ExternalOutput").ap()
        ddbg["gate"] = nc.dram_tensor("dbg_gate", [128, 512], F32,
                                      kind="ExternalOutput").ap()
        ddbg["wgT"] = nc.dram_tensor("dbg_wgT", [128, FI], F32,
                                     kind="ExternalOutput").ap()
        ddbg["wgq"] = nc.dram_tensor("dbg_wgq", [FI, D], F32,
                                     kind="ExternalOutput").ap()
        ddbg["ln"] = nc.dram_tensor("dbg_ln", [128, 8], F32,
                                    kind="ExternalOutput").ap()
        ddbg["r"] = nc.dram_tensor("dbg_r", [128, D], F32,
                                   kind="ExternalOutput").ap()
    d_abounce = nc.dram_tensor("a_bounce", [1, NST], F32, kind="Internal").ap()
    d_xdbl_in = nc.dram_tensor("cc_xdbl_in", [96, TC], F32, kind="Internal").ap()
    d_xdbl_out = nc.dram_tensor("cc_xdbl_out", [96, TC], F32, kind="Internal").ap()
    d_rs_in = nc.dram_tensor("cc_rs_in", [T, D], F32, kind="Internal").ap()
    d_rs_out = nc.dram_tensor("cc_rs_out", [TH, D], F32, kind="Internal").ap()
    d_wgT_pre = nc.dram_tensor("cc_wgT_pre", [D, ISL], BF16, kind="Internal").ap()
    d_wgT_blk = nc.dram_tensor("cc_wgT_blk", [8 * D, ISL], BF16, kind="Internal").ap()
    d_wuT_pre = nc.dram_tensor("cc_wuT_pre", [D, ISL], BF16, kind="Internal").ap()
    d_wuT_blk = nc.dram_tensor("cc_wuT_blk", [8 * D, ISL], BF16, kind="Internal").ap()
    d_wdT_pre = nc.dram_tensor("cc_wdT_pre", [FI, DSL], BF16, kind="Internal").ap()
    d_wdT_blk = nc.dram_tensor("cc_wdT_blk", [8 * FI, DSL], BF16, kind="Internal").ap()
    d_x2 = nc.dram_tensor("x2_scr", [TH, D], F32, kind="Internal").ap()
    d_hq = nc.dram_tensor("hq_scr", [TH, FI], BF16, kind="Internal").ap()

    es = contextlib.ExitStack()
    with es:
        sb = lambda name, shape, dt: es.enter_context(nc.sbuf_tensor(name, shape, dt))

        abc = sb("sb_abc", [128, NST], F32)
        ident = sb("sb_ident", [128, 128], BF16)
        convw = sb("sb_convw", [128, 8, 3], F32)
        convb = sb("sb_convb", [128, 8], F32)
        bdt = sb("sb_bdt", [128, 8], F32)
        dvec = sb("sb_dvec", [128, 8], F32)
        state = sb("sb_state", [128, 8, NST], F32)
        state_rd = sb("sb_state_rd", [128, 8, NST], F32)
        wxpT_h = sb("sb_wxpT_h", [128, 8, 96], BF16)
        wxpT_l = sb("sb_wxpT_l", [128, 8, 96], BF16)
        wdtT_h = sb("sb_wdtT_h", [DTR, HC], BF16)
        wdtT_l = sb("sb_wdtT_l", [DTR, HC], BF16)
        dtp = sb("sb_dtp", [DTR, TC], BF16)
        xdbl = sb("sb_xdbl", [96, TC], F32)
        alog_t = sb("sb_alog_t", [1, NST], F32)
        aneg_t = sb("sb_aneg_t", [1, NST], F32)
        ln = {k: sb(f"sb_ln_{k}", [128, 1], F32)
              for k in ("sr", "ssq", "mu", "m2e", "mu2", "ve", "iv2", "rstd")}
        qsml = {k: sb(f"sb_q_{k}", [128, FI // 16], F32) for k in ("m", "sh", "rm", "rm2")}

        arena_t = sb("sb_arena", [128, ARENA_BYTES], U8)
        A = Arena(arena_t[:])

        o_winT = A.alloc(8 * 2048 * 2)
        o_xTc = A.alloc(8 * (TC + 2) * 2, align=64)
        o_u = A.alloc(8 * (TC + 2) * 2, align=64)
        o_uc = A.alloc(8 * TC * 2, align=64)
        o_sz = A.alloc(8 * TC * 2, align=64)
        o_dt = A.alloc(8 * TC * 2, align=64)
        o_dA = [A.alloc(2 * TC * 4, align=64) for _ in range(2)]
        o_bu = A.alloc(TC * 2, align=64)
        o_pb = [A.alloc(TC * 2, align=64) for _ in range(6)]
        o_hbuf = A.alloc(4 * TC * 2, align=64)
        o_ymh = A.alloc(8 * TC * 2, align=64)
        o_woutT = A.alloc(8 * D * 2, align=64)
        o_rse = [A.alloc(D * 4, align=64) for _ in range(2)]
        mix_end = A.off
        o_BC = o_xTc
        o_yacc = o_u
        o_yml = o_dt

        winT = A.view(o_winT, BF16, [128, 8, 2048])
        xTc = A.view(o_xTc, BF16, [128, 8, TC + 2])
        u_sb = A.view(o_u, F16, [128, 8, TC + 2])
        uc_sb = A.view(o_uc, BF16, [128, 8, TC])
        sz_sb = A.view(o_sz, F16, [128, 8, TC])
        dt_sb = A.view(o_dt, F16, [128, 8, TC])
        dA = [A.view(o, F32, [128, 2, TC]) for o in o_dA]
        bu = A.view(o_bu, F16, [128, TC])
        pb = [A.view(o, F16, [128, TC]) for o in o_pb]
        hbuf = A.view(o_hbuf, F16, [128, 4, TC])
        ymh = A.view(o_ymh, BF16, [128, 8, TC])
        yml = A.view(o_yml, BF16, [128, 8, TC])
        yacc = A.view(o_yacc, F16, [128, 2, 4, TC])
        woutT = A.view(o_woutT, BF16, [128, 8, D])
        rse = [A.view(o, F32, [128, D]) for o in o_rse]
        sp_tmp = A.view(o_rse[1], F32, [128, 2, 512])
        BC = A.view(o_BC, F16, [128, 4, 2, TC])
        BCflat = A.view(o_BC, F16, [128, 4 * 2 * TC])
        wxf = A.view(o_sz, F32, [128, 8, 96])
        wdf = A.view(o_sz + 4096, F32, [128, HC])

        q_big_offs = [o_dt, o_ymh, o_woutT, o_xTc]

        oB = Arena(arena_t[:])
        oB.off = 0
        b_x2qT = oB.alloc(8 * TH * 2)
        b_rs = [oB.alloc(D * 4) for _ in range(2)]
        b_r = oB.alloc(D * 4)
        b_x2 = oB.alloc(D * 4)
        b_x2q = oB.alloc(D * 2)
        b_junk = oB.alloc(D * 2)
        b_q = [oB.alloc(D * 4) for _ in range(4)]
        b_wld = [oB.alloc(FI * 2) for _ in range(2)]
        b_ev = oB.alloc(1024 * 2)
        b_ti = oB.alloc(D * 4)
        assert oB.off <= ARENA_BYTES

        x2qT = A.view(b_x2qT, BF16, [128, 8, TH])
        rsld = [A.view(o, F32, [128, D]) for o in b_rs]
        r_sb = A.view(b_r, F32, [128, D])
        x2_sb = A.view(b_x2, F32, [128, D])
        x2q_sb = A.view(b_x2q, BF16, [128, D])
        junk_sb = A.view(b_junk, BF16, [128, D])
        wld = [A.view(o, BF16, [128, FI]) for o in b_wld]
        ev_sb = A.view(b_ev, BF16, [128, 1024])

        oC = Arena(arena_t[:])
        oC.off = b_rs[0]
        c_wg = [oC.alloc(8 * 512 * 2) for _ in range(2)]
        c_wu = [oC.alloc(8 * 512 * 2) for _ in range(2)]
        c_h = oC.alloc(512 * 4)
        c_sg = oC.alloc(512 * 2)
        c_hq = oC.alloc(512 * 2)
        c_q = [oC.alloc(512 * 4) for _ in range(4)]
        c_wdqT = oC.alloc(22 * 1024 * 2)
        c_hqld = [oC.alloc(FI * 2) for _ in range(2)]
        c_hqT = oC.alloc(22 * 128 * 2)
        c_x2l = [oC.alloc(D * 4) for _ in range(2)]
        c_out = [oC.alloc(D * 4) for _ in range(2)]
        assert oC.off <= ARENA_BYTES, f"phaseC overflow {oC.off}"

        wgc = [A.view(o, BF16, [128, 8, 512]) for o in c_wg]
        wuc = [A.view(o, BF16, [128, 8, 512]) for o in c_wu]
        h_sb = A.view(c_h, F32, [128, 512])
        sg_sb = A.view(c_sg, F16, [128, 512])
        hq_sb = A.view(c_hq, BF16, [128, 512])
        wdqT = A.view(c_wdqT, BF16, [128, 22, 1024])
        hqld = [A.view(o, BF16, [128, FI]) for o in c_hqld]
        hqT = A.view(c_hqT, BF16, [128, 22, 128])
        x2l = [A.view(o, F32, [128, D]) for o in c_x2l]
        out_sb = [A.view(o, F32, [128, D]) for o in c_out]

        PB = [es.enter_context(nc.psum_tensor(f"pb{i}", [128, 512], F32))
              for i in range(8)]

        block = es.enter_context(nc.Block())
        P = Prog(nc, es)
        bank_free = {i: None for i in range(8)}

        def bank_wait(eng, i):
            if bank_free[i] is not None:
                s, v = bank_free[i]
                P.wait(eng, s, v)

        def vfence(v_val):
            P.wait("gpsimd", "vector", v_val)
            return P.inc("gpsimd", "gpsimd", lambda e: e.engine_nop(), 1)

        def sload(dst, src):
            return P.inc("sync", "sync",
                         lambda e, d=dst, s=src: e.dma_start(out=d, in_=s), 16)

        def gload(dst, src):
            return P.inc("gpsimd", "gpsimd",
                         lambda e, d=dst, s=src: e.dma_start(out=d, in_=s), 16)

        v_convw = sload(convw[:],
                        dconvw.rearrange("(a p) b -> p a b", p=128))
        v_convb = sload(convb[:], dconvb)
        v_bdt = sload(bdt[:], dbdt)
        v_dvec = sload(dvec[:], ddvec)
        v_alog = sload(alog_t[:], dalog)
        v_ident = gload(ident[:], dident)
        v_wxf = sload(wxf[:], dwxpT.rearrange("(a p) b -> p a b", p=128))
        v_wdf = sload(wdf[0:DTR, :], dwdtT)
        P.wait("vector", "sync", max(v_wxf, v_wdf))
        P.inc("vector", "vector", lambda e: e.tensor_copy(out=wxpT_h[:], in_=wxf[:]))
        P.inc("vector", "vector", lambda e: e.scalar_tensor_tensor(
            out=wxpT_l[:], in0=wxpT_h[:], scalar=-1.0, in1=wxf[:],
            op0=ALU.mult, op1=ALU.add))
        P.inc("vector", "vector",
              lambda e: e.tensor_copy(out=wdtT_h[:], in_=wdf[0:DTR, :]))
        v_wsplit = P.inc("vector", "vector", lambda e: e.scalar_tensor_tensor(
            out=wdtT_l[:], in0=wdtT_h[:], scalar=-1.0, in1=wdf[0:DTR, :],
            op0=ALU.mult, op1=ALU.add))
        v_f_wsplit = vfence(v_wsplit)

        P.wait("scalar", "sync", v_alog)
        v_ax = P.inc("scalar", "scalar",
                     lambda e: e.activation(out=alog_t[:], in_=alog_t[:], func=AF.Exp))
        P.wait("gpsimd", "scalar", v_ax)
        P.inc("gpsimd", "gpsimd",
              lambda e: e.tensor_copy(out=aneg_t[:], in_=alog_t[:]), 1)
        v_an = P.inc("gpsimd", "gpsimd",
                     lambda e: e.tensor_scalar_mul(aneg_t[:], aneg_t[:], -1.0), 1)
        P.wait("sync", "gpsimd", v_an)
        v_ab = sload(d_abounce, aneg_t[:])
        P.wait("sync", "sync", v_ab)
        v_abc = sload(abc[:], d_abounce.broadcast_to([128, NST]))

        v_st0 = P.inc("vector", "vector", lambda e: e.memset(state[:], 0.0))

        q_prev_use = [0]

        def quant_fp4(src_ap, dst_ap, F, scratch_f32, scratch_i32, pc=128,
                      src_ready=None):
            nb = F // 16
            m = qsml["m"][0:pc, 0:nb]
            sh = qsml["sh"][0:pc, 0:nb]
            rm = qsml["rm"][0:pc, 0:nb]
            rm2 = qsml["rm2"][0:pc, 0:nb]
            asrc, aw, t1, t2 = [sc[0:pc, 0:F] for sc in scratch_f32]
            src3 = src_ap.rearrange("p (b k) -> p b k", k=16)
            if src_ready is not None:
                P.wait("vector", src_ready[0], src_ready[1])
                P.wait("scalar", src_ready[0], src_ready[1])
            v_m = P.inc("vector", "vector", lambda e: e.tensor_reduce(
                out=m, in_=src3, axis=AX.X, op=ALU.max, apply_absolute_value=True))
            v_as = P.inc("scalar", "scalar",
                         lambda e: e.activation(out=asrc, in_=src_ap, func=AF.Abs))
            v_sg = P.inc("scalar", "scalar",
                         lambda e: e.activation(out=aw, in_=src_ap, func=AF.Sign))
            P.wait("gpsimd", "vector", v_m)
            P.wait("gpsimd", "scalar", v_sg)
            P.wait("gpsimd", "vector", q_prev_use[0])
            v_sh = P.inc("gpsimd", "gpsimd", lambda e: e.tensor_scalar(
                out=sh, in0=m, scalar1=1.0 / 12.0, scalar2=5e-13,
                op0=ALU.mult, op1=ALU.max), 1)
            P.wait("vector", "gpsimd", v_sh)
            v_rm = P.inc("vector", "vector", lambda e: e.reciprocal(out=rm, in_=sh))
            P.wait("scalar", "vector", v_rm)
            v_rm2 = P.inc("scalar", "scalar",
                          lambda e: e.copy(rm2, rm))
            P.wait("vector", "scalar", v_rm2)
            P.inc("vector", "vector", lambda e: e.tensor_tensor(
                out=t1.rearrange("p (b k) -> p b k", k=16),
                in0=asrc.rearrange("p (b k) -> p b k", k=16),
                in1=rm2.unsqueeze(-1).broadcast_to([pc, nb, 16]), op=ALU.mult))
            P.inc("vector", "vector", lambda e: e.tensor_scalar(
                out=asrc, in0=t1, scalar1=0.5, scalar2=1.0,
                op0=ALU.is_gt, op1=ALU.mult))
            for mj in (1.5, 2.5, 3.5):
                P.inc("vector", "vector", lambda e, mj=mj: e.scalar_tensor_tensor(
                    out=asrc, in0=t1, scalar=mj, in1=asrc,
                    op0=ALU.is_gt, op1=ALU.add))
            P.inc("vector", "vector", lambda e: e.tensor_scalar(
                out=t2, in0=t1, scalar1=10.0, scalar2=2.0,
                op0=ALU.is_gt, op1=ALU.mult))
            for mj in (5.0, 7.0):
                P.inc("vector", "vector", lambda e, mj=mj: e.scalar_tensor_tensor(
                    out=t2, in0=t1, scalar=mj, in1=t2,
                    op0=ALU.is_gt, op1=ALU.add))
            P.inc("vector", "vector", lambda e: e.scalar_tensor_tensor(
                out=asrc, in0=t2, scalar=2.0, in1=asrc,
                op0=ALU.mult, op1=ALU.add))
            P.wait("vector", "scalar", v_sg)
            P.inc("vector", "vector", lambda e: e.tensor_tensor(
                out=t2.rearrange("p (b k) -> p b k", k=16),
                in0=aw.rearrange("p (b k) -> p b k", k=16),
                in1=sh.unsqueeze(-1).broadcast_to([pc, nb, 16]), op=ALU.mult))
            v_q = P.inc("vector", "vector",
                        lambda e: e.tensor_mul(dst_ap, asrc, t2))
            q_prev_use[0] = v_q
            return v_q

        ln_prev_use = [0]

        def ln_tail(r_ap, out_ap, v_r):
            v_sr = P.inc("vector", "vector", lambda e: e.tensor_reduce(
                out=ln["sr"][:], in_=r_ap, axis=AX.X, op=ALU.add))
            P.wait("scalar", "vector", v_r)
            v_ssq = P.inc("scalar", "scalar", lambda e: e.activation(
                out=junk_sb[:], in_=r_ap, func=AF.Square, accum_out=ln["ssq"][:]))
            P.wait("gpsimd", "vector", v_sr)
            P.wait("gpsimd", "scalar", v_ssq)
            P.wait("gpsimd", "vector", ln_prev_use[0])
            P.inc("gpsimd", "gpsimd", lambda e: e.tensor_scalar(
                out=ln["mu"][:], in0=ln["sr"][:], scalar1=1.0 / D, scalar2=None,
                op0=ALU.mult), 1)
            P.inc("gpsimd", "gpsimd", lambda e: e.tensor_scalar(
                out=ln["m2e"][:], in0=ln["ssq"][:], scalar1=1.0 / D,
                scalar2=LN_EPS, op0=ALU.mult, op1=ALU.add), 1)
            P.inc("gpsimd", "gpsimd", lambda e: e.tensor_mul(
                ln["mu2"][:], ln["mu"][:], ln["mu"][:]), 1)
            v_ve = P.inc("gpsimd", "gpsimd", lambda e: e.tensor_sub(
                ln["ve"][:], ln["m2e"][:], ln["mu2"][:]), 1)
            P.wait("vector", "gpsimd", v_ve)
            v_iv = P.inc("vector", "vector", lambda e: e.reciprocal(
                out=ln["iv2"][:], in_=ln["ve"][:]))
            P.wait("scalar", "vector", v_iv)
            v_rstd = P.inc("scalar", "scalar", lambda e: e.activation(
                out=ln["rstd"][:], in_=ln["iv2"][:], func=AF.Sqrt))
            P.wait("vector", "scalar", v_rstd)
            v_out = P.inc("vector", "vector", lambda e: e.tensor_scalar(
                out=out_ap, in0=r_ap, scalar1=ln["mu"][:], scalar2=ln["rstd"][:],
                op0=ALU.subtract, op1=ALU.mult))
            ln_prev_use[0] = v_out
            return v_out, v_ssq

        wq_scr_f = [A.view(o, F32, [128, FI]) for o in q_big_offs]
        wq_ti = None
        wq_in_v = A.view(o_winT, F32, [128, 3, D])
        wq_in_d = A.view(o_winT, F32, [128, FI])
        wq_out_v = A.view(o_uc, BF16, [128, 3, D])
        wq_out_d = A.view(o_uc, BF16, [128, FI])

        wq_plan = [(dwg_s, d_wgT_pre, ISL, D, wq_in_v, wq_out_v),
                   (dwu_s, d_wuT_pre, ISL, D, wq_in_v, wq_out_v),
                   (dwd_s, d_wdT_pre, DSL, FI, wq_in_d, wq_out_d)]
        v_wq_store = 0
        wq_ev = [A.view(o_sz + 8192 + 256 * i, BF16, [128, 128]) for i in range(2)]
        P.wait("tensor", "gpsimd", v_ident)
        for (dsrc, ddstT, R, F, vin, vout) in wq_plan:
            nt = cdiv(R, 128)
            for rt in range(nt):
                p0, p1 = rt * 128, min((rt + 1) * 128, R)
                pc = p1 - p0
                vv = sload(vin[0:pc, rt, :] if nt > 1 else vin[0:pc, :],
                           dsrc[p0:p1, :])
                P.wait("vector", "sync", vv)
                v_q = quant_fp4(vin[0:pc, rt, :] if nt > 1 else vin[0:pc, :],
                                vout[0:pc, rt, :] if nt > 1 else vout[0:pc, :],
                                F, wq_scr_f, wq_ti, pc, src_ready=("sync", vv))
                v_f_wq = vfence(v_q)
                P.wait("tensor", "gpsimd", v_f_wq)
                for ft in range(F // 128):
                    bnk = 6 + (ft % 2)
                    bank_wait("tensor", bnk)
                    tin = (vout[0:pc, rt, ft * 128:(ft + 1) * 128] if nt > 1
                           else vout[0:pc, ft * 128:(ft + 1) * 128])
                    v_tr = P.inc("tensor", "tensor",
                                 lambda e, tin=tin, bnk=bnk, pc=pc: e.transpose(
                                     PB[bnk][:, 0:64].bitcast(BF16)[:, 0:pc],
                                     tin, ident[0:pc, 0:pc]))
                    P.wait("vector", "tensor", v_tr)
                    P.wait("vector", "sync", v_wq_store)
                    v_evt = P.inc("vector", "vector",
                                  lambda e, ft=ft, bnk=bnk, pc=pc: e.tensor_copy(
                                      out=wq_ev[ft % 2][:, 0:pc],
                                      in_=PB[bnk][:, 0:64].bitcast(BF16)[:, 0:pc]))
                    bank_free[bnk] = ("vector", v_evt)
                    v_f_ev = vfence(v_evt)
                    P.wait("sync", "gpsimd", v_f_ev)
                    P.wait("sync", "vector", v_evt)
                    v_wq_store = sload(ddstT[ft * 128:(ft + 1) * 128, p0:p1],
                                       wq_ev[ft % 2][:, 0:pc])
        v_wq_quant_done = P.semv["vector"]
        v_wq_sgn_done = P.semv["scalar"]

        v_bc_read_prev = 0
        v_xdo_prev = 0
        v_inproj_last_prev = 0
        v_uevac_read_prev = 0
        v_dA_last_prev = 0
        v_gate_prev = 0
        v_ucread_last_prev = 0
        v_ar_prev = 0
        v_rs_stores = 0
        v_wq_ag = None
        v_outproj_evac_prev = 0
        v_gy_last_prev = 0
        KT_G = ()
        PSLOT = {0: (0, 0), 1: (1, 1), 2: (2, 3), 3: (4, 5)}
        g_last = {2: [0, 0], 3: [0, 0]}

        for c in range(NCH):
            t0 = c * TC
            lo = max(t0 - 1, 0)
            hi = min(t0 + TC + 1, T)
            ofs = lo - (t0 - 1)

            if c == 0:
                P.wait("gpsimd", "vector", v_wq_quant_done)
                P.wait("gpsimd", "scalar", v_wq_sgn_done)
                P.wait("gpsimd", "sync", v_wq_store)
                P.wait("scalar", "vector", v_wq_quant_done)
                P.wait("scalar", "sync", v_wq_store)
            else:
                P.wait("gpsimd", "tensor", v_inproj_last_prev)
                P.wait("gpsimd", "vector", v_bc_read_prev)
            v_xc = 0
            for dt_ in range(8):
                v_xc = gload(xTc[:, dt_, ofs:ofs + (hi - lo)],
                             dx_T[dt_ * 128:(dt_ + 1) * 128, lo:hi])
            if c == 0:
                v_wl = gload(winT[:],
                             dwinT.rearrange("(a p) b -> p a b", p=128))
                v_wo = gload(woutT[:],
                             dwoutT.rearrange("(a p) b -> p a b", p=128))
                v_xc = max(v_xc, v_wl, v_wo)

            if c == 0:
                P.inc("vector", "vector", lambda e: e.memset(u_sb[:, :, 0:1], 0.0))
            if c == NCH - 1:
                P.inc("vector", "vector",
                      lambda e: e.memset(u_sb[:, :, TC + 1:TC + 2], 0.0))

            P.wait("tensor", "gpsimd", v_xc)
            if c > 0:
                P.wait("scalar", "vector", v_uevac_read_prev)
                P.wait("scalar", "vector", v_gate_prev)
                P.wait("scalar", "gpsimd", v_gy_last_prev)
            pbrot = [0]

            def inproj_tile(et, col0, width, dst_kind):
                nccw = cdiv(width, 512)
                for ccw in range(nccw):
                    w0 = ccw * 512
                    w1 = min(w0 + 512, width)
                    bi = pbrot[0] % 2
                    pbrot[0] += 1
                    bank_wait("tensor", bi)
                    for dt_ in range(8):
                        def mm(e, bi=bi, dt_=dt_, et=et, a=col0 + w0, b=col0 + w1,
                               st=(dt_ == 0), sp=(dt_ == 7)):
                            return e.matmul(PB[bi][:, 0:b - a],
                                            winT[:, dt_, et * 128:(et + 1) * 128],
                                            xTc[:, dt_, a:b], start=st, stop=sp)
                        if dt_ == 7:
                            v_mm = P.inc("tensor", "tensor", mm)
                        else:
                            P.emit("tensor", mm)
                    P.wait("scalar", "tensor", v_mm)
                    kind, kt = dst_kind
                    if kind == "u":
                        v_ev = P.inc("scalar", "scalar",
                                     lambda e, bi=bi, kt=kt, a=col0 + w0, b=col0 + w1:
                                     e.activation(out=u_sb[:, kt, a:b],
                                                  in_=PB[bi][:, 0:b - a],
                                                  func=AF.Copy))
                    else:
                        v_ev = P.inc("scalar", "scalar",
                                     lambda e, bi=bi, kt=kt, a=col0 + w0, b=col0 + w1:
                                     e.activation(out=sz_sb[:, kt, a - 1:b - 1],
                                                  in_=PB[bi][:, 0:b - a],
                                                  func=AF.Silu))
                    bank_free[bi] = ("scalar", v_ev)
                return P.semv["tensor"]

            v_u_evac_et = []
            for et in range(8):
                v_inproj_u = inproj_tile(et, ofs, hi - lo, ("u", et))
                v_u_evac_et.append(P.semv["scalar"])

            P.wait("vector", "sync", max(v_convw, v_convb, v_rs_stores))
            v_uc_silu_prev = [0]
            for kt in range(8):
                P.wait("vector", "scalar", v_u_evac_et[kt])
                P.wait("vector", "scalar", v_uc_silu_prev[0])
                P.inc("vector", "vector", lambda e, kt=kt: e.tensor_scalar_mul(
                    rse[0][0:128, 0:TC], u_sb[:, kt, 1:TC + 1], convw[:, kt, 1:2]))
                P.inc("vector", "vector", lambda e, kt=kt: e.scalar_tensor_tensor(
                    out=rse[0][0:128, 0:TC], in0=u_sb[:, kt, 0:TC],
                    scalar=convw[:, kt, 0:1], in1=rse[0][0:128, 0:TC],
                    op0=ALU.mult, op1=ALU.add))
                v_cv = P.inc("vector", "vector", lambda e, kt=kt: e.scalar_tensor_tensor(
                    out=rse[0][0:128, 0:TC], in0=u_sb[:, kt, 2:TC + 2],
                    scalar=convw[:, kt, 2:3], in1=rse[0][0:128, 0:TC],
                    op0=ALU.mult, op1=ALU.add))
                P.wait("scalar", "vector", v_cv)
                if c > 0 and kt == 0:
                    P.wait("scalar", "tensor", v_ucread_last_prev)
                v_uc_silu_prev[0] = P.inc("scalar", "scalar",
                    lambda e, kt=kt: e.activation(
                        out=uc_sb[:, kt, :], in_=rse[0][0:128, 0:TC], func=AF.Silu,
                        bias=convb[:, kt:kt + 1]))
            v_uc_done = P.semv["scalar"]

            P.wait("tensor", "scalar", v_uc_done)
            if c == 0:
                P.wait("tensor", "gpsimd", v_f_wsplit)
            P.wait("vector", "sync", v_xdo_prev)
            for ccw in range(TC // 512):
                bi = 2 + (ccw % 2)
                bank_wait("tensor", bi)
                for kt in range(8):
                    for si, wsb in enumerate((wxpT_h, wxpT_l)):
                        def mmx(e, bi=bi, kt=kt, ccw=ccw, wsb=wsb,
                                st=(kt == 0 and si == 0), sp=(kt == 7 and si == 1)):
                            return e.matmul(PB[bi][0:96, 0:512], wsb[:, kt, :],
                                            uc_sb[:, kt, ccw * 512:(ccw + 1) * 512],
                                            start=st, stop=sp)
                        if kt == 7 and si == 1:
                            v_mmx = P.inc("tensor", "tensor", mmx)
                        else:
                            P.emit("tensor", mmx)
                P.wait("vector", "tensor", v_mmx)
                v_xd = P.inc("vector", "vector", lambda e, bi=bi, ccw=ccw:
                             e.tensor_copy(out=xdbl[:, ccw * 512:(ccw + 1) * 512],
                                           in_=PB[bi][0:96, 0:512]))
                bank_free[bi] = ("vector", v_xd)
            v_f_xd = vfence(v_xd)
            P.wait("sync", "gpsimd", v_f_xd)
            P.wait("sync", "vector", v_xd)
            if c > 0:
                P.wait("sync", "cc", v_ar_prev)
            v_xdo = sload(d_xdbl_in, xdbl[:])
            v_xdo_prev = v_xdo
            P.wait("gpsimd", "sync", v_xdo)
            v_ar = P.inc("gpsimd", "cc", lambda e: e.collective_compute(
                "AllReduce", ALU.add,
                replica_groups=[[0, 1], [2, 3], [4, 5], [6, 7]],
                ins=[d_xdbl_in], outs=[d_xdbl_out]))
            v_ar_prev = v_ar

            for et in range(8, 16):
                v_inproj_z = inproj_tile(et, 1, TC, ("z", et - 8))
            v_inproj_last_prev = P.semv["tensor"]
            v_z_evac = P.semv["scalar"]

            P.wait("gpsimd", "cc", v_ar)
            v_dtp = gload(dtp[:], d_xdbl_out[0:DTR, :])
            P.wait("tensor", "gpsimd", v_dtp)
            v_dtadd_prev = [0]
            if c > 0:
                P.wait("vector", "scalar", v_dA_last_prev)
                P.wait("scalar", "vector", v_outproj_evac_prev)
                P.wait("scalar", "sync", v_rs_stores)
            for kt in range(8):
                for ccw in range(TC // 512):
                    bi = 4 + ((kt * 2 + ccw) % 2)
                    bank_wait("tensor", bi)
                    for si, wsb in enumerate((wdtT_h, wdtT_l)):
                        def mmd_(e, bi=bi, kt=kt, ccw=ccw, wsb=wsb,
                                 st=(si == 0), sp=(si == 1)):
                            return e.matmul(
                                PB[bi][:, 0:512],
                                wsb[:, kt * 128:(kt + 1) * 128],
                                dtp[:, ccw * 512:(ccw + 1) * 512],
                                start=st, stop=sp)
                        if si == 1:
                            v_mmd = P.inc("tensor", "tensor", mmd_)
                        else:
                            P.emit("tensor", mmd_)
                    P.wait("scalar", "tensor", v_mmd)
                    P.wait("scalar", "vector", v_dtadd_prev[0])
                    P.inc("scalar", "scalar", lambda e, bi=bi, kt=kt: e.activation(
                        out=sp_tmp[:, 0, :], in_=PB[bi][:, 0:512], func=AF.Abs,
                        bias=bdt[:, kt:kt + 1]))
                    P.inc("scalar", "scalar", lambda e: e.activation(
                        out=sp_tmp[:, 0, :], in_=sp_tmp[:, 0, :], func=AF.Exp,
                        scale=-1.0))
                    P.inc("scalar", "scalar", lambda e: e.activation(
                        out=sp_tmp[:, 0, :], in_=sp_tmp[:, 0, :], func=AF.Ln,
                        bias=1.0))
                    v_sp = P.inc("scalar", "scalar", lambda e, bi=bi, kt=kt:
                                 e.activation(out=sp_tmp[:, 1, :],
                                              in_=PB[bi][:, 0:512], func=AF.Relu,
                                              bias=bdt[:, kt:kt + 1]))
                    bank_free[bi] = ("scalar", v_sp)
                    P.wait("vector", "scalar", v_sp)
                    v_dtadd_prev[0] = P.inc("vector", "vector",
                          lambda e, kt=kt, ccw=ccw:
                          e.tensor_add(dt_sb[:, kt, ccw * 512:(ccw + 1) * 512],
                                       sp_tmp[:, 1, :], sp_tmp[:, 0, :]))
                if kt == 3:
                    v_dt_half0 = P.semv["vector"]
            v_dt_done = P.semv["vector"]
            v_dt_half = [v_dt_half0, v_dt_done]

            P.wait("scalar", "vector", v_st0 if c == 0 else v_gate_prev)
            v_strd = P.inc("scalar", "scalar", lambda e: e.copy(
                state_rd[:].rearrange("p a b -> p (a b)"),
                state[:].rearrange("p a b -> p (a b)")))
            P.wait("vector", "scalar", v_strd)

            P.wait("scalar", "sync", v_abc)
            v_scan_vec_prev = [0]
            v_bc_pair = [0, 0]
            v_bc_read = [0, 0]

            def emit_bc_load(k, first_of_chunk):
                ng, j = (k % 8) // 2, k % 2
                if first_of_chunk:
                    P.wait("gpsimd", "tensor", v_inproj_last_prev)
                if c > 0 or k >= 2:
                    P.wait("gpsimd", "vector",
                           v_bc_read[j] if k >= 2 else v_bc_read_prev)
                for i2 in range(2):
                    s = 2 * j + i2
                    src = bass.AP(d_xdbl_out.tensor, (DTR + 4 * ng + s) * TC,
                                  [[0, 128], [NST * TC, 2], [1, TC]])
                    v_bc_pair[j] = gload(
                        BCflat[:, s * 2 * TC:(s + 1) * 2 * TC], src)

            emit_bc_load(0, True)
            emit_bc_load(1, True)
            for ktg in range(2):
                kts = range(ktg * 4, ktg * 4 + 4)
                P.wait("scalar", "vector", v_dt_half[ktg])
                for ng in range(4):
                    for n4 in range(4):
                        n = 4 * ng + n4
                        if n4 % 2 == 0:
                            P.wait("vector", "gpsimd", v_bc_pair[n4 // 2])
                        P.wait("scalar", "vector", v_scan_vec_prev[0])
                        v_dA_half = []
                        for q in range(2):
                            ktp = ktg * 4 + q * 2
                            v_dA_half.append(P.inc("scalar", "scalar",
                                  lambda e, q=q, ktp=ktp, n=n: e.activation(
                                      out=dA[q][:].rearrange("p a b -> p (a b)"),
                                      in_=dt_sb[:, ktp:ktp + 2, :]
                                      .rearrange("p a b -> p (a b)"),
                                      func=AF.Exp, scale=abc[:, n:n + 1])))
                        for ki, kt in enumerate(kts):
                            if ki % 2 == 0:
                                P.wait("vector", "scalar", v_dA_half[ki // 2])
                            P.inc("vector", "vector",
                                  lambda e, kt=kt, n4=n4: e.tensor_tensor(
                                      out=bu[:], in0=uc_sb[:, kt, :],
                                      in1=BC[:, n4, 0, :], op=ALU.mult))
                            P.inc("vector", "vector",
                                  lambda e, kt=kt, ki=ki, n=n:
                                  e.tensor_tensor_scan(
                                      out=hbuf[:, ki, :],
                                      data0=dA[(kt % 4) // 2][:, kt % 2, :],
                                      data1=bu[:],
                                      initial=state_rd[:, kt, n:n + 1],
                                      op0=ALU.mult, op1=ALU.add))
                        v_scan_vec_prev[0] = P.semv["vector"]
                        gp_kts = () if n == NST - 1 else KT_G
                        for ki, kt in enumerate(kts):
                            slot = PSLOT[ki][n % 2]
                            if ki in KT_G:
                                P.wait("vector", "gpsimd", g_last[ki][n % 2])
                            P.inc("vector", "vector",
                                  lambda e, ki=ki, n4=n4, slot=slot:
                                  e.tensor_tensor(out=pb[slot][:],
                                                  in0=hbuf[:, ki, :],
                                                  in1=BC[:, n4, 1, :],
                                                  op=ALU.mult))
                        P.inc("vector", "vector", lambda e, ktg=ktg, n=n:
                              e.tensor_copy(
                                  out=state[:, ktg * 4:ktg * 4 + 4, n:n + 1],
                                  in_=hbuf[:, :, TC - 1:TC]))
                        for ki in range(4):
                            if ki in gp_kts:
                                continue
                            slot = PSLOT[ki][n % 2]
                            if ki in KT_G:
                                P.wait("vector", "gpsimd", g_last[ki][(n + 1) % 2])
                            if n == 0:
                                P.inc("vector", "vector",
                                      lambda e, ki=ki, slot=slot: e.tensor_copy(
                                          out=yacc[:, 0, ki, :], in_=pb[slot][:]))
                            else:
                                P.inc("vector", "vector",
                                      lambda e, ki=ki, n=n, slot=slot:
                                      e.tensor_tensor(
                                          out=yacc[:, n % 2, ki, :],
                                          in0=yacc[:, (n + 1) % 2, ki, :],
                                          in1=pb[slot][:], op=ALU.add))
                        v_pm_end = P.semv["vector"]
                        if gp_kts:
                            P.wait("gpsimd", "vector", v_pm_end)
                            P.emit("gpsimd", lambda e: e.engine_nop())
                        for ki in gp_kts:
                            slot = PSLOT[ki][n % 2]
                            if n == 0:
                                g_last[ki][0] = P.inc(
                                    "gpsimd", "gpsimd",
                                    lambda e, ki=ki, slot=slot: e.tensor_copy(
                                        out=yacc[:, 0, ki, :], in_=pb[slot][:]), 1)
                            else:
                                g_last[ki][n % 2] = P.inc(
                                    "gpsimd", "gpsimd",
                                    lambda e, ki=ki, n=n, slot=slot:
                                    e.tensor_tensor(
                                        out=yacc[:, n % 2, ki, :],
                                        in0=yacc[:, (n + 1) % 2, ki, :],
                                        in1=pb[slot][:], op=ALU.add), 1)
                        if n4 % 2 == 1:
                            jj = n4 // 2
                            v_bc_read[jj] = P.semv["vector"]
                            kk = ktg * 8 + ng * 2 + jj
                            if kk + 2 < 16:
                                emit_bc_load(kk + 2, False)
                    v_bc_read_prev = P.semv["vector"]
                v_dA_ktg_end = P.semv["scalar"]
                v_gy_end = P.semv["gpsimd"]
                P.wait("vector", "gpsimd", v_gy_end)
                P.wait("vector", "scalar", v_dA_ktg_end)
                P.wait("vector", "tensor", v_ucread_last_prev)
                P.wait("vector", "scalar", v_z_evac)
                P.wait("vector", "sync", v_dvec)
                for ki, kt in enumerate(kts):
                    P.inc("vector", "vector", lambda e, kt=kt, ki=ki:
                          e.scalar_tensor_tensor(
                              out=rse[0][0:128, 0:TC], in0=uc_sb[:, kt, :],
                              scalar=dvec[:, kt:kt + 1],
                              in1=yacc[:, 1, ki, :], op0=ALU.mult, op1=ALU.add))
                    P.inc("vector", "vector", lambda e, kt=kt: e.tensor_tensor(
                        out=rse[1][0:128, 0:TC], in0=rse[0][0:128, 0:TC],
                        in1=sz_sb[:, kt, :], op=ALU.mult))
                    P.inc("vector", "vector", lambda e, kt=kt: e.tensor_copy(
                        out=ymh[:, kt, :], in_=rse[1][0:128, 0:TC]))
                    P.inc("vector", "vector", lambda e, kt=kt:
                          e.scalar_tensor_tensor(
                              out=yml[:, kt, :], in0=ymh[:, kt, :], scalar=-1.0,
                              in1=rse[1][0:128, 0:TC],
                              op0=ALU.mult, op1=ALU.add))
            v_gate_prev = P.semv["vector"]
            v_dA_last_prev = P.semv["scalar"]
            v_gy_last_prev = P.semv["gpsimd"]

            P.wait("gpsimd", "vector", v_gate_prev)
            P.wait("gpsimd", "scalar", P.semv["scalar"])
            v_echo_g = P.inc("gpsimd", "gpsimd", lambda e: e.engine_nop(), 1)
            P.wait("tensor", "gpsimd", v_echo_g)
            P.wait("tensor", "vector", v_gate_prev)
            P.wait("tensor", "scalar", P.semv["scalar"])
            for ts in range(8):
                ob = (0 + 2 * (ts % 2), 1 + 2 * (ts % 2))
                for bnk in ob:
                    bank_wait("tensor", bnk)
                for kt in range(8):
                    for si, ysb in enumerate((ymh, yml)):
                        for half in range(2):
                            def mmo(e, kt=kt, ts=ts, half=half, bnk=ob[half],
                                    ysb=ysb, st=(kt == 0 and si == 0),
                                    sp=(kt == 7 and si == 1)):
                                return e.matmul(
                                    PB[bnk][:, 0:512],
                                    ysb[:, kt, ts * 128:(ts + 1) * 128],
                                    woutT[:, kt, half * 512:(half + 1) * 512],
                                    start=st, stop=sp)
                            if kt == 7 and si == 1 and half == 1:
                                v_mmo = P.inc("tensor", "tensor", mmo)
                            else:
                                P.emit("tensor", mmo)
                P.wait("scalar", "tensor", v_mmo)
                reb = ts % 2
                v_ev = None
                for half in range(2):
                    v_ev = P.inc("scalar", "scalar",
                                 lambda e, reb=reb, half=half, bnk=ob[half]:
                                 e.copy(rse[reb][0:128, half * 512:(half + 1) * 512],
                                        PB[bnk][:, 0:512]))
                for half in range(2):
                    bank_free[ob[half]] = ("scalar", v_ev)
                P.wait("gpsimd", "scalar", v_ev)
                v_f_rs = P.inc("gpsimd", "gpsimd", lambda e: e.engine_nop(), 1)
                P.wait("sync", "gpsimd", v_f_rs)
                P.wait("sync", "scalar", v_ev)
                v_rs_stores = sload(d_rs_in[t0 + ts * 128:t0 + (ts + 1) * 128, :],
                                    rse[reb][0:128, 0:D])
            v_ucread_last_prev = P.semv["tensor"]
            v_outproj_evac_prev = P.semv["vector"]

            if debug == 2:
                P.wait("gpsimd", "vector", v_gate_prev)
                if c == NCH - 1:
                    P.inc("gpsimd", "gpsimd", lambda e: e.dma_start(
                        out=ddbg2["ym3"].rearrange("p (a b) -> p a b", a=8),
                        in_=ymh[:]), 16)
                if c == 0:
                    P.inc("gpsimd", "gpsimd", lambda e: e.dma_start(
                        out=ddbg2["ymine"].rearrange("p (a b) -> p a b", a=8),
                        in_=ymh[:]), 16)
                    P.inc("gpsimd", "gpsimd", lambda e: e.dma_start(
                        out=ddbg2["dt"].rearrange("p (a b) -> p a b", a=8),
                        in_=dt_sb[:]), 16)
                P.inc("gpsimd", "gpsimd", lambda e, c=c: e.dma_start(
                    out=ddbg2["state"].rearrange("p (c a b) -> p c a b", c=NCH, a=8)[:, c],
                    in_=state[:]), 16)
            if debug is True and c == 0:
                P.wait("gpsimd", "vector", P.semv["vector"])
                P.wait("gpsimd", "scalar", P.semv["scalar"])
                for key, src_ap in (("u", u_sb[:, :, 1:TC + 1]), ("uc", uc_sb[:]),
                                    ("sz", sz_sb[:]), ("dt", dt_sb[:]),
                                    ("ymine", ymh[:])):
                    P.inc("gpsimd", "gpsimd",
                          lambda e, d=ddbg[key], s=src_ap: e.dma_start(
                              out=d.rearrange("p (a b) -> p a b", a=8), in_=s), 16)
                P.wait("sync", "vector", P.semv["vector"])
                sload(ddbg["xdbl"], xdbl[:])
            if c == 0:
                v_wq_ag = []
                for a, b in ((d_wgT_pre, d_wgT_blk), (d_wuT_pre, d_wuT_blk),
                             (d_wdT_pre, d_wdT_blk)):
                    P.wait("gpsimd", "sync", v_wq_store)
                    v_wq_ag.append(P.inc("gpsimd", "cc",
                                         lambda e, a=a, b=b: e.collective_compute(
                                             "AllGather", ALU.bypass,
                                             replica_groups=[[0, 1, 2, 3, 4, 5, 6, 7]],
                                             ins=[a], outs=[b])))

        P.wait("gpsimd", "sync", v_rs_stores)
        v_rs = P.inc("gpsimd", "cc", lambda e: e.collective_compute(
            "ReduceScatter", ALU.add,
            replica_groups=[[0, 1], [2, 3], [4, 5], [6, 7]],
            ins=[d_rs_in], outs=[d_rs_out]))

        if debug == 2:
            P.wait("gpsimd", "sync", v_rs_stores)
            P.inc("gpsimd", "gpsimd",
                  lambda e: e.dma_start(out=ddbg2["rsin"], in_=d_rs_in), 16)
            P.wait("gpsimd", "cc", v_rs)
            P.inc("gpsimd", "gpsimd",
                  lambda e: e.dma_start(out=ddbg2["rsout"], in_=d_rs_out), 16)
        P.wait("sync", "cc", v_wq_ag[-1])
        P.wait("sync", "tensor", P.semv["tensor"])
        P.wait("sync", "vector", P.semv["vector"])
        P.wait("sync", "scalar", P.semv["scalar"])
        barrier_s = P.inc("sync", "sync", lambda e: e.dma_start(
            out=d_abounce, in_=alog_t[:]), 16)
        for eng in ("tensor", "vector", "scalar", "gpsimd"):
            P.wait(eng, "sync", barrier_s)

        v_wd_l = 0
        for it in range(22):
            wdsrc = bass.AP(d_wdT_blk.tensor, it * 128 * DSL,
                            [[DSL, 128], [FI * DSL, 8], [1, DSL]])
            v_wd_l = sload(wdqT[:, it, :], wdsrc)

        P.wait("sync", "cc", v_rs)
        b_qf = [A.view(o, F32, [128, D]) for o in b_q]
        b_qi = A.view(b_ti, F32, [128, D])
        v_b2_r_prev = [0]
        v_b2_x2store_prev = [0]
        v_b2_tr_prev = [0]
        v_b2_ssq_prev = [0]
        for tt in range(16):
            if tt >= 1:
                P.wait("sync", "vector", v_b2_r_prev[0])
            v_rl = sload(rsld[tt % 2][:], d_rs_out[tt * 128:(tt + 1) * 128, :])
            v_xr = sload(x2l[tt % 2][:], dx_res[tt * 128:(tt + 1) * 128, :])
            P.wait("vector", "sync", max(v_rl, v_xr))
            P.wait("vector", "scalar", v_b2_ssq_prev[0])
            v_r = P.inc("vector", "vector", lambda e, tt=tt: e.scalar_tensor_tensor(
                out=r_sb[:], in0=x2l[tt % 2][:], scalar=2.0,
                in1=rsld[tt % 2][:], op0=ALU.mult, op1=ALU.add))
            v_b2_r_prev[0] = v_r
            P.wait("vector", "sync", v_b2_x2store_prev[0])
            v_x2, v_ssq = ln_tail(r_sb[:], x2_sb[:], v_r)
            v_b2_ssq_prev[0] = v_ssq
            v_f_x2 = vfence(v_x2)
            P.wait("sync", "gpsimd", v_f_x2)
            P.wait("sync", "vector", v_x2)
            v_b2_x2store_prev[0] = sload(d_x2[tt * 128:(tt + 1) * 128, :], x2_sb[:])
            if debug is True and tt == 0:
                P.wait("sync", "vector", P.semv["vector"])
                P.wait("sync", "scalar", P.semv["scalar"])
                sload(ddbg["rs"], rsld[0][:])
                sload(ddbg["x2"], x2_sb[:])
                sload(ddbg["r"], r_sb[:])
                for li, lk in enumerate(("sr", "ssq", "mu", "m2e", "mu2", "ve",
                                         "iv2", "rstd")):
                    P.inc("sync", "sync",
                          lambda e, li=li, lk=lk: e.dma_start(
                              out=ddbg["ln"][:, li:li + 1], in_=ln[lk][:],
                              allow_slow_non_contiguous=True), 16)
                P.wait("vector", "sync", P.semv["sync"])
                P.wait("scalar", "sync", P.semv["sync"])
            P.wait("vector", "tensor", v_b2_tr_prev[0])
            v_x2q = quant_fp4(x2_sb[:], x2q_sb[:], D, b_qf, b_qi, 128,
                              src_ready=("vector", v_x2))
            if debug is True and tt == 0:
                P.wait("gpsimd", "vector", v_x2q)
                P.inc("gpsimd", "gpsimd", lambda e: e.dma_start(
                    out=ddbg["x2q"], in_=x2q_sb[:]), 16)
            P.wait("tensor", "vector", v_x2q)
            for dt_ in range(8):
                bnk = 6 + (dt_ % 2)
                bank_wait("tensor", bnk)
                v_tr = P.inc("tensor", "tensor", lambda e, dt_=dt_, bnk=bnk:
                             e.transpose(PB[bnk][:, 0:64].bitcast(BF16),
                                         x2q_sb[:, dt_ * 128:(dt_ + 1) * 128],
                                         ident[:]))
                P.wait("vector", "tensor", v_tr)
                v_evt = P.inc("vector", "vector",
                              lambda e, dt_=dt_, tt=tt, bnk=bnk: e.tensor_copy(
                                  out=x2qT[:, dt_, tt * 128:(tt + 1) * 128],
                                  in_=PB[bnk][:, 0:64].bitcast(BF16)))
                bank_free[bnk] = ("vector", v_evt)
            v_b2_tr_prev[0] = P.semv["tensor"]
        v_x2qT_done = P.semv["vector"]

        c_qf = [A.view(o, F32, [128, 512]) for o in c_q]
        c_qi = A.view(c_hqld[0], F32, [128, 512])
        P.wait("tensor", "vector", v_x2qT_done)
        ics = [(k * ISL, (k + 1) * ISL) for k in range(8)]
        v_hq_store = 0
        v_ffn_mm_prev = [0]
        v_ffn_h_prev = [0]
        for ici, (i0, i1) in enumerate(ics):
            iw = i1 - i0
            cb = ici % 2
            if ici == 0:
                P.wait("sync", "vector", v_x2qT_done)
            else:
                P.wait("sync", "tensor", v_ffn_mm_prev[0])
            v_wg_l = 0
            for dt_ in range(8):
                r0 = ici * D + dt_ * 128
                v_wg_l = sload(wgc[cb][:, dt_, 0:iw], d_wgT_blk[r0:r0 + 128, :])
                v_wg_l = max(v_wg_l, sload(wuc[cb][:, dt_, 0:iw],
                                           d_wuT_blk[r0:r0 + 128, :]))
            P.wait("tensor", "sync", v_wg_l)
            for tt in range(16):
                bank_wait("tensor", 2)
                bank_wait("tensor", 3)
                for dt_ in range(8):
                    def mmg(e, tt=tt, dt_=dt_, cb=cb, iw=iw, st=(dt_ == 0),
                            sp=(dt_ == 7)):
                        return e.matmul(PB[2][:, 0:iw],
                                        x2qT[:, dt_, tt * 128:(tt + 1) * 128],
                                        wgc[cb][:, dt_, 0:iw], start=st, stop=sp)
                    if dt_ == 7:
                        v_mmg = P.inc("tensor", "tensor", mmg)
                    else:
                        P.emit("tensor", mmg)
                for dt_ in range(8):
                    def mmu(e, tt=tt, dt_=dt_, cb=cb, iw=iw, st=(dt_ == 0),
                            sp=(dt_ == 7)):
                        return e.matmul(PB[3][:, 0:iw],
                                        x2qT[:, dt_, tt * 128:(tt + 1) * 128],
                                        wuc[cb][:, dt_, 0:iw], start=st, stop=sp)
                    if dt_ == 7:
                        v_mmu = P.inc("tensor", "tensor", mmu)
                    else:
                        P.emit("tensor", mmu)
                P.wait("scalar", "tensor", v_mmg)
                P.wait("scalar", "vector", v_ffn_h_prev[0])
                v_ffn_mm_prev[0] = v_mmu
                v_sg = P.inc("scalar", "scalar", lambda e, iw=iw: e.activation(
                    out=sg_sb[:, 0:iw], in_=PB[2][:, 0:iw], func=AF.Silu))
                bank_free[2] = ("scalar", v_sg)
                P.wait("vector", "scalar", v_sg)
                P.wait("vector", "tensor", v_mmu)
                v_h = P.inc("vector", "vector", lambda e, iw=iw: e.tensor_tensor(
                    out=h_sb[:, 0:iw], in0=sg_sb[:, 0:iw], in1=PB[3][:, 0:iw],
                    op=ALU.mult))
                bank_free[3] = ("vector", v_h)
                v_ffn_h_prev[0] = v_h
                if debug is True and ici == 0 and tt == 0:
                    P.wait("gpsimd", "scalar", v_sg)
                    P.inc("gpsimd", "gpsimd", lambda e: e.dma_start(
                        out=ddbg["gate"], in_=sg_sb[:, 0:512]), 16)
                P.wait("vector", "sync", v_hq_store)
                v_hq = quant_fp4(h_sb[:, 0:iw], hq_sb[:, 0:iw], iw, c_qf, c_qi,
                                 128, src_ready=("vector", v_h))
                v_f_hq = vfence(v_hq)
                P.wait("sync", "gpsimd", v_f_hq)
                P.wait("sync", "vector", v_hq)
                v_hq_store = sload(d_hq[tt * 128:(tt + 1) * 128, i0:i1],
                                   hq_sb[:, 0:iw])
                if debug is True and tt == 0:
                    P.wait("gpsimd", "vector", v_hq)
                    P.inc("gpsimd", "gpsimd",
                          lambda e, i0=i0, i1=i1: e.dma_start(
                              out=ddbg["hq"][:, i0:i1], in_=hq_sb[:, 0:i1 - i0]), 16)

        P.wait("sync", "sync", v_hq_store)
        v_mmd_prev = [0]
        v_tr_prev = [0]
        v_r2_prev = [0]
        v_b4_ssq_prev = [0]
        for tt in range(16):
            if tt >= 1:
                P.wait("sync", "tensor", v_tr_prev[0])
                P.wait("sync", "vector", v_r2_prev[0])
            v_hl = sload(hqld[tt % 2][:], d_hq[tt * 128:(tt + 1) * 128, :])
            v_x2r = sload(x2l[tt % 2][:], d_x2[tt * 128:(tt + 1) * 128, :])
            P.wait("tensor", "sync", v_hl)
            P.wait("vector", "tensor", v_mmd_prev[0])
            for it in range(22):
                bnk = 6 + (it % 2)
                bank_wait("tensor", bnk)
                v_tr = P.inc("tensor", "tensor", lambda e, tt=tt, it=it, bnk=bnk:
                             e.transpose(PB[bnk][:, 0:64].bitcast(BF16),
                                         hqld[tt % 2][:, it * 128:(it + 1) * 128],
                                         ident[:]))
                P.wait("vector", "tensor", v_tr)
                v_evt = P.inc("vector", "vector", lambda e, it=it, bnk=bnk:
                              e.tensor_copy(out=hqT[:, it, :], in_=PB[bnk][:, 0:64].bitcast(BF16)))
                bank_free[bnk] = ("vector", v_evt)
            v_hqT_done = P.semv["vector"]
            v_tr_prev[0] = P.semv["tensor"]
            P.wait("gpsimd", "vector", v_hqT_done)
            v_echo_h = P.inc("gpsimd", "gpsimd", lambda e: e.engine_nop(), 1)
            P.wait("tensor", "gpsimd", v_echo_h)
            P.wait("tensor", "vector", v_hqT_done)
            if tt == 0:
                P.wait("tensor", "sync", v_wd_l)
            db = (0 + 2 * (tt % 2), 1 + 2 * (tt % 2))
            for bnk in db:
                bank_wait("tensor", bnk)
            for it in range(22):
                for half in range(2):
                    def mmd(e, it=it, half=half, bnk=db[half], st=(it == 0),
                            sp=(it == 21)):
                        return e.matmul(PB[bnk][:, 0:512], hqT[:, it, :],
                                        wdqT[:, it, half * 512:(half + 1) * 512],
                                        start=st, stop=sp)
                    if it == 21 and half == 1:
                        v_mmd = P.inc("tensor", "tensor", mmd)
                    else:
                        P.emit("tensor", mmd)
            P.wait("vector", "tensor", v_mmd)
            P.wait("vector", "sync", v_x2r)
            P.wait("vector", "scalar", v_b4_ssq_prev[0])
            v_r2 = None
            for half in range(2):
                v_r2 = P.inc("vector", "vector",
                             lambda e, tt=tt, half=half, bnk=db[half]:
                             e.tensor_tensor(
                                 out=r_sb[:, half * 512:(half + 1) * 512],
                                 in0=PB[bnk][:, 0:512],
                                 in1=x2l[tt % 2][:, half * 512:(half + 1) * 512],
                                 op=ALU.add))
            for half in range(2):
                bank_free[db[half]] = ("vector", v_r2)
            v_mmd_prev[0] = v_mmd
            v_r2_prev[0] = v_r2
            if tt >= 2:
                P.wait("vector", "out", 16 * (tt - 1))
            v_o, v_ssq4 = ln_tail(r_sb[:], out_sb[tt % 2][:], v_r2)
            v_b4_ssq_prev[0] = v_ssq4
            v_f_o = vfence(v_o)
            P.wait("sync", "gpsimd", v_f_o)
            P.wait("sync", "vector", v_o)
            P.inc("sync", "out", lambda e, tt=tt: e.dma_start(
                out=dout[tt * 128:(tt + 1) * 128, :], in_=out_sb[tt % 2][:]), 16)

        P.wait("sync", "out", P.semv["out"])

        P.run(block)
    return nc


_NC_CACHE = [None]


def make_in_maps(x, W_in, conv_w, conv_b, W_xp, W_dt, b_dt, A_log, Dv, W_out,
                 wg, wu, wd):
    ident = np.eye(128, dtype=np.float32)
    in_maps = []
    for c in range(8):
        s, j = c // 2, c % 2
        ch = slice(j * HC, (j + 1) * HC)
        in_maps.append({
            "xT": np.ascontiguousarray(x[s].T),
            "xres": np.ascontiguousarray(x[s, j * TH:(j + 1) * TH, :]),
            "winT": np.ascontiguousarray(np.concatenate(
                [W_in[ch].T, W_in[H + j * HC:H + (j + 1) * HC].T], axis=1)),
            "convw": np.ascontiguousarray(conv_w[ch, 0, :]),
            "convb": np.ascontiguousarray(conv_b[ch].reshape(8, 128).T),
            "wxpT": np.ascontiguousarray(W_xp[:, ch].T),
            "wdtT": np.ascontiguousarray(W_dt[ch].T),
            "bdt": np.ascontiguousarray(b_dt[ch].reshape(8, 128).T),
            "alog": np.ascontiguousarray(A_log[None, :]),
            "dvec": np.ascontiguousarray(Dv[ch].reshape(8, 128).T),
            "woutT": np.ascontiguousarray(W_out[:, ch].T),
            "wg_s": np.ascontiguousarray(wg[c * ISL:(c + 1) * ISL]),
            "wu_s": np.ascontiguousarray(wu[c * ISL:(c + 1) * ISL]),
            "wd_s": np.ascontiguousarray(wd[c * DSL:(c + 1) * DSL]),
            "identin": ident,
        })
    return in_maps


def kernel(L0_attn_x, L0_attn_W_in, L0_attn_conv_w, L0_attn_conv_b,
           L0_attn_W_xproj, L0_attn_W_dt, L0_attn_b_dt, L0_attn_A_log,
           L0_attn_D, L0_attn_W_out, L0_ffn_w_gate, L0_ffn_w_up, L0_ffn_w_down):
    in_maps = make_in_maps(
        np.asarray(L0_attn_x, np.float32), np.asarray(L0_attn_W_in, np.float32),
        np.asarray(L0_attn_conv_w, np.float32),
        np.asarray(L0_attn_conv_b, np.float32),
        np.asarray(L0_attn_W_xproj, np.float32),
        np.asarray(L0_attn_W_dt, np.float32),
        np.asarray(L0_attn_b_dt, np.float32),
        np.asarray(L0_attn_A_log, np.float32),
        np.asarray(L0_attn_D, np.float32), np.asarray(L0_attn_W_out, np.float32),
        np.asarray(L0_ffn_w_gate, np.float32),
        np.asarray(L0_ffn_w_up, np.float32),
        np.asarray(L0_ffn_w_down, np.float32))
    if _NC_CACHE[0] is None:
        _NC_CACHE[0] = build()
    nc = _NC_CACHE[0]
    res = run_bass_kernel_spmd(nc, in_maps, core_ids=list(range(8)))
    out = np.empty((B, T, D), np.float32)
    for c in range(8):
        s, j = c // 2, c % 2
        out[s, j * TH:(j + 1) * TH, :] = res.results[c]["out"]
    return out


if __name__ == "__main__":
    print("building ...")
    nc = build()
    n = sum(len(b.instructions) for f in nc.m.functions for b in f.blocks)
    print("built, instructions:", n)
